# version 10
# baseline (speedup 1.0000x reference)
import math
from contextlib import ExitStack

import numpy as np
import ml_dtypes

import concourse.bass as bass
import concourse.mybir as mybir
from concourse.bass_utils import run_bass_kernel_spmd

F32 = mybir.dt.float32
BF16 = mybir.dt.bfloat16
AF = mybir.ActivationFunctionType
ALU = mybir.AluOpType
AX = mybir.AxisListType

EPS = 1e-5


def make_cfg(D, S, TOWN, H, HK, HID, TCA, TC, QC):
    DH = 128
    cfg = dict(D=D, S=S, TOWN=TOWN, H=H, HK=HK, HID=HID, DH=DH)
    cfg["G"] = H // HK
    cfg["KVD"] = HK * DH
    cfg["DD"] = D // 128
    cfg["NKT"] = S // 128
    cfg["NHT"] = (HID + 127) // 128
    cfg["HIDP"] = cfg["NHT"] * 128
    cfg["TCA"] = TCA
    cfg["TC"] = TC
    cfg["QC"] = QC
    cfg["NCHA"] = S // TCA
    cfg["OWNA"] = TOWN // TCA
    cfg["NTC"] = TOWN // TC
    cfg["NQC"] = TOWN // QC
    cfg["NID"] = cfg["NQC"] * H
    cfg["NW"] = HID * D
    cfg["SCL"] = 1.0 / math.sqrt(DH)
    assert TCA <= 256 and TC <= 512 and QC <= 1024
    tiles = []
    for m in (0, 1):
        for r in range(cfg["NHT"]):
            c0 = 0
            while c0 < D:
                w = min(1024, D - c0)
                tiles.append((m, r * 128, c0, w))
                c0 += w
    for r in range(cfg["DD"]):
        c0 = 0
        while c0 < cfg["HIDP"]:
            w = min(1024, cfg["HIDP"] - c0)
            tiles.append((2, r * 128, c0, w))
            c0 += w
    cfg["TILES"] = tiles
    cfg["FIN_IDX"] = min(4, cfg["NID"] - 2)
    cfg["W_START"] = cfg["FIN_IDX"] + 1
    cfg["WQPOS"] = min(len(tiles), max(1, (2 * len(tiles)) // 3))
    return cfg


REAL = make_cfg(D=2048, S=4096, TOWN=2048, H=16, HK=4, HID=5461,
                TCA=256, TC=512, QC=1024)
SMALL = make_cfg(D=512, S=512, TOWN=256, H=4, HK=2, HID=300,
                 TCA=128, TC=128, QC=128)
MED = make_cfg(D=1024, S=2048, TOWN=2048, H=8, HK=2, HID=700,
               TCA=256, TC=512, QC=1024)


class Sem:
    def __init__(self, ctx, nc, name):
        self.h = ctx.enter_context(nc.semaphore(name))
        self.n = 0

    def inc(self, ins, k=1):
        ins.then_inc(self.h, k)
        self.n += k
        return self.n

    def dma(self, ins):
        return self.inc(ins, 16)


def build_nc(cfg, dump=False):
    D, S, TOWN = cfg["D"], cfg["S"], cfg["TOWN"]
    H, HK, G, DH, KVD = cfg["H"], cfg["HK"], cfg["G"], cfg["DH"], cfg["KVD"]
    DD, NKT, NHT, HIDP = cfg["DD"], cfg["NKT"], cfg["NHT"], cfg["HIDP"]
    TCA, TC, QC = cfg["TCA"], cfg["TC"], cfg["QC"]
    NCHA, OWNA, NTC, NID = cfg["NCHA"], cfg["OWNA"], cfg["NTC"], cfg["NID"]
    TILES, FIN_IDX, W_START, WQPOS = (cfg["TILES"], cfg["FIN_IDX"],
                                      cfg["W_START"], cfg["WQPOS"])
    SCL = cfg["SCL"]
    NT = len(TILES)
    INV_D = 1.0 / D
    NM = KVD // 128
    NTT = TCA // 128
    NSC = (QC + 511) // 512
    ESR = 4

    nc = bass.Bass("TRN2", target_bir_lowering=False, debug=False,
                   detect_race_conditions=False)

    xT = nc.dram_tensor("xT", [D, S], F32, kind="ExternalInput").ap()
    wq = nc.dram_tensor("wq", [D, D], F32, kind="ExternalInput").ap()
    wk = nc.dram_tensor("wk", [D, KVD], F32, kind="ExternalInput").ap()
    wv = nc.dram_tensor("wv", [D, KVD], F32, kind="ExternalInput").ap()
    wo = nc.dram_tensor("wo", [D, D], F32, kind="ExternalInput").ap()
    bqd = nc.dram_tensor("bq", [1, D], F32, kind="ExternalInput").ap()
    bkd = nc.dram_tensor("bk", [1, KVD], F32, kind="ExternalInput").ap()
    bvd = nc.dram_tensor("bv", [1, KVD], F32, kind="ExternalInput").ap()
    bod = nc.dram_tensor("bo", [1, D], F32, kind="ExternalInput").ap()
    w1s = nc.dram_tensor("w1s", [NHT * 128, D], F32, kind="ExternalInput").ap()
    w3s = nc.dram_tensor("w3s", [NHT * 128, D], F32, kind="ExternalInput").ap()
    w2s = nc.dram_tensor("w2s", [DD * 128, HIDP], F32, kind="ExternalInput").ap()
    outT = nc.dram_tensor("outT", [D, TOWN], F32, kind="ExternalOutput").ap()

    xn_s = nc.dram_tensor("xn_s", [DD * 128, TOWN], BF16, kind="Internal").ap()
    qt_s = nc.dram_tensor("qt_s", [H * 128, TOWN], BF16, kind="Internal").ap()
    y_s = nc.dram_tensor("y_s", [DD * 128, TOWN], F32, kind="Internal").ap()
    yn_s = nc.dram_tensor("yn_s", [DD * 128, TOWN], BF16, kind="Internal").ap()
    w1q_s = nc.dram_tensor("w1q_s", [NHT * 128, D], BF16, kind="Internal").ap()
    w3q_s = nc.dram_tensor("w3q_s", [NHT * 128, D], BF16, kind="Internal").ap()
    w2q_s = nc.dram_tensor("w2q_s", [DD * 128, HIDP], BF16, kind="Internal").ap()
    wo_s = nc.dram_tensor("wo_s", [DD * 128, D], BF16, kind="Internal").ap()
    WSRC = [w1s, w3s, w2s]
    WDST = [w1q_s, w3q_s, w2q_s]

    onesr_f_t = nc.inline_tensor(np.ones((1, 128), np.float32), "onesr_f")
    onesc_f_t = nc.inline_tensor(np.ones((128, 1), np.float32), "onesc_f")
    onesc_b_t = nc.inline_tensor(
        np.ones((128, 1), np.float32).astype(ml_dtypes.bfloat16), "onesc_b")
    onesr_b_t = nc.inline_tensor(
        np.ones((1, 512), np.float32).astype(ml_dtypes.bfloat16), "onesr_b")

    ctx = ExitStack()

    def SB(stack, name, shape, dt, side="left"):
        return stack.enter_context(nc.sbuf_tensor(name, shape, dt, side=side))

    onesr_f = SB(ctx, "c_onesr_f", [1, 128], F32)
    onesc_f = SB(ctx, "c_onesc_f", [128, 1], F32)
    onesc_b = SB(ctx, "c_onesc_b", [128, 1], BF16)
    onesr_b = SB(ctx, "c_onesr_b", [1, 512], BF16)
    negb = SB(ctx, "c_negb", [128, 1], F32)
    thrP = SB(ctx, "c_thrP", [128, 3], F32)
    thrN = SB(ctx, "c_thrN", [128, 3], F32)
    sF = SB(ctx, "c_sF", [128, 3], F32)
    sacc_dbg = SB(ctx, "c_sacc_dbg", [128, 3], F32)
    qn_all = SB(ctx, "c_qn_all", [1, 256], F32)
    kn_all = SB(ctx, "c_kn_all", [1, 256], F32)
    sthr_row = SB(ctx, "c_sthr", [1, 8], F32)
    qmax = SB(ctx, "c_qmax", [1, 1], F32)
    kmax = SB(ctx, "c_kmax", [1, 1], F32)
    r11a = SB(ctx, "c_r11a", [1, 1], F32)
    r11b = SB(ctx, "c_r11b", [1, 1], F32)
    m_row = SB(ctx, "c_m_row", [1, 512], F32)
    var_row = SB(ctx, "c_var_row", [1, 512], F32)
    ln_row = SB(ctx, "c_ln_row", [1, 512], F32)
    r_row = SB(ctx, "c_r_row", [1, 512], F32)
    c2_row = SB(ctx, "c_c2_row", [1, 512], F32)
    thr_row = SB(ctx, "c_thr_row", [1, 3], F32)

    sT = Sem(ctx, nc, "sT")
    sS = Sem(ctx, nc, "sS")
    sV = Sem(ctx, nc, "sV")
    sG = Sem(ctx, nc, "sG")
    sC = Sem(ctx, nc, "sC")
    sX = [Sem(ctx, nc, f"sX{i}") for i in range(2)]
    sXNW = [Sem(ctx, nc, f"sXNW{i}") for i in range(2)]
    sXN2 = Sem(ctx, nc, "sXN2")
    sQW = [Sem(ctx, nc, f"sQW{i}") for i in range(2)]
    sQR = [Sem(ctx, nc, f"sQR{i}") for i in range(2)]
    sWF = [Sem(ctx, nc, f"sWF{i}") for i in range(2)]
    sWB = [Sem(ctx, nc, f"sWB{i}") for i in range(2)]
    sXO = [Sem(ctx, nc, f"sXO{i}") for i in range(4)]
    sY = Sem(ctx, nc, "sY")
    sYNW = [Sem(ctx, nc, f"sYNW{i}") for i in range(2)]
    sYNR = Sem(ctx, nc, "sYNR")
    sY16 = Sem(ctx, nc, "sY16")
    sW1 = [Sem(ctx, nc, f"sW1{i}") for i in range(2)]
    sW2 = [Sem(ctx, nc, f"sW2{i}") for i in range(2)]
    sWO = [Sem(ctx, nc, f"sWO{i}") for i in range(2)]
    sGC = Sem(ctx, nc, "sGC")
    sGW = [Sem(ctx, nc, f"sGW{i}") for i in range(2)]
    sWBG = [Sem(ctx, nc, f"sWBG{i}") for i in range(2)]
    sOUT = Sem(ctx, nc, "sOUT")

    def W(eng, sv):
        eng.wait_ge(sv[0].h, sv[1])

    block = ctx.enter_context(nc.Block())

    def on(engname, fn):
        getattr(block, engname)(fn)

    def dv(ap, c0, n):
        return ap.rearrange("(a p) s -> p a s", p=128)[:, :, c0:c0 + n]

    def _pro_sync(sy):
        sC.dma(sy.dma_start(onesr_f[:], onesr_f_t.ap()))
        sC.dma(sy.dma_start(onesc_f[:], onesc_f_t.ap()))
        sC.dma(sy.dma_start(onesc_b[:], onesc_b_t.ap()))
        sC.dma(sy.dma_start(onesr_b[:], onesr_b_t.ap()))
    on("sync", _pro_sync)
    cD_const = (sC, sC.n)

    def _pro_vec(v):
        v.memset(qmax[:], 0.0)
        sV.inc(v.memset(kmax[:], 0.0))
    on("vector", _pro_vec)
    vInit = sV.n

    ktv_ctx = ExitStack()
    KTs = SB(ktv_ctx, "KTs", [128, HK * S], BF16)
    Vs = SB(ktv_ctx, "Vs", [128, NKT * KVD], BF16)
    wfp = [SB(ktv_ctx, f"wfp{i}", [128, 1024], F32) for i in range(2)]
    absd = SB(ktv_ctx, "absd", [128, 1024], BF16)
    rsall = SB(ktv_ctx, "rsall", [128, NT], F32)
    sacc = SB(ktv_ctx, "sacc", [128, 3], F32)

    a1_ctx = ExitStack()
    bk_b = SB(a1_ctx, "bk_b", [1, KVD], BF16)
    bv_b = SB(a1_ctx, "bv_b", [1, KVD], BF16)
    wkb = SB(a1_ctx, "wkb", [128, DD * KVD], BF16)
    wvb = SB(a1_ctx, "wvb", [128, DD * KVD], BF16)
    xbuf = [SB(a1_ctx, f"xbuf{i}", [128, DD * TCA], F32) for i in range(2)]
    xnbuf = [SB(a1_ctx, f"xnbuf{i}", [128, DD * TCA], BF16) for i in range(2)]
    xsq = [SB(a1_ctx, f"xsq{i}", [128, TCA], BF16) for i in range(2)]
    tmpx = SB(a1_ctx, "tmpx", [128, TCA], F32)

    def _pro_gp(g):
        sGC.dma(g.dma_start(bk_b[:], bkd))
        sGC.dma(g.dma_start(bv_b[:], bvd))
        sGC.dma(g.dma_start(wkb[:].rearrange("p (a s) -> p a s", s=KVD),
                            wk.rearrange("(a p) s -> p a s", p=128)))
        sGC.dma(g.dma_start(wvb[:].rearrange("p (a s) -> p a s", s=KVD),
                            wv.rearrange("(a p) s -> p a s", p=128)))
    on("gpsimd", _pro_gp)
    cGD_wkv = (sGC, sGC.n)

    cGW = {}
    sAbs = {}
    MATR = {}
    _c = 0
    for _t, (_m, _r, _cc, _n) in enumerate(TILES):
        MATR.setdefault(_m, [_t, _t])[1] = _t

    def spass_emit(t0, t1):
        if t0 >= t1:
            return
        sBase = sS.n

        def sp_g(g, t0=t0, t1=t1, sBase=sBase):
            for t in range(t0, t1):
                mat, r0, c0, ncols = TILES[t]
                if t - 2 >= t0:
                    g.wait_ge(sS.h, sBase + (t - 2 - t0) + 1)
                elif (t - 2) in sAbs:
                    g.wait_ge(sS.h, sAbs[t - 2])
                cGW[t] = (sGW[t % 2], sGW[t % 2].dma(g.dma_start(
                    wfp[t % 2][:, 0:ncols], WSRC[mat][r0:r0 + 128, c0:c0 + ncols])))
        on("gpsimd", sp_g)

        def sp_s(s, t0=t0, t1=t1, sBase=sBase):
            for t in range(t0, t1):
                mat, r0, c0, ncols = TILES[t]
                W(s, cGW[t])
                sAbs[t] = sS.inc(s.activation(absd[:, 0:ncols], wfp[t % 2][:, 0:ncols],
                                              AF.Abs, accum_out=rsall[:, t:t + 1]))
                assert sAbs[t] == sBase + (t - t0) + 1
        on("scalar", sp_s)

    NT_A1 = (3 * NT) // 4
    spA1 = [(NT_A1 * j // NCHA, NT_A1 * (j + 1) // NCHA) for j in range(NCHA)]
    spA2 = [(NT_A1 + (NT - NT_A1) * j // NTC, NT_A1 + (NT - NT_A1) * (j + 1) // NTC)
            for j in range(NTC)]

    p_ctx = ExitStack()

    def PS(stack, name, shape):
        return stack.enter_context(nc.psum_tensor(name, shape, F32))

    pKT = [PS(p_ctx, "pKT0", [128, TCA]), PS(p_ctx, "pKT1", [128, TCA])]
    pV = [PS(p_ctx, "pV0", [128, KVD]), PS(p_ctx, "pV1", [128, KVD])]
    pST = PS(p_ctx, "pST", [1, 1024])
    pBC = [PS(p_ctx, "pBC0", [128, 2 * TCA]), PS(p_ctx, "pBC1", [128, 2 * TCA])]

    chunk_order = list(range(OWNA, NCHA)) + list(range(OWNA))
    cD_x = {}
    cD_xnw = {}
    tSQ = {}
    tST = {-1: 0}
    vST = {-1: vInit}
    vXN = {-1: vInit, -2: vInit}
    tKV = {-1: 0, -2: 0}
    sKTe = {}
    sVe = {}

    for jj, j in enumerate(chunk_order):
        c0 = j * TCA
        own = j < OWNA
        p = jj % 2

        def a1_sync(sy, jj=jj, c0=c0, p=p):
            if jj == 0:
                W(sy, cD_const)
            sy.wait_ge(sV.h, vXN[jj - 2])
            cD_x[jj] = (sX[p], sX[p].dma(sy.dma_start(
                xbuf[p][:].rearrange("p (a s) -> p a s", s=TCA), dv(xT, c0, TCA))))
        on("sync", a1_sync)

        tBase = sT.n
        sSQ = {}

        def a1_sq(v, jj=jj, p=p, tBase=tBase, sSQ=sSQ):
            for dd in range(DD):
                if dd < 2:
                    if (jj - 1, DD - 2 + dd) in tSQ:
                        v.wait_ge(sT.h, tSQ[(jj - 1, DD - 2 + dd)])
                else:
                    v.wait_ge(sT.h, tBase + (dd - 2) + 1)
                if dd == 0:
                    W(v, cD_x[jj])
                sl = xbuf[p][:, dd * TCA:(dd + 1) * TCA]
                sSQ[dd] = sV.inc(v.tensor_mul(xsq[dd % 2][:], sl, sl))
        on("vector", a1_sq)

        def a1_stats_mm(t, jj=jj, p=p, tBase=tBase, sSQ=sSQ):
            for dd in range(DD):
                if dd == 0:
                    t.wait_ge(sV.h, vST[jj - 1])
                    W(t, cD_x[jj])
                t.matmul(pST[0:1, 0:TCA], onesc_f[:],
                         xbuf[p][:, dd * TCA:(dd + 1) * TCA],
                         start=(dd == 0), stop=(dd == DD - 1))
                t.wait_ge(sV.h, sSQ[dd])
                ins = t.matmul(pST[0:1, 512:512 + TCA], onesc_b[:], xsq[dd % 2][:],
                               start=(dd == 0), stop=(dd == DD - 1))
                tSQ[(jj, dd)] = sT.inc(ins)
                assert tSQ[(jj, dd)] == tBase + dd + 1
            tST[jj] = sT.n
        on("tensor", a1_stats_mm)

        def a1_stats_v(v, jj=jj):
            v.wait_ge(sT.h, tST[jj])
            v.tensor_scalar(m_row[0:1, 0:TCA], pST[0:1, 0:TCA], INV_D, None, ALU.mult)
            v.tensor_scalar(var_row[0:1, 0:TCA], pST[0:1, 512:512 + TCA], INV_D,
                            EPS, ALU.mult, ALU.add)
            v.tensor_mul(ln_row[0:1, 0:TCA], m_row[0:1, 0:TCA], m_row[0:1, 0:TCA])
            vST[jj] = sV.inc(v.tensor_sub(var_row[0:1, 0:TCA], var_row[0:1, 0:TCA],
                                          ln_row[0:1, 0:TCA]))
        on("vector", a1_stats_v)

        def a1_rstd(s, jj=jj):
            s.wait_ge(sV.h, vST[jj])
            s.activation(ln_row[0:1, 0:TCA], var_row[0:1, 0:TCA], AF.Ln)
            sS.inc(s.activation(r_row[0:1, 0:TCA], ln_row[0:1, 0:TCA], AF.Exp,
                                scale=-0.5))
        on("scalar", a1_rstd)
        sR1 = sS.n

        def a1_c2(v, sR1=sR1):
            v.wait_ge(sS.h, sR1)
            sV.inc(v.scalar_tensor_tensor(c2_row[0:1, 0:TCA], m_row[0:1, 0:TCA],
                                          -1.0, r_row[0:1, 0:TCA], ALU.mult, ALU.mult))
        on("vector", a1_c2)
        vC2 = sV.n

        def a1_bc(t, jj=jj, p=p, vC2=vC2):
            t.wait_ge(sV.h, vC2)
            t.wait_ge(sV.h, vXN[jj - 2])
            t.matmul(pBC[p][:, 0:TCA], onesr_f[0:1, 0:128], r_row[0:1, 0:TCA],
                     start=True, stop=True)
            sT.inc(t.matmul(pBC[p][:, TCA:2 * TCA], onesr_f[0:1, 0:128],
                            c2_row[0:1, 0:TCA], start=True, stop=True))
        on("tensor", a1_bc)
        tBC1 = sT.n

        def a1_xn(v, jj=jj, p=p, tBC1=tBC1):
            v.wait_ge(sT.h, tBC1)
            v.wait_ge(sT.h, tKV[jj - 2])
            if (jj - 2) in cD_xnw:
                W(v, cD_xnw[jj - 2])
            for dd in range(DD):
                v.tensor_mul(tmpx[:], xbuf[p][:, dd * TCA:(dd + 1) * TCA],
                             pBC[p][:, 0:TCA])
                ins = v.tensor_add(xnbuf[p][:, dd * TCA:(dd + 1) * TCA], tmpx[:],
                                   pBC[p][:, TCA:2 * TCA])
            vXN[jj] = sV.inc(ins)
        on("vector", a1_xn)

        if own:
            def a1_xnw(sy, jj=jj, c0=c0, p=p):
                sy.wait_ge(sV.h, vXN[jj])
                sem = sXNW[jj % 2]
                cD_xnw[jj] = (sem, sem.dma(sy.dma_start(
                    dv(xn_s, c0, TCA),
                    xnbuf[p][:].rearrange("p (a s) -> p a s", s=TCA))))
            on("sync", a1_xnw)

        for m in range(NM):
            def a1_k(t, jj=jj, c0=c0, p=p, m=m):
                if m == 0:
                    t.wait_ge(sV.h, vXN[jj])
                    if jj == 0:
                        W(t, cGD_wkv)
                if (jj, m - 2) in sKTe:
                    t.wait_ge(sV.h, sKTe[(jj, m - 2)])
                elif (jj - 1, NM - 2 + m) in sKTe:
                    t.wait_ge(sV.h, sKTe[(jj - 1, NM - 2 + m)])
                for dd in range(DD):
                    t.matmul(pKT[m % 2][:],
                             wkb[:, dd * KVD + m * 128:dd * KVD + (m + 1) * 128],
                             xnbuf[p][:, dd * TCA:(dd + 1) * TCA],
                             start=(dd == 0), stop=False)
                ins = t.matmul(pKT[m % 2][:], bk_b[0:1, m * 128:(m + 1) * 128],
                               onesr_b[0:1, 0:TCA], start=False, stop=True)
                sKTe[(jj, m, "mm")] = sT.inc(ins)
            on("tensor", a1_k)

            def a1_ke(v, jj=jj, c0=c0, m=m):
                v.wait_ge(sT.h, sKTe[(jj, m, "mm")])
                sKTe[(jj, m)] = sV.inc(v.tensor_copy(
                    KTs[:, m * S + c0:m * S + c0 + TCA], pKT[m % 2][:]))
            on("vector", a1_ke)

        for tt in range(NTT):
            def a1_v(t, jj=jj, c0=c0, p=p, tt=tt):
                if (jj, tt - 2) in sVe:
                    t.wait_ge(sV.h, sVe[(jj, tt - 2)])
                elif (jj - 1, NTT - 2 + tt) in sVe:
                    t.wait_ge(sV.h, sVe[(jj - 1, NTT - 2 + tt)])
                for dd in range(DD):
                    t.matmul(pV[tt % 2][:],
                             xnbuf[p][:, dd * TCA + tt * 128:dd * TCA + (tt + 1) * 128],
                             wvb[:, dd * KVD:(dd + 1) * KVD],
                             start=(dd == 0), stop=False)
                ins = t.matmul(pV[tt % 2][:], onesr_b[0:1, 0:128], bv_b[0:1, 0:KVD],
                               start=False, stop=True)
                sVe[(jj, tt, "mm")] = sT.inc(ins)
            on("tensor", a1_v)

            def a1_ve(v, jj=jj, c0=c0, tt=tt):
                v.wait_ge(sT.h, sVe[(jj, tt, "mm")])
                ktg = c0 // 128 + tt
                sVe[(jj, tt)] = sV.inc(v.tensor_copy(
                    Vs[:, ktg * KVD:(ktg + 1) * KVD], pV[tt % 2][:]))
            on("vector", a1_ve)
        tKV[jj] = sT.n
        spass_emit(*spA1[jj])

    a1_endT, a1_endS, a1_endV = sT.n, sS.n, sV.n
    a1_xnw_fin = [(s_, s_.n) for s_ in sXNW]
    p_ctx.close()
    a1_ctx.close()

    a2_ctx = ExitStack()
    bq_b = SB(a2_ctx, "bq_b", [1, D], BF16)
    wqb = SB(a2_ctx, "wqb", [128, DD * D], BF16)
    xn_in = SB(a2_ctx, "xn_in", [128, DD * TC], BF16)
    qtev = SB(a2_ctx, "qtev", [128, TC], BF16)
    qsq = SB(a2_ctx, "qsq", [128, TC], BF16)

    p_ctx = ExitStack()
    pQT = [PS(p_ctx, "pQT0", [128, TC]), PS(p_ctx, "pQT1", [128, TC])]
    pQN = PS(p_ctx, "pQN", [1, 512])

    def _gp_wq(g):
        g.wait_ge(sT.h, a1_endT)
        g.wait_ge(sS.h, a1_endS)
        g.wait_ge(sV.h, a1_endV)
        sGC.dma(g.dma_start(bq_b[:], bqd))
        sGC.dma(g.dma_start(wqb[:].rearrange("p (a s) -> p a s", s=D),
                            wq.rearrange("(a p) s -> p a s", p=128)))
        sGC.dma(g.dma_start(wo_s, wo))
    on("gpsimd", _gp_wq)
    cGD_wq = (sGC, sGC.n)

    sKq = {}
    tKn = {}
    vKn = {-1: a1_endV}
    KCW = min(512, S, TC)
    nkc = S // KCW
    for kvh in range(HK):
        for ch in range(nkc):
            i = kvh * nkc + ch

            def kn_s(v, kvh=kvh, ch=ch, i=i):
                if (i - 1) in tKn:
                    v.wait_ge(sT.h, tKn[i - 1])
                if i == 0:
                    v.wait_ge(sV.h, a1_endV)
                sl = KTs[:, kvh * S + ch * KCW:kvh * S + (ch + 1) * KCW]
                sKq[i] = sV.inc(v.tensor_mul(qsq[:, 0:KCW], sl, sl))
            on("vector", kn_s)

            def kn_t(t, i=i):
                t.wait_ge(sV.h, sKq[i])
                t.wait_ge(sV.h, vKn[i - 1])
                if i == 0:
                    t.wait_ge(sV.h, a1_endV)
                tKn[i] = sT.inc(t.matmul(pQN[0:1, 0:KCW], onesc_b[:],
                                         qsq[:, 0:KCW], start=True, stop=True))
            on("tensor", kn_t)

            def kn_v(v, i=i):
                v.wait_ge(sT.h, tKn[i])
                vKn[i] = sV.inc(v.reduce_max(kn_all[0:1, i:i + 1],
                                             pQN[0:1, 0:KCW], AX.X))
            on("vector", kn_v)
    vKn_end = sV.n

    cD_xn2 = {}
    cD_qtw = {}
    tQ = {}
    sQe = {}
    tQn = {}
    vQn = {-1: vKn_end}

    for tc in range(NTC):
        def a2_sync(sy, tc=tc):
            if tc == 0:
                sy.wait_ge(sT.h, a1_endT)
                sy.wait_ge(sS.h, a1_endS)
                sy.wait_ge(sV.h, a1_endV)
                for sv in a1_xnw_fin:
                    W(sy, sv)
            else:
                sy.wait_ge(sT.h, tQ[(tc - 1, H - 1)])
            cD_xn2[tc] = (sXN2, sXN2.dma(sy.dma_start(
                xn_in[:].rearrange("p (a s) -> p a s", s=TC), dv(xn_s, tc * TC, TC))))
        on("sync", a2_sync)

        for m in range(H):
            i = tc * H + m

            def a2_q(t, tc=tc, m=m, i=i):
                if m == 0:
                    W(t, cD_xn2[tc])
                if i == 0:
                    W(t, cGD_wq)
                    t.wait_ge(sS.h, a1_endS)
                    t.wait_ge(sV.h, a1_endV)
                if (i - 2) in sQe:
                    t.wait_ge(sV.h, sQe[i - 2])
                for dd in range(DD):
                    t.matmul(pQT[i % 2][:],
                             wqb[:, dd * D + m * 128:dd * D + (m + 1) * 128],
                             xn_in[:, dd * TC:(dd + 1) * TC],
                             start=(dd == 0), stop=False)
                ins = t.matmul(pQT[i % 2][:], bq_b[0:1, m * 128:(m + 1) * 128],
                               onesr_b[0:1, 0:TC], start=False, stop=True)
                tQ[(tc, m)] = sT.inc(ins)
            on("tensor", a2_q)

            def a2_ev(v, tc=tc, m=m, i=i):
                v.wait_ge(sT.h, tQ[(tc, m)])
                if (i - 1) in cD_qtw:
                    W(v, cD_qtw[i - 1])
                if (i - 1) in tQn:
                    v.wait_ge(sT.h, tQn[i - 1])
                v.tensor_copy(qtev[:], pQT[i % 2][:])
                sQe[i] = sV.inc(v.tensor_mul(qsq[:], qtev[:], qtev[:]))
            on("vector", a2_ev)

            def a2_qn(t, i=i):
                t.wait_ge(sV.h, sQe[i])
                t.wait_ge(sV.h, vQn[i - 1])
                tQn[i] = sT.inc(t.matmul(pQN[0:1, 0:TC], onesc_b[:], qsq[:],
                                         start=True, stop=True))
            on("tensor", a2_qn)

            def a2_qnv(v, i=i):
                v.wait_ge(sT.h, tQn[i])
                vQn[i] = sV.inc(v.reduce_max(qn_all[0:1, i:i + 1],
                                             pQN[0:1, 0:TC], AX.X))
            on("vector", a2_qnv)

            def a2_qw(sy, tc=tc, m=m, i=i):
                sy.wait_ge(sV.h, sQe[i])
                sem = sQW[i % 2]
                cD_qtw[i] = (sem, sem.dma(sy.dma_start(
                    qt_s[m * 128:(m + 1) * 128, tc * TC:(tc + 1) * TC], qtev[:])))
            on("sync", a2_qw)
        if m == H - 1:
            spass_emit(*spA2[tc])

    qtw_fin = [(s_, s_.n) for s_ in sQW]
    vQn_end = sV.n

    NQN = NTC * H
    NKN = HK * nkc

    def a2_nb_v(v):
        v.reduce_max(r11a[:], qn_all[0:1, 0:NQN], AX.X)
        v.reduce_max(r11b[:], kn_all[0:1, 0:NKN], AX.X)
        v.tensor_copy(ln_row[0:1, 0:TCA], var_row[0:1, 0:TCA])
        sV.inc(v.tensor_mul(r11a[:], r11a[:], r11b[:]))
    on("vector", a2_nb_v)
    vQK = sV.n

    def a2_nb_s(s):
        s.wait_ge(sV.h, vQK)
        s.activation(r11b[:], r11a[:], AF.Ln)
        s.activation(ln_row[0:1, 0:TCA], var_row[0:1, 0:TCA], AF.Copy)
        sS.inc(s.activation(r11a[:], r11b[:], AF.Exp, scale=0.5))
    on("scalar", a2_nb_s)
    sQK = sS.n

    def a2_nb_v2(v):
        v.wait_ge(sS.h, sQK)
        sV.inc(v.tensor_scalar(r11b[:], r11a[:], -SCL, -2.0, ALU.mult, ALU.add))
    on("vector", a2_nb_v2)
    vNB = sV.n

    def a2_nb_t(t):
        t.wait_ge(sV.h, vNB)
        t.wait_ge(sV.h, sQe[NTC * H - 1])
        sT.inc(t.matmul(pQT[1][:, 0:1], onesr_f[0:1, 0:128], r11b[0:1, 0:1],
                        start=True, stop=True))
    on("tensor", a2_nb_t)
    tNB = sT.n

    def a2_nb_c(s):
        s.wait_ge(sT.h, tNB)
        sS.inc(s.copy(negb[:], pQT[1][:, 0:1]))
    on("scalar", a2_nb_c)

    a2_endT, a2_endS, a2_endV = sT.n, sS.n, sV.n
    p_ctx.close()
    a2_ctx.close()

    oall_ctx = ExitStack()
    Oall = SB(oall_ctx, "Oall", [128, H * TOWN], BF16, side="right")

    b_ctx = ExitStack()
    qtb = [SB(b_ctx, f"qtb{i}", [128, QC], BF16) for i in range(2)]
    es = [SB(b_ctx, f"es{i}", [128, QC], BF16) for i in range(ESR)]
    zbs = SB(b_ctx, "zbs", [128, QC], F32)
    zrow = SB(b_ctx, "zrow", [1, QC], F32)
    zlog = SB(b_ctx, "zlog", [1, QC], F32)
    gneg = [SB(b_ctx, f"gneg{i}", [128, 1024], BF16) for i in range(2)]
    epair = [SB(b_ctx, f"epair{i}", [128, QC], BF16) for i in range(2)]
    equad = [SB(b_ctx, f"equad{i}", [128, QC], BF16) for i in range(2)]
    wqo = [SB(b_ctx, f"wqo{i}", [128, 1024], BF16) for i in range(2)]

    p_ctx = ExitStack()
    SP = [PS(p_ctx, "SP0", [128, QC]), PS(p_ctx, "SP1", [128, QC])]
    OAC = PS(p_ctx, "OAC", [128, QC])
    ZS = PS(p_ctx, "ZS", [1, QC])

    wt = {i: [] for i in range(NID)}
    niters = NID - W_START
    for k in range(NT):
        wt[W_START + (k * niters) // NT].append(k)

    tS_r = {}
    tAVZ = {}
    sEXP = {}
    for kt in range(NKT):
        tAVZ[(-1, kt)] = a2_endT
        sEXP[(-1, kt)] = a2_endS
    tZB = {-1: a2_endT}
    tZp = {}
    for pk in range(max(1, NKT // 4)):
        tZp[(-1, pk)] = a2_endT
    sZBS = {-1: a2_endS}
    sREC = {}
    vMul = {-1: a2_endV}
    cQT = {}
    cWF = {}
    vQt = {}
    vThr = a2_endV

    def qt_src(i):
        qc, h = divmod(i, H)
        return qt_s[h * 128:(h + 1) * 128, qc * QC:(qc + 1) * QC]

    def b_sync0(sy):
        for sv in qtw_fin:
            W(sy, sv)
        cQT[0] = (sQR[0], sQR[0].dma(sy.dma_start(qtb[0][:], qt_src(0))))
    on("sync", b_sync0)

    for idx in range(NID):
        qc, h = divmod(idx, H)
        kvh = h // G

        if idx == FIN_IDX:
            def fin_v0(v):
                v.wait_ge(sS.h, sAbs[NT - 1])
                for mat in range(3):
                    lo, hi = MATR[mat]
                    ins = v.tensor_reduce(sacc[:, mat:mat + 1],
                                          rsall[:, lo:hi + 1], AX.X, ALU.add)
                sV.inc(ins)
            on("vector", fin_v0)
            vRed3 = sV.n

            def fin_t1(t, idx=idx, vRed3=vRed3):
                t.wait_ge(sV.h, vRed3)
                t.wait_ge(sS.h, sREC[idx - 1])
                for mat in range(3):
                    ins = t.matmul(ZS[0:1, mat:mat + 1], onesc_f[:],
                                   sacc[:, mat:mat + 1], start=True, stop=True)
                sT.inc(ins)
            on("tensor", fin_t1)
            tFin1 = sT.n

            def fin_v1(v, tFin1=tFin1):
                v.wait_ge(sT.h, tFin1)
                v.tensor_copy(sacc_dbg[:], sacc[:])
                v.tensor_scalar(sthr_row[0:1, 0:3], ZS[0:1, 0:3],
                                0.5 / cfg["NW"], None, ALU.mult)
                sV.inc(v.tensor_scalar(sthr_row[0:1, 3:6], ZS[0:1, 0:3],
                                       1.0 / cfg["NW"], None, ALU.mult))
            on("vector", fin_v1)
            vFin1 = sV.n

            def fin_t2(t, idx=idx, vFin1=vFin1):
                t.wait_ge(sV.h, vFin1)
                t.wait_ge(sS.h, sZBS[idx - 1])
                sT.inc(t.matmul(SP[0][:, 0:6], onesr_f[0:1, 0:128],
                                sthr_row[0:1, 0:6], start=True, stop=True))
            on("tensor", fin_t2)
            tFin2 = sT.n

            def fin_v2(v, tFin2=tFin2):
                v.wait_ge(sT.h, tFin2)
                v.tensor_copy(thrP[:], SP[0][:, 0:3])
                v.tensor_copy(sF[:], SP[0][:, 3:6])
                sV.inc(v.tensor_scalar(thrN[:], SP[0][:, 0:3], -1.0, None, ALU.mult))
            on("vector", fin_v2)
            vThr = sV.n

        vBase = sV.n
        wlist = wt[idx]

        NPR_ = NKT // 2
        NQD_ = NKT // 4
        qq = []
        for t in wlist:
            qq.append((t, "lt"))
            qq.append((t, "stt"))
        sched = []
        qi = 0
        for pk in range(NPR_):
            sched.append(("pair", pk))
            if pk % 2 == 1:
                sched.append(("quad", pk // 2))
            want = ((pk + 1) * len(qq)) // NPR_
            while qi < want:
                sched.append(("q",) + qq[qi])
                qi += 1
        sched.append(("mult",))
        while qi < len(qq):
            sched.append(("q",) + qq[qi])
            qi += 1
        pos = {}
        _n = vBase
        for op in sched:
            if op[0] in ("pair", "quad"):
                _n += 1
                pos[(op[0], op[1])] = _n
            elif op[0] == "mult":
                _n += 1
                pos[("mult",)] = _n
            elif op[2] == "stt":
                _n += 1
                pos[("stt", op[1])] = _n

        def vq_pred(t, pos=pos):
            return pos[("stt", t)]

        def b_sync(sy, idx=idx, vBase=vBase, wlist=wlist):
            if idx + 1 < NID:
                if idx >= 1:
                    sy.wait_ge(sT.h, tS_r[(idx - 1, NKT - 1)])
                sem = sQR[(idx + 1) % 2]
                cQT[idx + 1] = (sem, sem.dma(sy.dma_start(qtb[(idx + 1) % 2][:],
                                                          qt_src(idx + 1))))
            for t in wlist:
                if t < 2:
                    sy.wait_ge(sS.h, sAbs[NT - 1])
                elif (t - 2) in vQt:
                    sy.wait_ge(sV.h, vQt[t - 2])
                else:
                    sy.wait_ge(sV.h, vq_pred(t - 2))
                mat, r0, c0, ncols = TILES[t]
                sem = sWF[t % 2]
                cWF[t] = (sem, sem.dma(sy.dma_start(
                    wfp[t % 2][:, 0:ncols], WSRC[mat][r0:r0 + 128, c0:c0 + ncols])))
        on("sync", b_sync)

        sBase = sS.n

        def b_tensor(t, idx=idx, qc=qc, h=h, kvh=kvh, sBase=sBase, vBase=vBase):
            NPR = NKT // 2

            def pexp(kt):
                return sBase + kt + 1

            def vpair(pk):
                return pos[("pair", pk)]

            def s_mm(kt, first=False):
                if first:
                    W(t, cQT[idx])
                    t.wait_ge(sS.h, sZBS[idx - 1])
                    if idx == FIN_IDX:
                        t.wait_ge(sV.h, vThr)
                elif kt == 1:
                    t.wait_ge(sS.h, sEXP[(idx - 1, NKT - 1)])
                else:
                    t.wait_ge(sS.h, pexp(kt - 2))
                for ns in range(NSC):
                    n0, n1 = ns * 512, min(QC, ns * 512 + 512)
                    ins = t.matmul(SP[kt % 2][:, n0:n1],
                                   KTs[:, kvh * S + kt * 128:kvh * S + (kt + 1) * 128],
                                   qtb[idx % 2][:, n0:n1], start=True, stop=True)
                tS_r[(idx, kt)] = sT.inc(ins)

            def av_mm(kt):
                t.wait_ge(sS.h, pexp(kt))
                if kt == 0:
                    t.wait_ge(sV.h, vMul[idx - 1])
                    if idx == FIN_IDX:
                        t.wait_ge(sV.h, vThr)
                for ns in range(NSC):
                    n0, n1 = ns * 512, min(QC, ns * 512 + 512)
                    ins = t.matmul(OAC[:, n0:n1],
                                   Vs[:, kt * KVD + kvh * DH:kt * KVD + (kvh + 1) * DH],
                                   es[kt % ESR][:, n0:n1],
                                   start=(kt == 0), stop=(kt == NKT - 1))
                tAVZ[(idx, kt)] = sT.inc(ins)

            def z_mm(qd):
                t.wait_ge(sV.h, pos[("quad", qd)])
                if qd == 0 and idx == FIN_IDX:
                    t.wait_ge(sV.h, vThr)
                for ns in range(NSC):
                    n0, n1 = ns * 512, min(QC, ns * 512 + 512)
                    ins = t.matmul(ZS[0:1, n0:n1], onesc_b[:], equad[qd % 2][:, n0:n1],
                                   start=(qd == 0), stop=(qd == NKT // 4 - 1))
                tZp[(idx, qd)] = sT.inc(ins)

            s_mm(0, first=True)
            if NKT > 1:
                s_mm(1)
            for kt in range(1, NKT):
                av_mm(kt - 1)
                if kt % 4 == 0:
                    z_mm(kt // 4 - 1)
                if kt + 1 < NKT:
                    s_mm(kt + 1)
            av_mm(NKT - 1)
            z_mm(NKT // 4 - 1)
            t.wait_ge(sS.h, sBase + NKT + 1)
            for ns in range(NSC):
                n0, n1 = ns * 512, min(QC, ns * 512 + 512)
                ins = t.matmul(SP[0][:, n0:n1], onesr_f[0:1, 0:128],
                               zrow[0:1, n0:n1], start=True, stop=True)
            tZB[idx] = sT.inc(ins)
        on("tensor", b_tensor)

        def b_scalar(s, idx=idx, sBase=sBase, vBase=vBase):
            NPR = NKT // 2
            for kt in range(NKT):
                s.wait_ge(sT.h, tS_r[(idx, kt)])
                if kt >= ESR:
                    s.wait_ge(sT.h, tAVZ[(idx, kt - ESR)])
                    s.wait_ge(sV.h, vBase + (kt - ESR) // 2 + 1)
                else:
                    s.wait_ge(sT.h, tAVZ[(idx - 1, NKT - ESR + kt)])
                    s.wait_ge(sT.h, tZp[(idx - 1, (NKT - ESR + kt) // 4)])
                c = sS.inc(s.activation(es[kt % ESR][:], SP[kt % 2][:], AF.Exp,
                                        bias=negb[:], scale=SCL))
                sEXP[(idx, kt)] = c
                assert c == sBase + kt + 1
            s.wait_ge(sT.h, tZp[(idx, NKT // 4 - 1)])
            s.activation(zlog[:], ZS[:], AF.Ln)
            sREC[idx] = sS.inc(s.activation(zrow[:], zlog[:], AF.Exp, scale=-1.0))
            s.wait_ge(sT.h, tZB[idx])
            sZBS[idx] = sS.inc(s.copy(zbs[:], SP[0][:]))
            assert sZBS[idx] == sBase + NKT + 2
        on("scalar", b_scalar)

        def b_vector(v, idx=idx, qc=qc, h=h, vBase=vBase, sBase=sBase,
                     sched=sched, pos=pos):
            NPR = NKT // 2
            for op in sched:
                if op[0] == "pair":
                    pk = op[1]
                    v.wait_ge(sS.h, sBase + 2 * pk + 2)
                    c = sV.inc(v.tensor_add(epair[pk % 2][:], es[(2 * pk) % ESR][:],
                                            es[(2 * pk + 1) % ESR][:]))
                    assert c == pos[("pair", pk)]
                elif op[0] == "quad":
                    qd = op[1]
                    if (idx, qd - 2) in tZp:
                        v.wait_ge(sT.h, tZp[(idx, qd - 2)])
                    elif (idx - 1, NKT // 4 - 2 + qd) in tZp:
                        v.wait_ge(sT.h, tZp[(idx - 1, NKT // 4 - 2 + qd)])
                    c = sV.inc(v.tensor_add(equad[qd % 2][:], epair[0][:],
                                            epair[1][:]))
                    assert c == pos[("quad", qd)]
                elif op[0] == "mult":
                    v.wait_ge(sS.h, sZBS[idx])
                    v.wait_ge(sT.h, tAVZ[(idx, NKT - 1)])
                    vMul[idx] = sV.inc(v.tensor_mul(
                        Oall[:, h * TOWN + qc * QC:h * TOWN + (qc + 1) * QC],
                        OAC[:], zbs[:]))
                    assert vMul[idx] == pos[("mult",)]
                else:
                    t, kind = op[1], op[2]
                    mat, r0, c0, ncols = TILES[t]
                    if kind == "lt":
                        W(v, cWF[t])
                        v.tensor_scalar(gneg[t % 2][:, 0:ncols],
                                        wfp[t % 2][:, 0:ncols],
                                        thrN[:, mat:mat + 1], None, ALU.is_lt)
                    else:
                        if t >= 2:
                            v.wait_ge(sWBG[t % 2].h, 16 * ((t - 2) // 2 + 1))
                        c = sV.inc(v.scalar_tensor_tensor(
                            wqo[t % 2][:, 0:ncols], wfp[t % 2][:, 0:ncols],
                            thrP[:, mat:mat + 1], gneg[t % 2][:, 0:ncols],
                            ALU.is_gt, ALU.subtract))
                        vQt[t] = c
                        assert c == pos[("stt", t)]
        on("vector", b_vector)

        def b_wbg(g, wlist=wlist):
            for t in wlist:
                g.wait_ge(sV.h, vQt[t])
                mat, r0, c0, ncols = TILES[t]
                c = sWBG[t % 2].dma(g.dma_start(
                    WDST[mat][r0:r0 + 128, c0:c0 + ncols], wqo[t % 2][:, 0:ncols]))
                assert c == 16 * (t // 2 + 1)
        if wlist:
            on("gpsimd", b_wbg)

    wb_fin = [(s_, s_.n) for s_ in sWBG]
    assert sWBG[0].n + sWBG[1].n == 16 * NT
    b_endT, b_endS, b_endV = sT.n, sS.n, sV.n
    p_ctx.close()
    b_ctx.close()
    ktv_ctx.close()

    c_ctx = ExitStack()
    bo_b = SB(c_ctx, "bo_b", [1, D], BF16)
    wob = [SB(c_ctx, f"wob{i}", [128, D], BF16) for i in range(2)]
    ao = [SB(c_ctx, f"ao{i}", [128, DD * TC], F32) for i in range(2)]
    xo = [SB(c_ctx, f"xo{i}", [128, TC], F32) for i in range(4)]
    aosq = [SB(c_ctx, f"aosq{i}", [128, TC], BF16) for i in range(4)]
    ysq = [SB(c_ctx, f"ysq{i}", [128, TC], BF16) for i in range(4)]
    ynt = SB(c_ctx, "ynt", [128, TC], BF16)

    def _gp_bo(g):
        g.wait_ge(sT.h, b_endT)
        g.wait_ge(sV.h, b_endV)
        g.wait_ge(sS.h, b_endS)
        sGC.dma(g.dma_start(bo_b[:], bod))
    on("gpsimd", _gp_bo)
    cGD_wo = (sGC, sGC.n)

    p_ctx = ExitStack()
    pAO = [PS(p_ctx, "pAO0", [128, TC]), PS(p_ctx, "pAO1", [128, TC])]
    pSTa = PS(p_ctx, "pSTa", [1, 1024])
    pSTb = PS(p_ctx, "pSTb", [1, 1024])
    pBCr = PS(p_ctx, "pBCr", [128, TC])
    pBCc = PS(p_ctx, "pBCc", [128, TC])

    cD_xo = {}
    cD_wo = {}
    cD_y = {}
    cD_yn = {}
    tAO = {}
    sAOe = {}
    tSts = {}
    vRes = {}
    vYn = {}
    vSt2 = {}
    vSt3 = {-1: b_endV}

    for tc2 in range(NTC):
        def c_x0(s, tc2=tc2):
            if tc2 == 0:
                s.wait_ge(sT.h, b_endT)
                s.wait_ge(sV.h, b_endV)
            else:
                s.wait_ge(sV.h, vYn[(tc2 - 1, DD - 1)])
            for dd in range(min(4, DD)):
                sem = sXO[dd % 4]
                cD_xo[(tc2, dd)] = (sem, sem.dma(s.dma_start(
                    xo[dd % 4][:],
                    xT[dd * 128:(dd + 1) * 128, tc2 * TC:(tc2 + 1) * TC])))
        on("scalar", c_x0)

        for dd in range(DD):
            def c_wol(sy, tc2=tc2, dd=dd):
                if tc2 == 0 and dd < 2:
                    sy.wait_ge(sT.h, b_endT)
                    sy.wait_ge(sS.h, b_endS)
                    sy.wait_ge(sV.h, b_endV)
                if (tc2, dd - 2) in tAO:
                    sy.wait_ge(sT.h, tAO[(tc2, dd - 2)])
                elif tc2 > 0 and (tc2 - 1, DD - 2 + dd) in tAO:
                    sy.wait_ge(sT.h, tAO[(tc2 - 1, DD - 2 + dd)])
                sem = sWO[dd % 2]
                cD_wo[(tc2, dd)] = (sem, sem.dma(sy.dma_start(
                    wob[dd % 2][:], wo_s[dd * 128:(dd + 1) * 128, :])))
            on("sync", c_wol)

            def c_ao(t, tc2=tc2, dd=dd):
                if tc2 == 0 and dd == 0:
                    W(t, cGD_wo)
                    t.wait_ge(sV.h, b_endV)
                    t.wait_ge(sS.h, b_endS)
                W(t, cD_wo[(tc2, dd)])
                if (tc2, dd - 2) in sAOe:
                    t.wait_ge(sV.h, sAOe[(tc2, dd - 2)])
                elif tc2 > 0 and (tc2 - 1, DD - 2 + dd) in sAOe:
                    t.wait_ge(sV.h, sAOe[(tc2 - 1, DD - 2 + dd)])
                for hh in range(H):
                    t.matmul(pAO[dd % 2][:],
                             wob[dd % 2][:, hh * 128:(hh + 1) * 128],
                             Oall[:, hh * TOWN + tc2 * TC:hh * TOWN + (tc2 + 1) * TC],
                             start=(hh == 0), stop=False)
                ins = t.matmul(pAO[dd % 2][:], bo_b[0:1, dd * 128:(dd + 1) * 128],
                               onesr_b[0:1, 0:TC], start=False, stop=True)
                tAO[(tc2, dd)] = sT.inc(ins)
            on("tensor", c_ao)

            def c_ev(v, tc2=tc2, dd=dd):
                v.wait_ge(sT.h, tAO[(tc2, dd)])
                if tc2 > 1:
                    v.wait_ge(sV.h, vYn[(tc2 - 2, dd)])
                    W(v, cD_y[(tc2 - 2, dd)])
                if (tc2, dd - 3) in tSts:
                    v.wait_ge(sT.h, tSts[(tc2, dd - 3)])
                sl = ao[tc2 % 2][:, dd * TC:(dd + 1) * TC]
                v.tensor_copy(sl, pAO[dd % 2][:])
                sAOe[(tc2, dd)] = sV.inc(v.tensor_mul(aosq[dd % 4][:], sl, sl))
            on("vector", c_ev)

            def c_st(t, tc2=tc2, dd=dd):
                t.wait_ge(sV.h, sAOe[(tc2, dd)])
                if dd == 0 and tc2 > 1:
                    t.wait_ge(sV.h, vSt2[tc2 - 2, "read"])
                t.matmul(pSTa[0:1, 0:TC], onesc_f[:], ao[tc2 % 2][:, dd * TC:(dd + 1) * TC],
                         start=(dd == 0), stop=(dd == DD - 1))
                ins = t.matmul(pSTa[0:1, 512:512 + TC], onesc_b[:], aosq[dd % 4][:],
                               start=(dd == 0), stop=(dd == DD - 1))
                tSts[(tc2, dd)] = sT.inc(ins)
            on("tensor", c_st)

        def c_stats_v(v, tc2=tc2):
            v.wait_ge(sT.h, tSts[(tc2, DD - 1)])
            v.tensor_scalar(m_row[0:1, 0:TC], pSTa[0:1, 0:TC], INV_D, None, ALU.mult)
            v.tensor_scalar(var_row[0:1, 0:TC], pSTa[0:1, 512:512 + TC], INV_D,
                            EPS, ALU.mult, ALU.add)
            v.tensor_mul(ln_row[0:1, 0:TC], m_row[0:1, 0:TC], m_row[0:1, 0:TC])
            vSt2[tc2] = sV.inc(v.tensor_sub(var_row[0:1, 0:TC], var_row[0:1, 0:TC],
                                            ln_row[0:1, 0:TC]))
            vSt2[tc2, "read"] = vSt2[tc2]
        on("vector", c_stats_v)

        def c_rstd(s, tc2=tc2):
            s.wait_ge(sV.h, vSt2[tc2])
            s.activation(ln_row[0:1, 0:TC], var_row[0:1, 0:TC], AF.Ln)
            sS.inc(s.activation(r_row[0:1, 0:TC], ln_row[0:1, 0:TC], AF.Exp, scale=-0.5))
        on("scalar", c_rstd)
        sR2 = sS.n

        def c_c2(v, sR2=sR2):
            v.wait_ge(sS.h, sR2)
            sV.inc(v.scalar_tensor_tensor(c2_row[0:1, 0:TC], m_row[0:1, 0:TC], -1.0,
                                          r_row[0:1, 0:TC], ALU.mult, ALU.mult))
        on("vector", c_c2)
        vC2c = sV.n

        def c_bc(t, tc2=tc2, vC2c=vC2c):
            t.wait_ge(sV.h, vC2c)
            if tc2 > 0:
                t.wait_ge(sV.h, vYn[(tc2 - 1, DD - 1)])
            t.matmul(pBCr[:], onesr_f[0:1, 0:128], r_row[0:1, 0:TC],
                     start=True, stop=True)
            sT.inc(t.matmul(pBCc[:], onesr_f[0:1, 0:128], c2_row[0:1, 0:TC],
                            start=True, stop=True))
        on("tensor", c_bc)
        tBC2 = sT.n

        for dd in range(DD):
            def c_res(v, tc2=tc2, dd=dd, tBC2=tBC2):
                if dd == 0:
                    v.wait_ge(sT.h, tBC2)
                W(v, cD_xo[(tc2, dd)])
                sl = ao[tc2 % 2][:, dd * TC:(dd + 1) * TC]
                v.tensor_mul(sl, sl, pBCr[:])
                v.tensor_add(sl, sl, pBCc[:])
                vRes[(tc2, dd)] = sV.inc(v.tensor_add(sl, sl, xo[dd % 4][:]))
            on("vector", c_res)

            def c_yw(g, tc2=tc2, dd=dd):
                g.wait_ge(sV.h, vRes[(tc2, dd)])
                cD_y[(tc2, dd)] = (sY, sY.dma(g.dma_start(
                    y_s[dd * 128:(dd + 1) * 128, tc2 * TC:(tc2 + 1) * TC],
                    ao[tc2 % 2][:, dd * TC:(dd + 1) * TC])))
            on("gpsimd", c_yw)

            def c_xo2(s, tc2=tc2, dd=dd):
                if dd + 4 < DD:
                    s.wait_ge(sV.h, vRes[(tc2, dd)])
                    sem = sXO[(dd + 4) % 4]
                    cD_xo[(tc2, dd + 4)] = (sem, sem.dma(s.dma_start(
                        xo[(dd + 4) % 4][:],
                        xT[(dd + 4) * 128:(dd + 5) * 128, tc2 * TC:(tc2 + 1) * TC])))
            on("scalar", c_xo2)

        for dd in range(DD):
            def c_ysq(v, tc2=tc2, dd=dd):
                if (tc2, dd - 3, "y") in tSts:
                    v.wait_ge(sT.h, tSts[(tc2, dd - 3, "y")])
                sl = ao[tc2 % 2][:, dd * TC:(dd + 1) * TC]
                sAOe[(tc2, dd, "y")] = sV.inc(v.tensor_mul(ysq[dd % 4][:], sl, sl))
            on("vector", c_ysq)

            def c_yst(t, tc2=tc2, dd=dd):
                t.wait_ge(sV.h, sAOe[(tc2, dd, "y")])
                if dd == 0 and tc2 > 1:
                    t.wait_ge(sV.h, vSt3[tc2 - 2])
                t.matmul(pSTb[0:1, 0:TC], onesc_f[:], ao[tc2 % 2][:, dd * TC:(dd + 1) * TC],
                         start=(dd == 0), stop=(dd == DD - 1))
                ins = t.matmul(pSTb[0:1, 512:512 + TC], onesc_b[:], ysq[dd % 4][:],
                               start=(dd == 0), stop=(dd == DD - 1))
                tSts[(tc2, dd, "y")] = sT.inc(ins)
            on("tensor", c_yst)

        def c_stats3(v, tc2=tc2):
            v.wait_ge(sT.h, tSts[(tc2, DD - 1, "y")])
            v.tensor_scalar(m_row[0:1, 0:TC], pSTb[0:1, 0:TC], INV_D, None, ALU.mult)
            v.tensor_scalar(var_row[0:1, 0:TC], pSTb[0:1, 512:512 + TC], INV_D,
                            EPS, ALU.mult, ALU.add)
            v.tensor_mul(ln_row[0:1, 0:TC], m_row[0:1, 0:TC], m_row[0:1, 0:TC])
            vSt3[tc2] = sV.inc(v.tensor_sub(var_row[0:1, 0:TC], var_row[0:1, 0:TC],
                                            ln_row[0:1, 0:TC]))
        on("vector", c_stats3)

        def c_rstd3(s, tc2=tc2):
            s.wait_ge(sV.h, vSt3[tc2])
            s.activation(ln_row[0:1, 0:TC], var_row[0:1, 0:TC], AF.Ln)
            sS.inc(s.activation(r_row[0:1, 0:TC], ln_row[0:1, 0:TC], AF.Exp, scale=-0.5))
        on("scalar", c_rstd3)
        sR3 = sS.n

        def c_c23(v, sR3=sR3):
            v.wait_ge(sS.h, sR3)
            sV.inc(v.scalar_tensor_tensor(c2_row[0:1, 0:TC], m_row[0:1, 0:TC], -1.0,
                                          r_row[0:1, 0:TC], ALU.mult, ALU.mult))
        on("vector", c_c23)
        vC23 = sV.n

        def c_bc3(t, tc2=tc2, vC23=vC23):
            t.wait_ge(sV.h, vC23)
            t.wait_ge(sV.h, vRes[(tc2, DD - 1)])
            t.matmul(pBCr[:], onesr_f[0:1, 0:128], r_row[0:1, 0:TC],
                     start=True, stop=True)
            sT.inc(t.matmul(pBCc[:], onesr_f[0:1, 0:128], c2_row[0:1, 0:TC],
                            start=True, stop=True))
        on("tensor", c_bc3)
        tBC3 = sT.n

        for dd in range(DD):
            def c_yn(v, tc2=tc2, dd=dd, tBC3=tBC3):
                if dd == 0:
                    v.wait_ge(sT.h, tBC3)
                if (tc2, dd - 1) in cD_yn:
                    W(v, cD_yn[(tc2, dd - 1)])
                elif tc2 > 0 and (tc2 - 1, DD - 1) in cD_yn:
                    W(v, cD_yn[(tc2 - 1, DD - 1)])
                v.tensor_mul(ynt[:], ao[tc2 % 2][:, dd * TC:(dd + 1) * TC], pBCr[:])
                vYn[(tc2, dd)] = sV.inc(v.tensor_add(ynt[:], ynt[:], pBCc[:]))
            on("vector", c_yn)

            def c_ynw(g, tc2=tc2, dd=dd):
                g.wait_ge(sV.h, vYn[(tc2, dd)])
                sem = sYNW[dd % 2]
                cD_yn[(tc2, dd)] = (sem, sem.dma(g.dma_start(
                    yn_s[dd * 128:(dd + 1) * 128, tc2 * TC:(tc2 + 1) * TC],
                    ynt[:])))
            on("gpsimd", c_ynw)

    c_endT, c_endS, c_endV = sT.n, sS.n, sV.n
    y_fin = (sY, sY.n)
    yn_fin = [(s_, s_.n) for s_ in sYNW]
    p_ctx.close()
    c_ctx.close()
    oall_ctx.close()

    d_ctx = ExitStack()
    yn_in = [SB(d_ctx, f"yn_in{i}", [128, DD * TC], BF16) for i in range(2)]
    gbuf = SB(d_ctx, "gbuf", [128, NHT * TC], BF16)
    w1b = [SB(d_ctx, f"w1b{i}", [128, D], BF16) for i in range(2)]
    w3b = [SB(d_ctx, f"w3b{i}", [128, D], BF16) for i in range(2)]
    w2b = [SB(d_ctx, f"w2b{i}", [128, HIDP], BF16) for i in range(2)]
    fbuf = SB(d_ctx, "fbuf", [128, DD * TC], F32)
    y16 = SB(d_ctx, "y16", [128, DD * TC], F32)
    sil = [SB(d_ctx, f"sil{i}", [128, TC], BF16) for i in range(2)]
    fsq = [SB(d_ctx, f"fsq{i}", [128, TC], BF16) for i in range(2)]

    p_ctx = ExitStack()
    pH1 = [PS(p_ctx, "pH1a", [128, TC]), PS(p_ctx, "pH1b", [128, TC])]
    pH3 = [PS(p_ctx, "pH3a", [128, TC]), PS(p_ctx, "pH3b", [128, TC])]
    pDO = [PS(p_ctx, "pDO0", [128, TC]), PS(p_ctx, "pDO1", [128, TC])]
    pSTd = PS(p_ctx, "pSTd", [1, 1024])

    cD_ynr = {}
    cD_w1 = {}
    cD_w2 = {}
    cD_y16 = {}
    tUP = {}
    sSil = {}
    vG = {}
    tDO = {}
    sFe = {}
    tStd = {}
    vOut = {}
    vSt4 = {}
    cOUT = {}

    for tc in range(NTC):
        def d_sync0(sy, tc=tc):
            if tc == 0:
                sy.wait_ge(sT.h, c_endT)
                sy.wait_ge(sS.h, c_endS)
                sy.wait_ge(sV.h, c_endV)
                W(sy, y_fin)
                for sv in yn_fin:
                    W(sy, sv)
                for sv in wb_fin:
                    W(sy, sv)
            else:
                if (tc - 2, NHT - 1) in tUP:
                    sy.wait_ge(sT.h, tUP[(tc - 2, NHT - 1)])
                sy.wait_ge(sV.h, vOut[(tc - 1, DD - 1)])
            cD_ynr[tc] = (sYNR, sYNR.dma(sy.dma_start(
                yn_in[tc % 2][:].rearrange("p (a s) -> p a s", s=TC),
                dv(yn_s, tc * TC, TC))))
            cD_y16[tc] = (sY16, sY16.dma(sy.dma_start(
                y16[:].rearrange("p (a s) -> p a s", s=TC), dv(y_s, tc * TC, TC))))
        on("sync", d_sync0)

        sSilBase = sS.n

        def psil(hc, sSilBase=sSilBase, tc=tc):
            sSil[(tc, hc)] = sSilBase + hc + 1
            return sSil[(tc, hc)]

        for hc in range(NHT):
            def d_wup(sy, tc=tc, hc=hc):
                if (tc, hc - 2) in tUP:
                    sy.wait_ge(sT.h, tUP[(tc, hc - 2)])
                elif tc > 0 and (tc - 1, NHT - 2 + hc) in tUP:
                    sy.wait_ge(sT.h, tUP[(tc - 1, NHT - 2 + hc)])
                sem = sW1[hc % 2]
                sem.dma(sy.dma_start(w1b[hc % 2][:],
                                     w1q_s[hc * 128:(hc + 1) * 128, :]))
                cD_w1[(tc, hc)] = (sem, sem.dma(sy.dma_start(
                    w3b[hc % 2][:], w3q_s[hc * 128:(hc + 1) * 128, :])))
            on("sync", d_wup)

            def d_up(t, tc=tc, hc=hc):
                W(t, cD_w1[(tc, hc)])
                if hc == 0:
                    W(t, cD_ynr[tc])
                    if tc == 0:
                        t.wait_ge(sV.h, c_endV)
                        t.wait_ge(sS.h, c_endS)
                if hc - 2 >= 0:
                    t.wait_ge(sS.h, psil(hc - 2))
                    t.wait_ge(sV.h, vG[(tc, hc - 2)])
                elif tc > 0 and (tc - 1, NHT - 2 + hc) in sSil:
                    t.wait_ge(sS.h, sSil[(tc - 1, NHT - 2 + hc)])
                    t.wait_ge(sV.h, vG[(tc - 1, NHT - 2 + hc)])
                for dd in range(DD):
                    t.matmul(pH1[hc % 2][:], w1b[hc % 2][:, dd * 128:(dd + 1) * 128],
                             yn_in[tc % 2][:, dd * TC:(dd + 1) * TC],
                             start=(dd == 0), stop=(dd == DD - 1))
                for dd in range(DD):
                    ins = t.matmul(pH3[hc % 2][:], w3b[hc % 2][:, dd * 128:(dd + 1) * 128],
                                   yn_in[tc % 2][:, dd * TC:(dd + 1) * TC],
                                   start=(dd == 0), stop=(dd == DD - 1))
                tUP[(tc, hc)] = sT.inc(ins)
            on("tensor", d_up)

            def d_g(v, tc=tc, hc=hc):
                v.wait_ge(sS.h, psil(hc))
                if tc > 0 and hc == 0:
                    v.wait_ge(sT.h, tDO[(tc - 1, DD - 1)])
                vG[(tc, hc)] = sV.inc(v.scalar_tensor_tensor(
                    gbuf[:, hc * TC:(hc + 1) * TC], pH3[hc % 2][:], sF[:, 1:2],
                    sil[hc % 2][:], ALU.mult, ALU.mult))
            on("vector", d_g)

        def d_sil_all(s, tc=tc, sSilBase=sSilBase):
            for hc in range(NHT):
                s.wait_ge(sT.h, tUP[(tc, hc)])
                if hc - 2 >= 0:
                    s.wait_ge(sV.h, vG[(tc, hc - 2)])
                c = sS.inc(s.activation(sil[hc % 2][:], pH1[hc % 2][:],
                                        AF.Silu, scale=sF[:, 0:1]))
                assert c == sSilBase + hc + 1
        on("scalar", d_sil_all)

        for dd in range(DD):
            def d_w2l(sy, tc=tc, dd=dd):
                if (tc, dd - 2) in tDO:
                    sy.wait_ge(sT.h, tDO[(tc, dd - 2)])
                elif tc > 0 and (tc - 1, DD - 2 + dd) in tDO:
                    sy.wait_ge(sT.h, tDO[(tc - 1, DD - 2 + dd)])
                sem = sW2[dd % 2]
                cD_w2[(tc, dd)] = (sem, sem.dma(sy.dma_start(
                    w2b[dd % 2][:], w2q_s[dd * 128:(dd + 1) * 128, :])))
            on("sync", d_w2l)

            def d_do(t, tc=tc, dd=dd):
                W(t, cD_w2[(tc, dd)])
                if dd == 0:
                    t.wait_ge(sV.h, vG[(tc, NHT - 1)])
                    if tc > 0:
                        t.wait_ge(sV.h, vOut[(tc - 1, DD - 1)])
                if (tc, dd - 2) in sFe:
                    t.wait_ge(sV.h, sFe[(tc, dd - 2)])
                elif tc > 0 and (tc - 1, DD - 2 + dd) in sFe:
                    t.wait_ge(sV.h, sFe[(tc - 1, DD - 2 + dd)])
                for hc in range(NHT):
                    ins = t.matmul(pDO[dd % 2][:],
                                   w2b[dd % 2][:, hc * 128:(hc + 1) * 128],
                                   gbuf[:, hc * TC:(hc + 1) * TC],
                                   start=(hc == 0), stop=(hc == NHT - 1))
                tDO[(tc, dd)] = sT.inc(ins)
            on("tensor", d_do)

            def d_fe(v, tc=tc, dd=dd):
                v.wait_ge(sT.h, tDO[(tc, dd)])
                if tc > 0 and dd == 0:
                    v.wait_ge(sOUT.h, cOUT[tc - 1])
                sl = fbuf[:, dd * TC:(dd + 1) * TC]
                v.tensor_scalar(sl, pDO[dd % 2][:], sF[:, 2:3], None, ALU.mult)
                sFe[(tc, dd)] = sV.inc(v.tensor_mul(fsq[dd % 2][:], sl, sl))
            on("vector", d_fe)

            def d_st(t, tc=tc, dd=dd):
                t.wait_ge(sV.h, sFe[(tc, dd)])
                if dd == 0 and tc > 0:
                    t.wait_ge(sV.h, vSt4[tc - 1])
                t.matmul(pSTd[0:1, 0:TC], onesc_f[:], fbuf[:, dd * TC:(dd + 1) * TC],
                         start=(dd == 0), stop=(dd == DD - 1))
                ins = t.matmul(pSTd[0:1, 512:512 + TC], onesc_b[:], fsq[dd % 2][:],
                               start=(dd == 0), stop=(dd == DD - 1))
                tStd[(tc, dd)] = sT.inc(ins)
            on("tensor", d_st)

        def d_stats(v, tc=tc):
            v.wait_ge(sT.h, tStd[(tc, DD - 1)])
            v.tensor_scalar(m_row[0:1, 0:TC], pSTd[0:1, 0:TC], INV_D, None, ALU.mult)
            v.tensor_scalar(var_row[0:1, 0:TC], pSTd[0:1, 512:512 + TC], INV_D,
                            EPS, ALU.mult, ALU.add)
            v.tensor_mul(ln_row[0:1, 0:TC], m_row[0:1, 0:TC], m_row[0:1, 0:TC])
            vSt4[tc] = sV.inc(v.tensor_sub(var_row[0:1, 0:TC], var_row[0:1, 0:TC],
                                           ln_row[0:1, 0:TC]))
        on("vector", d_stats)

        def d_rstd(s, tc=tc):
            s.wait_ge(sV.h, vSt4[tc])
            s.activation(ln_row[0:1, 0:TC], var_row[0:1, 0:TC], AF.Ln)
            sS.inc(s.activation(r_row[0:1, 0:TC], ln_row[0:1, 0:TC], AF.Exp, scale=-0.5))
        on("scalar", d_rstd)
        sR4 = sS.n

        def d_c2(v, sR4=sR4):
            v.wait_ge(sS.h, sR4)
            sV.inc(v.scalar_tensor_tensor(c2_row[0:1, 0:TC], m_row[0:1, 0:TC], -1.0,
                                          r_row[0:1, 0:TC], ALU.mult, ALU.mult))
        on("vector", d_c2)
        vC24 = sV.n

        def d_bc(t, tc=tc, vC24=vC24):
            t.wait_ge(sV.h, vC24)
            t.wait_ge(sV.h, sFe[(tc, DD - 1)])
            t.matmul(pDO[0][:], onesr_f[0:1, 0:128], r_row[0:1, 0:TC],
                     start=True, stop=True)
            sT.inc(t.matmul(pDO[1][:], onesr_f[0:1, 0:128], c2_row[0:1, 0:TC],
                            start=True, stop=True))
        on("tensor", d_bc)
        tBC4 = sT.n

        for dd in range(DD):
            def d_out(v, tc=tc, dd=dd, tBC4=tBC4):
                if dd == 0:
                    v.wait_ge(sT.h, tBC4)
                    W(v, cD_y16[tc])
                sl = fbuf[:, dd * TC:(dd + 1) * TC]
                v.tensor_mul(sl, sl, pDO[0][:])
                v.tensor_add(sl, sl, pDO[1][:])
                vOut[(tc, dd)] = sV.inc(v.tensor_add(sl, sl,
                                                     y16[:, dd * TC:(dd + 1) * TC]))
            on("vector", d_out)

        def d_ow(g, tc=tc):
            g.wait_ge(sV.h, vOut[(tc, DD - 1)])
            cOUT[tc] = sOUT.dma(g.dma_start(
                dv(outT, tc * TC, TC), fbuf[:].rearrange("p (a s) -> p a s", s=TC)))
        on("gpsimd", d_ow)

    if dump:
        dthr = nc.dram_tensor("dump_thr", [128, 3], F32, kind="ExternalOutput").ap()
        dsf = nc.dram_tensor("dump_sf", [128, 3], F32, kind="ExternalOutput").ap()
        dsacc = nc.dram_tensor("dump_sacc", [128, 3], F32, kind="ExternalOutput").ap()
        dmp = {}
        for nm, t_, rows, cols in (
                ("xn_s", xn_s, DD * 128, TOWN), ("qt_s", qt_s, H * 128, TOWN),
                ("y_s", y_s, DD * 128, TOWN), ("yn_s", yn_s, DD * 128, TOWN),
                ("w1q_s", w1q_s, NHT * 128, D), ("w3q_s", w3q_s, NHT * 128, D),
                ("w2q_s", w2q_s, DD * 128, HIDP)):
            dt_ = F32 if nm == "y_s" else BF16
            dmp[nm] = nc.dram_tensor("dump_" + nm, [rows, cols], dt_,
                                     kind="ExternalOutput").ap()

        def d_dump(sy):
            sy.wait_ge(sOUT.h, cOUT[NTC - 1])
            n = 0
            for nm, t_, rows, cols in (
                    ("xn_s", xn_s, DD * 128, TOWN), ("qt_s", qt_s, H * 128, TOWN),
                    ("y_s", y_s, DD * 128, TOWN), ("yn_s", yn_s, DD * 128, TOWN),
                    ("w1q_s", w1q_s, NHT * 128, D), ("w3q_s", w3q_s, NHT * 128, D),
                    ("w2q_s", w2q_s, DD * 128, HIDP)):
                sOUT.dma(sy.dma_start(dmp[nm], t_))
                n += 16
            sOUT.dma(sy.dma_start(dthr, thrP[:]))
            sOUT.dma(sy.dma_start(dsf, sF[:]))
            sOUT.dma(sy.dma_start(dsacc, sacc_dbg[:]))
        on("sync", d_dump)

    def d_final(sy):
        sy.wait_ge(sOUT.h, sOUT.n)
    on("sync", d_final)

    p_ctx.close()
    d_ctx.close()
    ctx.close()
    return nc



def _swizzle_w13(w, cfg):
    D, HID, HIDP, DD, NHT = cfg["D"], cfg["HID"], cfg["HIDP"], cfg["DD"], cfg["NHT"]
    wt = np.zeros((D, HIDP), np.float32)
    wt[:, :HID] = np.asarray(w, np.float32).T
    return np.ascontiguousarray(
        wt.reshape(DD, 128, NHT, 128).transpose(2, 1, 0, 3).reshape(NHT * 128, D))


def _swizzle_wo(w):
    w = np.asarray(w, np.float32)
    hd, d = w.shape
    H_, DD_ = hd // 128, d // 128
    return np.ascontiguousarray(
        w.reshape(H_, 128, DD_, 128).transpose(2, 1, 0, 3).reshape(d, hd))


def _swizzle_w2(w, cfg):
    D, HID, HIDP, DD, NHT = cfg["D"], cfg["HID"], cfg["HIDP"], cfg["DD"], cfg["NHT"]
    wt = np.zeros((HIDP, D), np.float32)
    wt[:HID, :] = np.asarray(w, np.float32).T
    return np.ascontiguousarray(
        wt.reshape(NHT, 128, DD, 128).transpose(2, 1, 0, 3).reshape(DD * 128, HIDP))


def make_in_maps(cfg, x, Wq, Bq, Wk, Bk, Wv, Bv, Wo, Bo, W1, W3, W2, n_cores):
    TOWN = cfg["TOWN"]
    shared = {
        "wq": np.ascontiguousarray(np.asarray(Wq, np.float32)),
        "wk": np.ascontiguousarray(np.asarray(Wk, np.float32)),
        "wv": np.ascontiguousarray(np.asarray(Wv, np.float32)),
        "wo": _swizzle_wo(Wo),
        "bq": np.ascontiguousarray(np.asarray(Bq, np.float32).reshape(1, -1)),
        "bk": np.ascontiguousarray(np.asarray(Bk, np.float32).reshape(1, -1)),
        "bv": np.ascontiguousarray(np.asarray(Bv, np.float32).reshape(1, -1)),
        "bo": np.ascontiguousarray(np.asarray(Bo, np.float32).reshape(1, -1)),
        "w1s": _swizzle_w13(W1, cfg),
        "w3s": _swizzle_w13(W3, cfg),
        "w2s": _swizzle_w2(W2, cfg),
    }
    in_maps = []
    nseq = x.shape[1] // TOWN
    for c in range(n_cores):
        b, half = divmod(c, nseq)
        xt = x[b].T
        own = xt[:, half * TOWN:(half + 1) * TOWN]
        parts = [own] + [xt[:, i * TOWN:(i + 1) * TOWN]
                         for i in range(nseq) if i != half]
        m = dict(shared)
        m["xT"] = np.ascontiguousarray(np.concatenate(parts, axis=1))
        in_maps.append(m)
    return in_maps


def kernel(L0_attn_inputs_q, L0_attn_wq, L0_attn_bq, L0_attn_wk, L0_attn_bk,
           L0_attn_wv, L0_attn_bv, L0_attn_wo, L0_attn_bo,
           L0_ffn_w1, L0_ffn_w3, L0_ffn_w2):
    cfg = REAL
    TOWN = cfg["TOWN"]
    x = np.asarray(L0_attn_inputs_q, np.float32)
    B, S, D = x.shape
    in_maps = make_in_maps(cfg, x, L0_attn_wq, L0_attn_bq, L0_attn_wk, L0_attn_bk,
                           L0_attn_wv, L0_attn_bv, L0_attn_wo, L0_attn_bo,
                           L0_ffn_w1, L0_ffn_w3, L0_ffn_w2, 8)
    nc = build_nc(cfg)
    res = run_bass_kernel_spmd(nc, in_maps, core_ids=list(range(8)))
    out = np.empty((B, S, D), np.float32)
    for c in range(8):
        b, half = divmod(c, 2)
        out[b, half * TOWN:(half + 1) * TOWN, :] = res.results[c]["outT"].T
    return out


if __name__ == "__main__":
    import time, sys
    cfgname = sys.argv[1] if len(sys.argv) > 1 else "SMALL"
    t0 = time.time()
    nc = build_nc(REAL if cfgname == "REAL" else SMALL)
    ni = sum(1 for _ in nc.all_instructions())
    print(f"built {cfgname} ok in {time.time()-t0:.1f}s, {ni} instructions")


# version 12
# speedup vs baseline: 1.0600x; 1.0600x over previous
import math
from contextlib import ExitStack

import numpy as np
import ml_dtypes

import concourse.bass as bass
import concourse.mybir as mybir
from concourse.bass_utils import run_bass_kernel_spmd

F32 = mybir.dt.float32
BF16 = mybir.dt.bfloat16
AF = mybir.ActivationFunctionType
ALU = mybir.AluOpType
AX = mybir.AxisListType

EPS = 1e-5


def make_cfg(D, S, TOWN, H, HK, HID, TCA, TC, QC):
    DH = 128
    cfg = dict(D=D, S=S, TOWN=TOWN, H=H, HK=HK, HID=HID, DH=DH)
    cfg["G"] = H // HK
    cfg["KVD"] = HK * DH
    cfg["DD"] = D // 128
    cfg["NKT"] = S // 128
    cfg["NHT"] = (HID + 127) // 128
    cfg["HIDP"] = cfg["NHT"] * 128
    cfg["TCA"] = TCA
    cfg["TC"] = TC
    cfg["QC"] = QC
    cfg["NCHA"] = S // TCA
    cfg["OWNA"] = TOWN // TCA
    cfg["NTC"] = TOWN // TC
    cfg["NQC"] = TOWN // QC
    cfg["NID"] = cfg["NQC"] * H
    cfg["NW"] = HID * D
    cfg["SCL"] = 1.0 / math.sqrt(DH)
    assert TCA <= 256 and TC <= 512 and QC <= 1024
    tiles = []
    for m in (0, 1):
        for r in range(cfg["NHT"]):
            c0 = 0
            while c0 < D:
                w = min(1024, D - c0)
                tiles.append((m, r * 128, c0, w))
                c0 += w
    for r in range(cfg["DD"]):
        c0 = 0
        while c0 < cfg["HIDP"]:
            w = min(1024, cfg["HIDP"] - c0)
            tiles.append((2, r * 128, c0, w))
            c0 += w
    cfg["TILES"] = tiles
    cfg["FIN_IDX"] = min(4, cfg["NID"] - 2)
    cfg["W_START"] = cfg["FIN_IDX"] + 1
    cfg["WQPOS"] = min(len(tiles), max(1, (2 * len(tiles)) // 3))
    return cfg


REAL = make_cfg(D=2048, S=4096, TOWN=2048, H=16, HK=4, HID=5461,
                TCA=256, TC=512, QC=1024)
SMALL = make_cfg(D=512, S=512, TOWN=256, H=4, HK=2, HID=300,
                 TCA=128, TC=128, QC=128)
MED = make_cfg(D=1024, S=2048, TOWN=2048, H=8, HK=2, HID=700,
               TCA=256, TC=512, QC=1024)


class Sem:
    def __init__(self, ctx, nc, name):
        self.h = ctx.enter_context(nc.semaphore(name))
        self.n = 0

    def inc(self, ins, k=1):
        ins.then_inc(self.h, k)
        self.n += k
        return self.n

    def dma(self, ins):
        return self.inc(ins, 16)


def build_nc(cfg, dump=False):
    D, S, TOWN = cfg["D"], cfg["S"], cfg["TOWN"]
    H, HK, G, DH, KVD = cfg["H"], cfg["HK"], cfg["G"], cfg["DH"], cfg["KVD"]
    DD, NKT, NHT, HIDP = cfg["DD"], cfg["NKT"], cfg["NHT"], cfg["HIDP"]
    TCA, TC, QC = cfg["TCA"], cfg["TC"], cfg["QC"]
    NCHA, OWNA, NTC, NID = cfg["NCHA"], cfg["OWNA"], cfg["NTC"], cfg["NID"]
    TILES, FIN_IDX, W_START, WQPOS = (cfg["TILES"], cfg["FIN_IDX"],
                                      cfg["W_START"], cfg["WQPOS"])
    SCL = cfg["SCL"]
    NT = len(TILES)
    INV_D = 1.0 / D
    NM = KVD // 128
    NTT = TCA // 128
    NSC = (QC + 511) // 512
    ESR = 4

    nc = bass.Bass("TRN2", target_bir_lowering=False, debug=False,
                   detect_race_conditions=False)

    xT = nc.dram_tensor("xT", [D, S], F32, kind="ExternalInput").ap()
    wq = nc.dram_tensor("wq", [D, D], F32, kind="ExternalInput").ap()
    wk = nc.dram_tensor("wk", [D, KVD], F32, kind="ExternalInput").ap()
    wv = nc.dram_tensor("wv", [D, KVD], F32, kind="ExternalInput").ap()
    wo = nc.dram_tensor("wo", [D, D], F32, kind="ExternalInput").ap()
    bqd = nc.dram_tensor("bq", [1, D], F32, kind="ExternalInput").ap()
    bkd = nc.dram_tensor("bk", [1, KVD], F32, kind="ExternalInput").ap()
    bvd = nc.dram_tensor("bv", [1, KVD], F32, kind="ExternalInput").ap()
    bod = nc.dram_tensor("bo", [1, D], F32, kind="ExternalInput").ap()
    w1s = nc.dram_tensor("w1s", [NHT * 128, D], F32, kind="ExternalInput").ap()
    w3s = nc.dram_tensor("w3s", [NHT * 128, D], F32, kind="ExternalInput").ap()
    w2s = nc.dram_tensor("w2s", [DD * 128, HIDP], F32, kind="ExternalInput").ap()
    outT = nc.dram_tensor("outT", [D, TOWN], F32, kind="ExternalOutput").ap()

    xn_s = nc.dram_tensor("xn_s", [DD * 128, TOWN], BF16, kind="Internal").ap()
    qt_s = nc.dram_tensor("qt_s", [H * 128, TOWN], BF16, kind="Internal").ap()
    y_s = nc.dram_tensor("y_s", [DD * 128, TOWN], F32, kind="Internal").ap()
    yn_s = nc.dram_tensor("yn_s", [DD * 128, TOWN], BF16, kind="Internal").ap()
    w1q_s = nc.dram_tensor("w1q_s", [NHT * 128, D], BF16, kind="Internal").ap()
    w3q_s = nc.dram_tensor("w3q_s", [NHT * 128, D], BF16, kind="Internal").ap()
    w2q_s = nc.dram_tensor("w2q_s", [DD * 128, HIDP], BF16, kind="Internal").ap()
    wo_s = nc.dram_tensor("wo_s", [DD * 128, D], BF16, kind="Internal").ap()
    WSRC = [w1s, w3s, w2s]
    WDST = [w1q_s, w3q_s, w2q_s]

    onesr_f_t = nc.inline_tensor(np.ones((1, 128), np.float32), "onesr_f")
    onesc_f_t = nc.inline_tensor(np.ones((128, 1), np.float32), "onesc_f")
    onesc_b_t = nc.inline_tensor(
        np.ones((128, 1), np.float32).astype(ml_dtypes.bfloat16), "onesc_b")
    onesr_b_t = nc.inline_tensor(
        np.ones((1, 512), np.float32).astype(ml_dtypes.bfloat16), "onesr_b")

    ctx = ExitStack()

    def SB(stack, name, shape, dt, side="left"):
        return stack.enter_context(nc.sbuf_tensor(name, shape, dt, side=side))

    onesr_f = SB(ctx, "c_onesr_f", [1, 128], F32)
    onesc_f = SB(ctx, "c_onesc_f", [128, 1], F32)
    onesc_b = SB(ctx, "c_onesc_b", [128, 1], BF16)
    onesr_b = SB(ctx, "c_onesr_b", [1, 512], BF16)
    negb = SB(ctx, "c_negb", [128, 1], F32)
    thrP = SB(ctx, "c_thrP", [128, 3], F32)
    thrN = SB(ctx, "c_thrN", [128, 3], F32)
    sF = SB(ctx, "c_sF", [128, 3], F32)
    sacc_dbg = SB(ctx, "c_sacc_dbg", [128, 3], F32)
    qn_all = SB(ctx, "c_qn_all", [1, 256], F32)
    kn_all = SB(ctx, "c_kn_all", [1, 256], F32)
    sthr_row = SB(ctx, "c_sthr", [1, 8], F32)
    qmax = SB(ctx, "c_qmax", [1, 1], F32)
    kmax = SB(ctx, "c_kmax", [1, 1], F32)
    r11a = SB(ctx, "c_r11a", [1, 1], F32)
    r11b = SB(ctx, "c_r11b", [1, 1], F32)
    m_row = SB(ctx, "c_m_row", [1, 512], F32)
    var_row = SB(ctx, "c_var_row", [1, 512], F32)
    ln_row = SB(ctx, "c_ln_row", [1, 512], F32)
    r_row = SB(ctx, "c_r_row", [1, 512], F32)
    c2_row = SB(ctx, "c_c2_row", [1, 512], F32)
    thr_row = SB(ctx, "c_thr_row", [1, 3], F32)

    sT = Sem(ctx, nc, "sT")
    sS = Sem(ctx, nc, "sS")
    sV = Sem(ctx, nc, "sV")
    sG = Sem(ctx, nc, "sG")
    sC = Sem(ctx, nc, "sC")
    sX = [Sem(ctx, nc, f"sX{i}") for i in range(2)]
    sXNW = [Sem(ctx, nc, f"sXNW{i}") for i in range(2)]
    sXN2 = Sem(ctx, nc, "sXN2")
    sQW = [Sem(ctx, nc, f"sQW{i}") for i in range(2)]
    sQR = [Sem(ctx, nc, f"sQR{i}") for i in range(2)]
    sWF = [Sem(ctx, nc, f"sWF{i}") for i in range(2)]
    sWB = [Sem(ctx, nc, f"sWB{i}") for i in range(2)]
    sXO = [Sem(ctx, nc, f"sXO{i}") for i in range(4)]
    sY = Sem(ctx, nc, "sY")
    sYNW = [Sem(ctx, nc, f"sYNW{i}") for i in range(2)]
    sYNR = Sem(ctx, nc, "sYNR")
    sY16 = Sem(ctx, nc, "sY16")
    sW1 = [Sem(ctx, nc, f"sW1{i}") for i in range(2)]
    sW2 = [Sem(ctx, nc, f"sW2{i}") for i in range(2)]
    sWO = [Sem(ctx, nc, f"sWO{i}") for i in range(2)]
    sGC = Sem(ctx, nc, "sGC")
    sGW = [Sem(ctx, nc, f"sGW{i}") for i in range(2)]
    sWBG = [Sem(ctx, nc, f"sWBG{i}") for i in range(2)]
    sOUT = Sem(ctx, nc, "sOUT")

    def W(eng, sv):
        eng.wait_ge(sv[0].h, sv[1])

    block = ctx.enter_context(nc.Block())

    def on(engname, fn):
        getattr(block, engname)(fn)

    def dv(ap, c0, n):
        return ap.rearrange("(a p) s -> p a s", p=128)[:, :, c0:c0 + n]

    def _pro_sync(sy):
        sC.dma(sy.dma_start(onesr_f[:], onesr_f_t.ap()))
        sC.dma(sy.dma_start(onesc_f[:], onesc_f_t.ap()))
        sC.dma(sy.dma_start(onesc_b[:], onesc_b_t.ap()))
        sC.dma(sy.dma_start(onesr_b[:], onesr_b_t.ap()))
    on("sync", _pro_sync)
    cD_const = (sC, sC.n)

    def _pro_vec(v):
        v.memset(qmax[:], 0.0)
        sV.inc(v.memset(kmax[:], 0.0))
    on("vector", _pro_vec)
    vInit = sV.n

    ktv_ctx = ExitStack()
    KTs = SB(ktv_ctx, "KTs", [128, HK * S], BF16)
    Vs = SB(ktv_ctx, "Vs", [128, NKT * KVD], BF16)
    wfp = [SB(ktv_ctx, f"wfp{i}", [128, 1024], F32) for i in range(2)]
    absd = SB(ktv_ctx, "absd", [128, 1024], BF16)
    rsall = SB(ktv_ctx, "rsall", [128, NT], F32)
    sacc = SB(ktv_ctx, "sacc", [128, 3], F32)

    a1_ctx = ExitStack()
    bk_b = SB(a1_ctx, "bk_b", [1, KVD], BF16)
    bv_b = SB(a1_ctx, "bv_b", [1, KVD], BF16)
    wkb = SB(a1_ctx, "wkb", [128, DD * KVD], BF16)
    wvb = SB(a1_ctx, "wvb", [128, DD * KVD], BF16)
    xbuf = [SB(a1_ctx, f"xbuf{i}", [128, DD * TCA], F32) for i in range(2)]
    xnbuf = [SB(a1_ctx, f"xnbuf{i}", [128, DD * TCA], BF16) for i in range(2)]
    xsq = [SB(a1_ctx, f"xsq{i}", [128, TCA], BF16) for i in range(2)]
    tmpx = SB(a1_ctx, "tmpx", [128, TCA], F32)

    def _pro_gp(g):
        sGC.dma(g.dma_start(bk_b[:], bkd))
        sGC.dma(g.dma_start(bv_b[:], bvd))
        sGC.dma(g.dma_start(wkb[:].rearrange("p (a s) -> p a s", s=KVD),
                            wk.rearrange("(a p) s -> p a s", p=128)))
        sGC.dma(g.dma_start(wvb[:].rearrange("p (a s) -> p a s", s=KVD),
                            wv.rearrange("(a p) s -> p a s", p=128)))
    on("gpsimd", _pro_gp)
    cGD_wkv = (sGC, sGC.n)

    cGW = {}
    sAbs = {}
    MATR = {}
    _c = 0
    for _t, (_m, _r, _cc, _n) in enumerate(TILES):
        MATR.setdefault(_m, [_t, _t])[1] = _t

    def spass_emit(t0, t1):
        if t0 >= t1:
            return
        sBase = sS.n

        def sp_g(g, t0=t0, t1=t1, sBase=sBase):
            for t in range(t0, t1):
                mat, r0, c0, ncols = TILES[t]
                if t - 2 >= t0:
                    g.wait_ge(sS.h, sBase + (t - 2 - t0) + 1)
                elif (t - 2) in sAbs:
                    g.wait_ge(sS.h, sAbs[t - 2])
                cGW[t] = (sGW[t % 2], sGW[t % 2].dma(g.dma_start(
                    wfp[t % 2][:, 0:ncols], WSRC[mat][r0:r0 + 128, c0:c0 + ncols])))
        on("gpsimd", sp_g)

        def sp_s(s, t0=t0, t1=t1, sBase=sBase):
            for t in range(t0, t1):
                mat, r0, c0, ncols = TILES[t]
                W(s, cGW[t])
                sAbs[t] = sS.inc(s.activation(absd[:, 0:ncols], wfp[t % 2][:, 0:ncols],
                                              AF.Abs, accum_out=rsall[:, t:t + 1]))
                assert sAbs[t] == sBase + (t - t0) + 1
        on("scalar", sp_s)

    NT_A1 = (3 * NT) // 4
    spA1 = [(NT_A1 * j // NCHA, NT_A1 * (j + 1) // NCHA) for j in range(NCHA)]
    spA2 = [(NT_A1 + (NT - NT_A1) * j // NTC, NT_A1 + (NT - NT_A1) * (j + 1) // NTC)
            for j in range(NTC)]

    p_ctx = ExitStack()

    def PS(stack, name, shape):
        return stack.enter_context(nc.psum_tensor(name, shape, F32))

    pKT = [PS(p_ctx, "pKT0", [128, TCA]), PS(p_ctx, "pKT1", [128, TCA])]
    pV = [PS(p_ctx, "pV0", [128, KVD]), PS(p_ctx, "pV1", [128, KVD])]
    pST = PS(p_ctx, "pST", [1, 1024])
    pBC = [PS(p_ctx, "pBC0", [128, 2 * TCA]), PS(p_ctx, "pBC1", [128, 2 * TCA])]

    chunk_order = list(range(OWNA, NCHA)) + list(range(OWNA))
    cD_x = {}
    cD_xnw = {}
    tSQ = {}
    tST = {-1: 0}
    vST = {-1: vInit}
    vXN = {-1: vInit, -2: vInit}
    tKV = {-1: 0, -2: 0}
    sKTe = {}
    sVe = {}

    for jj, j in enumerate(chunk_order):
        c0 = j * TCA
        own = j < OWNA
        p = jj % 2

        def a1_sync(sy, jj=jj, c0=c0, p=p):
            if jj == 0:
                W(sy, cD_const)
            sy.wait_ge(sV.h, vXN[jj - 2])
            cD_x[jj] = (sX[p], sX[p].dma(sy.dma_start(
                xbuf[p][:].rearrange("p (a s) -> p a s", s=TCA), dv(xT, c0, TCA))))
        on("sync", a1_sync)

        tBase = sT.n
        sSQ = {}

        def a1_sq(v, jj=jj, p=p, tBase=tBase, sSQ=sSQ):
            for dd in range(DD):
                if dd < 2:
                    if (jj - 1, DD - 2 + dd) in tSQ:
                        v.wait_ge(sT.h, tSQ[(jj - 1, DD - 2 + dd)])
                else:
                    v.wait_ge(sT.h, tBase + (dd - 2) + 1)
                if dd == 0:
                    W(v, cD_x[jj])
                sl = xbuf[p][:, dd * TCA:(dd + 1) * TCA]
                sSQ[dd] = sV.inc(v.tensor_mul(xsq[dd % 2][:], sl, sl))
        on("vector", a1_sq)

        def a1_stats_mm(t, jj=jj, p=p, tBase=tBase, sSQ=sSQ):
            for dd in range(DD):
                if dd == 0:
                    t.wait_ge(sV.h, vST[jj - 1])
                    W(t, cD_x[jj])
                t.matmul(pST[0:1, 0:TCA], onesc_f[:],
                         xbuf[p][:, dd * TCA:(dd + 1) * TCA],
                         start=(dd == 0), stop=(dd == DD - 1))
                t.wait_ge(sV.h, sSQ[dd])
                ins = t.matmul(pST[0:1, 512:512 + TCA], onesc_b[:], xsq[dd % 2][:],
                               start=(dd == 0), stop=(dd == DD - 1))
                tSQ[(jj, dd)] = sT.inc(ins)
                assert tSQ[(jj, dd)] == tBase + dd + 1
            tST[jj] = sT.n
        on("tensor", a1_stats_mm)

        def a1_stats_v(v, jj=jj):
            v.wait_ge(sT.h, tST[jj])
            v.tensor_scalar(m_row[0:1, 0:TCA], pST[0:1, 0:TCA], INV_D, None, ALU.mult)
            v.tensor_scalar(var_row[0:1, 0:TCA], pST[0:1, 512:512 + TCA], INV_D,
                            EPS, ALU.mult, ALU.add)
            v.tensor_mul(ln_row[0:1, 0:TCA], m_row[0:1, 0:TCA], m_row[0:1, 0:TCA])
            vST[jj] = sV.inc(v.tensor_sub(var_row[0:1, 0:TCA], var_row[0:1, 0:TCA],
                                          ln_row[0:1, 0:TCA]))
        on("vector", a1_stats_v)

        def a1_rstd(s, jj=jj):
            s.wait_ge(sV.h, vST[jj])
            s.activation(ln_row[0:1, 0:TCA], var_row[0:1, 0:TCA], AF.Ln)
            sS.inc(s.activation(r_row[0:1, 0:TCA], ln_row[0:1, 0:TCA], AF.Exp,
                                scale=-0.5))
        on("scalar", a1_rstd)
        sR1 = sS.n

        def a1_c2(v, sR1=sR1):
            v.wait_ge(sS.h, sR1)
            sV.inc(v.scalar_tensor_tensor(c2_row[0:1, 0:TCA], m_row[0:1, 0:TCA],
                                          -1.0, r_row[0:1, 0:TCA], ALU.mult, ALU.mult))
        on("vector", a1_c2)
        vC2 = sV.n

        def a1_bc(t, jj=jj, p=p, vC2=vC2):
            t.wait_ge(sV.h, vC2)
            t.wait_ge(sV.h, vXN[jj - 2])
            t.matmul(pBC[p][:, 0:TCA], onesr_f[0:1, 0:128], r_row[0:1, 0:TCA],
                     start=True, stop=True)
            sT.inc(t.matmul(pBC[p][:, TCA:2 * TCA], onesr_f[0:1, 0:128],
                            c2_row[0:1, 0:TCA], start=True, stop=True))
        on("tensor", a1_bc)
        tBC1 = sT.n

        def a1_xn(v, jj=jj, p=p, tBC1=tBC1):
            v.wait_ge(sT.h, tBC1)
            v.wait_ge(sT.h, tKV[jj - 2])
            if (jj - 2) in cD_xnw:
                W(v, cD_xnw[jj - 2])
            for dd in range(DD):
                v.tensor_mul(tmpx[:], xbuf[p][:, dd * TCA:(dd + 1) * TCA],
                             pBC[p][:, 0:TCA])
                ins = v.tensor_add(xnbuf[p][:, dd * TCA:(dd + 1) * TCA], tmpx[:],
                                   pBC[p][:, TCA:2 * TCA])
            vXN[jj] = sV.inc(ins)
        on("vector", a1_xn)

        if own:
            def a1_xnw(sy, jj=jj, c0=c0, p=p):
                sy.wait_ge(sV.h, vXN[jj])
                sem = sXNW[jj % 2]
                cD_xnw[jj] = (sem, sem.dma(sy.dma_start(
                    dv(xn_s, c0, TCA),
                    xnbuf[p][:].rearrange("p (a s) -> p a s", s=TCA))))
            on("sync", a1_xnw)

        for m in range(NM):
            def a1_k(t, jj=jj, c0=c0, p=p, m=m):
                if m == 0:
                    t.wait_ge(sV.h, vXN[jj])
                    if jj == 0:
                        W(t, cGD_wkv)
                if (jj, m - 2) in sKTe:
                    t.wait_ge(sV.h, sKTe[(jj, m - 2)])
                elif (jj - 1, NM - 2 + m) in sKTe:
                    t.wait_ge(sV.h, sKTe[(jj - 1, NM - 2 + m)])
                for dd in range(DD):
                    t.matmul(pKT[m % 2][:],
                             wkb[:, dd * KVD + m * 128:dd * KVD + (m + 1) * 128],
                             xnbuf[p][:, dd * TCA:(dd + 1) * TCA],
                             start=(dd == 0), stop=False)
                ins = t.matmul(pKT[m % 2][:], bk_b[0:1, m * 128:(m + 1) * 128],
                               onesr_b[0:1, 0:TCA], start=False, stop=True)
                sKTe[(jj, m, "mm")] = sT.inc(ins)
            on("tensor", a1_k)

            def a1_ke(v, jj=jj, c0=c0, m=m):
                v.wait_ge(sT.h, sKTe[(jj, m, "mm")])
                sKTe[(jj, m)] = sV.inc(v.tensor_copy(
                    KTs[:, m * S + c0:m * S + c0 + TCA], pKT[m % 2][:]))
            on("vector", a1_ke)

        for tt in range(NTT):
            def a1_v(t, jj=jj, c0=c0, p=p, tt=tt):
                if (jj, tt - 2) in sVe:
                    t.wait_ge(sV.h, sVe[(jj, tt - 2)])
                elif (jj - 1, NTT - 2 + tt) in sVe:
                    t.wait_ge(sV.h, sVe[(jj - 1, NTT - 2 + tt)])
                for dd in range(DD):
                    t.matmul(pV[tt % 2][:],
                             xnbuf[p][:, dd * TCA + tt * 128:dd * TCA + (tt + 1) * 128],
                             wvb[:, dd * KVD:(dd + 1) * KVD],
                             start=(dd == 0), stop=False)
                ins = t.matmul(pV[tt % 2][:], onesr_b[0:1, 0:128], bv_b[0:1, 0:KVD],
                               start=False, stop=True)
                sVe[(jj, tt, "mm")] = sT.inc(ins)
            on("tensor", a1_v)

            def a1_ve(v, jj=jj, c0=c0, tt=tt):
                v.wait_ge(sT.h, sVe[(jj, tt, "mm")])
                ktg = c0 // 128 + tt
                sVe[(jj, tt)] = sV.inc(v.tensor_copy(
                    Vs[:, ktg * KVD:(ktg + 1) * KVD], pV[tt % 2][:]))
            on("vector", a1_ve)
        tKV[jj] = sT.n
        spass_emit(*spA1[jj])

    a1_endT, a1_endS, a1_endV = sT.n, sS.n, sV.n
    a1_xnw_fin = [(s_, s_.n) for s_ in sXNW]
    p_ctx.close()
    a1_ctx.close()

    a2_ctx = ExitStack()
    bq_b = SB(a2_ctx, "bq_b", [1, D], BF16)
    wqb = SB(a2_ctx, "wqb", [128, DD * D], BF16)
    xn_in = SB(a2_ctx, "xn_in", [128, DD * TC], BF16)
    qtev = SB(a2_ctx, "qtev", [128, TC], BF16)
    qsq = SB(a2_ctx, "qsq", [128, TC], BF16)

    p_ctx = ExitStack()
    pQT = [PS(p_ctx, "pQT0", [128, TC]), PS(p_ctx, "pQT1", [128, TC])]
    pQN = PS(p_ctx, "pQN", [1, 512])

    def _gp_wq(g):
        g.wait_ge(sT.h, a1_endT)
        g.wait_ge(sS.h, a1_endS)
        g.wait_ge(sV.h, a1_endV)
        sGC.dma(g.dma_start(bq_b[:], bqd))
        sGC.dma(g.dma_start(wqb[:].rearrange("p (a s) -> p a s", s=D),
                            wq.rearrange("(a p) s -> p a s", p=128)))
        sGC.dma(g.dma_start(wo_s, wo))
    on("gpsimd", _gp_wq)
    cGD_wq = (sGC, sGC.n)

    sKq = {}
    tKn = {}
    vKn = {-1: a1_endV}
    KCW = min(512, S, TC)
    nkc = S // KCW
    for kvh in range(HK):
        for ch in range(nkc):
            i = kvh * nkc + ch

            def kn_s(v, kvh=kvh, ch=ch, i=i):
                if (i - 1) in tKn:
                    v.wait_ge(sT.h, tKn[i - 1])
                if i == 0:
                    v.wait_ge(sV.h, a1_endV)
                sl = KTs[:, kvh * S + ch * KCW:kvh * S + (ch + 1) * KCW]
                sKq[i] = sV.inc(v.tensor_mul(qsq[:, 0:KCW], sl, sl))
            on("vector", kn_s)

            def kn_t(t, i=i):
                t.wait_ge(sV.h, sKq[i])
                t.wait_ge(sV.h, vKn[i - 1])
                if i == 0:
                    t.wait_ge(sV.h, a1_endV)
                tKn[i] = sT.inc(t.matmul(pQN[0:1, 0:KCW], onesc_b[:],
                                         qsq[:, 0:KCW], start=True, stop=True))
            on("tensor", kn_t)

            def kn_v(v, i=i):
                v.wait_ge(sT.h, tKn[i])
                vKn[i] = sV.inc(v.reduce_max(kn_all[0:1, i:i + 1],
                                             pQN[0:1, 0:KCW], AX.X))
            on("vector", kn_v)
    vKn_end = sV.n

    cD_xn2 = {}
    cD_qtw = {}
    tQ = {}
    sQe = {}
    tQn = {}
    vQn = {-1: vKn_end}

    for tc in range(NTC):
        def a2_sync(sy, tc=tc):
            if tc == 0:
                sy.wait_ge(sT.h, a1_endT)
                sy.wait_ge(sS.h, a1_endS)
                sy.wait_ge(sV.h, a1_endV)
                for sv in a1_xnw_fin:
                    W(sy, sv)
            else:
                sy.wait_ge(sT.h, tQ[(tc - 1, H - 1)])
            cD_xn2[tc] = (sXN2, sXN2.dma(sy.dma_start(
                xn_in[:].rearrange("p (a s) -> p a s", s=TC), dv(xn_s, tc * TC, TC))))
        on("sync", a2_sync)

        for m in range(H):
            i = tc * H + m

            def a2_q(t, tc=tc, m=m, i=i):
                if m == 0:
                    W(t, cD_xn2[tc])
                if i == 0:
                    W(t, cGD_wq)
                    t.wait_ge(sS.h, a1_endS)
                    t.wait_ge(sV.h, a1_endV)
                if (i - 2) in sQe:
                    t.wait_ge(sV.h, sQe[i - 2])
                for dd in range(DD):
                    t.matmul(pQT[i % 2][:],
                             wqb[:, dd * D + m * 128:dd * D + (m + 1) * 128],
                             xn_in[:, dd * TC:(dd + 1) * TC],
                             start=(dd == 0), stop=False)
                ins = t.matmul(pQT[i % 2][:], bq_b[0:1, m * 128:(m + 1) * 128],
                               onesr_b[0:1, 0:TC], start=False, stop=True)
                tQ[(tc, m)] = sT.inc(ins)
            on("tensor", a2_q)

            def a2_ev(v, tc=tc, m=m, i=i):
                v.wait_ge(sT.h, tQ[(tc, m)])
                if (i - 1) in cD_qtw:
                    W(v, cD_qtw[i - 1])
                if (i - 1) in tQn:
                    v.wait_ge(sT.h, tQn[i - 1])
                v.tensor_copy(qtev[:], pQT[i % 2][:])
                sQe[i] = sV.inc(v.tensor_mul(qsq[:], qtev[:], qtev[:]))
            on("vector", a2_ev)

            def a2_qn(t, i=i):
                t.wait_ge(sV.h, sQe[i])
                t.wait_ge(sV.h, vQn[i - 1])
                tQn[i] = sT.inc(t.matmul(pQN[0:1, 0:TC], onesc_b[:], qsq[:],
                                         start=True, stop=True))
            on("tensor", a2_qn)

            def a2_qnv(v, i=i):
                v.wait_ge(sT.h, tQn[i])
                vQn[i] = sV.inc(v.reduce_max(qn_all[0:1, i:i + 1],
                                             pQN[0:1, 0:TC], AX.X))
            on("vector", a2_qnv)

            def a2_qw(sy, tc=tc, m=m, i=i):
                sy.wait_ge(sV.h, sQe[i])
                sem = sQW[i % 2]
                cD_qtw[i] = (sem, sem.dma(sy.dma_start(
                    qt_s[m * 128:(m + 1) * 128, tc * TC:(tc + 1) * TC], qtev[:])))
            on("sync", a2_qw)
        if m == H - 1:
            spass_emit(*spA2[tc])

    qtw_fin = [(s_, s_.n) for s_ in sQW]
    vQn_end = sV.n

    NQN = NTC * H
    NKN = HK * nkc

    def a2_nb_v(v):
        v.reduce_max(r11a[:], qn_all[0:1, 0:NQN], AX.X)
        v.reduce_max(r11b[:], kn_all[0:1, 0:NKN], AX.X)
        v.tensor_copy(ln_row[0:1, 0:TCA], var_row[0:1, 0:TCA])
        sV.inc(v.tensor_mul(r11a[:], r11a[:], r11b[:]))
    on("vector", a2_nb_v)
    vQK = sV.n

    def a2_nb_s(s):
        s.wait_ge(sV.h, vQK)
        s.activation(r11b[:], r11a[:], AF.Ln)
        s.activation(ln_row[0:1, 0:TCA], var_row[0:1, 0:TCA], AF.Copy)
        sS.inc(s.activation(r11a[:], r11b[:], AF.Exp, scale=0.5))
    on("scalar", a2_nb_s)
    sQK = sS.n

    def a2_nb_v2(v):
        v.wait_ge(sS.h, sQK)
        sV.inc(v.tensor_scalar(r11b[:], r11a[:], -SCL, -2.0, ALU.mult, ALU.add))
    on("vector", a2_nb_v2)
    vNB = sV.n

    def a2_nb_t(t):
        t.wait_ge(sV.h, vNB)
        t.wait_ge(sV.h, sQe[NTC * H - 1])
        sT.inc(t.matmul(pQT[1][:, 0:1], onesr_f[0:1, 0:128], r11b[0:1, 0:1],
                        start=True, stop=True))
    on("tensor", a2_nb_t)
    tNB = sT.n

    def a2_nb_c(s):
        s.wait_ge(sT.h, tNB)
        sS.inc(s.copy(negb[:], pQT[1][:, 0:1]))
    on("scalar", a2_nb_c)

    a2_endT, a2_endS, a2_endV = sT.n, sS.n, sV.n
    p_ctx.close()
    a2_ctx.close()

    oall_ctx = ExitStack()
    Oall = SB(oall_ctx, "Oall", [128, H * TOWN], BF16, side="right")

    b_ctx = ExitStack()
    qtb = [SB(b_ctx, f"qtb{i}", [128, QC], BF16) for i in range(2)]
    es = [SB(b_ctx, f"es{i}", [128, QC], BF16) for i in range(ESR)]
    zbs = SB(b_ctx, "zbs", [128, QC], F32)
    zrow = SB(b_ctx, "zrow", [1, QC], F32)
    zlog = SB(b_ctx, "zlog", [1, QC], F32)
    gneg = [SB(b_ctx, f"gneg{i}", [128, 1024], BF16) for i in range(2)]
    epair = [SB(b_ctx, f"epair{i}", [128, QC], BF16) for i in range(2)]
    equad = [SB(b_ctx, f"equad{i}", [128, QC], BF16) for i in range(2)]
    wqo = [SB(b_ctx, f"wqo{i}", [128, 1024], BF16) for i in range(2)]

    p_ctx = ExitStack()
    SP = [PS(p_ctx, "SP0", [128, QC]), PS(p_ctx, "SP1", [128, QC])]
    OAC = PS(p_ctx, "OAC", [128, QC])
    ZS = PS(p_ctx, "ZS", [1, QC])

    wt = {i: [] for i in range(NID)}
    niters = NID - W_START
    for k in range(NT):
        wt[W_START + (k * niters) // NT].append(k)

    tS_r = {}
    tAVZ = {}
    sEXP = {}
    for kt in range(NKT):
        tAVZ[(-1, kt)] = a2_endT
        sEXP[(-1, kt)] = a2_endS
    tZB = {-1: a2_endT}
    tZp = {}
    for pk in range(max(1, NKT // 4)):
        tZp[(-1, pk)] = a2_endT
    sZBS = {-1: a2_endS}
    sREC = {}
    vMul = {-1: a2_endV}
    cQT = {}
    cWF = {}
    vQt = {}
    vThr = a2_endV

    def qt_src(i):
        qc, h = divmod(i, H)
        return qt_s[h * 128:(h + 1) * 128, qc * QC:(qc + 1) * QC]

    def b_sync0(sy):
        for sv in qtw_fin:
            W(sy, sv)
        cQT[0] = (sQR[0], sQR[0].dma(sy.dma_start(qtb[0][:], qt_src(0))))
    on("sync", b_sync0)

    for idx in range(NID):
        qc, h = divmod(idx, H)
        kvh = h // G

        if idx == FIN_IDX:
            def fin_v0(v):
                v.wait_ge(sS.h, sAbs[NT - 1])
                for mat in range(3):
                    lo, hi = MATR[mat]
                    ins = v.tensor_reduce(sacc[:, mat:mat + 1],
                                          rsall[:, lo:hi + 1], AX.X, ALU.add)
                sV.inc(ins)
            on("vector", fin_v0)
            vRed3 = sV.n

            def fin_t1(t, idx=idx, vRed3=vRed3):
                t.wait_ge(sV.h, vRed3)
                t.wait_ge(sS.h, sREC[idx - 1])
                for mat in range(3):
                    ins = t.matmul(ZS[0:1, mat:mat + 1], onesc_f[:],
                                   sacc[:, mat:mat + 1], start=True, stop=True)
                sT.inc(ins)
            on("tensor", fin_t1)
            tFin1 = sT.n

            def fin_v1(v, tFin1=tFin1):
                v.wait_ge(sT.h, tFin1)
                v.tensor_copy(sacc_dbg[:], sacc[:])
                v.tensor_scalar(sthr_row[0:1, 0:3], ZS[0:1, 0:3],
                                0.5 / cfg["NW"], None, ALU.mult)
                sV.inc(v.tensor_scalar(sthr_row[0:1, 3:6], ZS[0:1, 0:3],
                                       1.0 / cfg["NW"], None, ALU.mult))
            on("vector", fin_v1)
            vFin1 = sV.n

            def fin_t2(t, idx=idx, vFin1=vFin1):
                t.wait_ge(sV.h, vFin1)
                t.wait_ge(sS.h, sZBS[idx - 1])
                sT.inc(t.matmul(SP[0][:, 0:6], onesr_f[0:1, 0:128],
                                sthr_row[0:1, 0:6], start=True, stop=True))
            on("tensor", fin_t2)
            tFin2 = sT.n

            def fin_v2(v, tFin2=tFin2):
                v.wait_ge(sT.h, tFin2)
                v.tensor_copy(thrP[:], SP[0][:, 0:3])
                v.tensor_copy(sF[:], SP[0][:, 3:6])
                sV.inc(v.tensor_scalar(thrN[:], SP[0][:, 0:3], -1.0, None, ALU.mult))
            on("vector", fin_v2)
            vThr = sV.n

        vBase = sV.n
        wlist = wt[idx]

        NPR_ = NKT // 2
        NQD_ = NKT // 4
        qq = []
        for t in wlist:
            qq.append((t, "lt"))
            qq.append((t, "stt"))
        sched = []
        qi = 0
        for pk in range(NPR_):
            sched.append(("pair", pk))
            if pk % 2 == 1:
                sched.append(("quad", pk // 2))
            want = ((pk + 1) * len(qq)) // NPR_
            while qi < want:
                sched.append(("q",) + qq[qi])
                qi += 1
        sched.append(("mult",))
        while qi < len(qq):
            sched.append(("q",) + qq[qi])
            qi += 1
        pos = {}
        _n = vBase
        for op in sched:
            if op[0] in ("pair", "quad"):
                _n += 1
                pos[(op[0], op[1])] = _n
            elif op[0] == "mult":
                _n += 1
                pos[("mult",)] = _n
            elif op[2] == "stt":
                _n += 1
                pos[("stt", op[1])] = _n

        def vq_pred(t, pos=pos):
            return pos[("stt", t)]

        def b_sync(sy, idx=idx, vBase=vBase, wlist=wlist):
            if idx + 1 < NID:
                if idx >= 1:
                    sy.wait_ge(sT.h, tS_r[(idx - 1, NKT - 1)])
                sem = sQR[(idx + 1) % 2]
                cQT[idx + 1] = (sem, sem.dma(sy.dma_start(qtb[(idx + 1) % 2][:],
                                                          qt_src(idx + 1))))
            for t in wlist:
                if t < 2:
                    sy.wait_ge(sS.h, sAbs[NT - 1])
                elif (t - 2) in vQt:
                    sy.wait_ge(sV.h, vQt[t - 2])
                else:
                    sy.wait_ge(sV.h, vq_pred(t - 2))
                mat, r0, c0, ncols = TILES[t]
                sem = sWF[t % 2]
                cWF[t] = (sem, sem.dma(sy.dma_start(
                    wfp[t % 2][:, 0:ncols], WSRC[mat][r0:r0 + 128, c0:c0 + ncols])))
        on("sync", b_sync)

        sBase = sS.n

        def b_tensor(t, idx=idx, qc=qc, h=h, kvh=kvh, sBase=sBase, vBase=vBase):
            NPR = NKT // 2

            def pexp(kt):
                return sBase + kt + 1

            def vpair(pk):
                return pos[("pair", pk)]

            def s_mm(kt, first=False):
                if first:
                    W(t, cQT[idx])
                    t.wait_ge(sS.h, sZBS[idx - 1])
                    if idx == FIN_IDX:
                        t.wait_ge(sV.h, vThr)
                elif kt == 1:
                    t.wait_ge(sS.h, sEXP[(idx - 1, NKT - 1)])
                else:
                    t.wait_ge(sS.h, pexp(kt - 2))
                for ns in range(NSC):
                    n0, n1 = ns * 512, min(QC, ns * 512 + 512)
                    ins = t.matmul(SP[kt % 2][:, n0:n1],
                                   KTs[:, kvh * S + kt * 128:kvh * S + (kt + 1) * 128],
                                   qtb[idx % 2][:, n0:n1], start=True, stop=True)
                tS_r[(idx, kt)] = sT.inc(ins)

            def av_mm(kt):
                t.wait_ge(sS.h, pexp(kt))
                if kt == 0:
                    t.wait_ge(sV.h, vMul[idx - 1])
                    if idx == FIN_IDX:
                        t.wait_ge(sV.h, vThr)
                for ns in range(NSC):
                    n0, n1 = ns * 512, min(QC, ns * 512 + 512)
                    ins = t.matmul(OAC[:, n0:n1],
                                   Vs[:, kt * KVD + kvh * DH:kt * KVD + (kvh + 1) * DH],
                                   es[kt % ESR][:, n0:n1],
                                   start=(kt == 0), stop=(kt == NKT - 1))
                tAVZ[(idx, kt)] = sT.inc(ins)

            def z_mm(qd):
                t.wait_ge(sV.h, pos[("quad", qd)])
                if qd == 0 and idx == FIN_IDX:
                    t.wait_ge(sV.h, vThr)
                for ns in range(NSC):
                    n0, n1 = ns * 512, min(QC, ns * 512 + 512)
                    ins = t.matmul(ZS[0:1, n0:n1], onesc_b[:], equad[qd % 2][:, n0:n1],
                                   start=(qd == 0), stop=(qd == NKT // 4 - 1))
                tZp[(idx, qd)] = sT.inc(ins)

            s_mm(0, first=True)
            if NKT > 1:
                s_mm(1)
            for kt in range(1, NKT):
                if kt + 1 < NKT:
                    s_mm(kt + 1)
                av_mm(kt - 1)
                if kt % 4 == 0:
                    z_mm(kt // 4 - 1)
            av_mm(NKT - 1)
            z_mm(NKT // 4 - 1)
            t.wait_ge(sS.h, sBase + NKT + 1)
            for ns in range(NSC):
                n0, n1 = ns * 512, min(QC, ns * 512 + 512)
                ins = t.matmul(SP[0][:, n0:n1], onesr_f[0:1, 0:128],
                               zrow[0:1, n0:n1], start=True, stop=True)
            tZB[idx] = sT.inc(ins)
        on("tensor", b_tensor)

        def b_scalar(s, idx=idx, sBase=sBase, vBase=vBase):
            NPR = NKT // 2
            for kt in range(NKT):
                s.wait_ge(sT.h, tS_r[(idx, kt)])
                if kt >= ESR:
                    s.wait_ge(sT.h, tAVZ[(idx, kt - ESR)])
                    s.wait_ge(sV.h, vBase + (kt - ESR) // 2 + 1)
                else:
                    s.wait_ge(sT.h, tAVZ[(idx - 1, NKT - ESR + kt)])
                    s.wait_ge(sT.h, tZp[(idx - 1, (NKT - ESR + kt) // 4)])
                c = sS.inc(s.activation(es[kt % ESR][:], SP[kt % 2][:], AF.Exp,
                                        bias=negb[:], scale=SCL))
                sEXP[(idx, kt)] = c
                assert c == sBase + kt + 1
            s.wait_ge(sT.h, tZp[(idx, NKT // 4 - 1)])
            s.activation(zlog[:], ZS[:], AF.Ln)
            sREC[idx] = sS.inc(s.activation(zrow[:], zlog[:], AF.Exp, scale=-1.0))
            s.wait_ge(sT.h, tZB[idx])
            sZBS[idx] = sS.inc(s.copy(zbs[:], SP[0][:]))
            assert sZBS[idx] == sBase + NKT + 2
        on("scalar", b_scalar)

        def b_vector(v, idx=idx, qc=qc, h=h, vBase=vBase, sBase=sBase,
                     sched=sched, pos=pos):
            NPR = NKT // 2
            for op in sched:
                if op[0] == "pair":
                    pk = op[1]
                    v.wait_ge(sS.h, sBase + 2 * pk + 2)
                    c = sV.inc(v.tensor_add(epair[pk % 2][:], es[(2 * pk) % ESR][:],
                                            es[(2 * pk + 1) % ESR][:]))
                    assert c == pos[("pair", pk)]
                elif op[0] == "quad":
                    qd = op[1]
                    if (idx, qd - 2) in tZp:
                        v.wait_ge(sT.h, tZp[(idx, qd - 2)])
                    elif (idx - 1, NKT // 4 - 2 + qd) in tZp:
                        v.wait_ge(sT.h, tZp[(idx - 1, NKT // 4 - 2 + qd)])
                    c = sV.inc(v.tensor_add(equad[qd % 2][:], epair[0][:],
                                            epair[1][:]))
                    assert c == pos[("quad", qd)]
                elif op[0] == "mult":
                    v.wait_ge(sS.h, sZBS[idx])
                    v.wait_ge(sT.h, tAVZ[(idx, NKT - 1)])
                    vMul[idx] = sV.inc(v.tensor_mul(
                        Oall[:, h * TOWN + qc * QC:h * TOWN + (qc + 1) * QC],
                        OAC[:], zbs[:]))
                    assert vMul[idx] == pos[("mult",)]
                else:
                    t, kind = op[1], op[2]
                    mat, r0, c0, ncols = TILES[t]
                    if kind == "lt":
                        W(v, cWF[t])
                        v.tensor_scalar(gneg[t % 2][:, 0:ncols],
                                        wfp[t % 2][:, 0:ncols],
                                        thrN[:, mat:mat + 1], None, ALU.is_lt)
                    else:
                        if t >= 2:
                            v.wait_ge(sWBG[t % 2].h, 16 * ((t - 2) // 2 + 1))
                        c = sV.inc(v.scalar_tensor_tensor(
                            wqo[t % 2][:, 0:ncols], wfp[t % 2][:, 0:ncols],
                            thrP[:, mat:mat + 1], gneg[t % 2][:, 0:ncols],
                            ALU.is_gt, ALU.subtract))
                        vQt[t] = c
                        assert c == pos[("stt", t)]
        on("vector", b_vector)

        def b_wbg(g, wlist=wlist):
            for t in wlist:
                g.wait_ge(sV.h, vQt[t])
                mat, r0, c0, ncols = TILES[t]
                c = sWBG[t % 2].dma(g.dma_start(
                    WDST[mat][r0:r0 + 128, c0:c0 + ncols], wqo[t % 2][:, 0:ncols]))
                assert c == 16 * (t // 2 + 1)
        if wlist:
            on("gpsimd", b_wbg)

    wb_fin = [(s_, s_.n) for s_ in sWBG]
    assert sWBG[0].n + sWBG[1].n == 16 * NT
    b_endT, b_endS, b_endV = sT.n, sS.n, sV.n
    p_ctx.close()
    b_ctx.close()
    ktv_ctx.close()

    c_ctx = ExitStack()
    bo_b = SB(c_ctx, "bo_b", [1, D], BF16)
    wob = [SB(c_ctx, f"wob{i}", [128, D], BF16) for i in range(2)]
    ao = [SB(c_ctx, f"ao{i}", [128, DD * TC], F32) for i in range(2)]
    xo = [SB(c_ctx, f"xo{i}", [128, TC], F32) for i in range(4)]
    aosq = [SB(c_ctx, f"aosq{i}", [128, TC], BF16) for i in range(4)]
    ysq = [SB(c_ctx, f"ysq{i}", [128, TC], BF16) for i in range(4)]
    ynt = SB(c_ctx, "ynt", [128, TC], BF16)

    def _gp_bo(g):
        g.wait_ge(sT.h, b_endT)
        g.wait_ge(sV.h, b_endV)
        g.wait_ge(sS.h, b_endS)
        sGC.dma(g.dma_start(bo_b[:], bod))
    on("gpsimd", _gp_bo)
    cGD_wo = (sGC, sGC.n)

    p_ctx = ExitStack()
    pAO = [PS(p_ctx, "pAO0", [128, TC]), PS(p_ctx, "pAO1", [128, TC])]
    pSTa = PS(p_ctx, "pSTa", [1, 1024])
    pSTb = PS(p_ctx, "pSTb", [1, 1024])
    pBCr = PS(p_ctx, "pBCr", [128, TC])
    pBCc = PS(p_ctx, "pBCc", [128, TC])

    cD_xo = {}
    cD_wo = {}
    cD_y = {}
    cD_yn = {}
    tAO = {}
    sAOe = {}
    tSts = {}
    vRes = {}
    vYn = {}
    vSt2 = {}
    vSt3 = {-1: b_endV}

    for tc2 in range(NTC):
        def c_x0(s, tc2=tc2):
            if tc2 == 0:
                s.wait_ge(sT.h, b_endT)
                s.wait_ge(sV.h, b_endV)
            else:
                s.wait_ge(sV.h, vRes[(tc2 - 1, DD - 1)])
            for dd in range(min(4, DD)):
                sem = sXO[dd % 4]
                cD_xo[(tc2, dd)] = (sem, sem.dma(s.dma_start(
                    xo[dd % 4][:],
                    xT[dd * 128:(dd + 1) * 128, tc2 * TC:(tc2 + 1) * TC])))
        on("scalar", c_x0)

        for dd in range(DD):
            def c_wol(sy, tc2=tc2, dd=dd):
                if tc2 == 0 and dd < 2:
                    sy.wait_ge(sT.h, b_endT)
                    sy.wait_ge(sS.h, b_endS)
                    sy.wait_ge(sV.h, b_endV)
                if (tc2, dd - 2) in tAO:
                    sy.wait_ge(sT.h, tAO[(tc2, dd - 2)])
                elif tc2 > 0 and (tc2 - 1, DD - 2 + dd) in tAO:
                    sy.wait_ge(sT.h, tAO[(tc2 - 1, DD - 2 + dd)])
                sem = sWO[dd % 2]
                cD_wo[(tc2, dd)] = (sem, sem.dma(sy.dma_start(
                    wob[dd % 2][:], wo_s[dd * 128:(dd + 1) * 128, :])))
            on("sync", c_wol)

            def c_ao(t, tc2=tc2, dd=dd):
                if tc2 == 0 and dd == 0:
                    W(t, cGD_wo)
                    t.wait_ge(sV.h, b_endV)
                    t.wait_ge(sS.h, b_endS)
                W(t, cD_wo[(tc2, dd)])
                if (tc2, dd - 2) in sAOe:
                    t.wait_ge(sS.h, sAOe[(tc2, dd - 2)])
                elif tc2 > 0 and (tc2 - 1, DD - 2 + dd) in sAOe:
                    t.wait_ge(sS.h, sAOe[(tc2 - 1, DD - 2 + dd)])
                for hh in range(H):
                    t.matmul(pAO[dd % 2][:],
                             wob[dd % 2][:, hh * 128:(hh + 1) * 128],
                             Oall[:, hh * TOWN + tc2 * TC:hh * TOWN + (tc2 + 1) * TC],
                             start=(hh == 0), stop=False)
                ins = t.matmul(pAO[dd % 2][:], bo_b[0:1, dd * 128:(dd + 1) * 128],
                               onesr_b[0:1, 0:TC], start=False, stop=True)
                tAO[(tc2, dd)] = sT.inc(ins)
            on("tensor", c_ao)

            def c_ev(s, tc2=tc2, dd=dd):
                s.wait_ge(sT.h, tAO[(tc2, dd)])
                if tc2 > 1:
                    s.wait_ge(sV.h, vYn[(tc2 - 2, dd)])
                    W(s, cD_y[(tc2 - 2, dd)])
                if (tc2, dd - 3) in tSts:
                    s.wait_ge(sT.h, tSts[(tc2, dd - 3)])
                sl = ao[tc2 % 2][:, dd * TC:(dd + 1) * TC]
                s.activation(sl, pAO[dd % 2][:], AF.Copy)
                sAOe[(tc2, dd)] = sS.inc(
                    s.activation(aosq[dd % 4][:], pAO[dd % 2][:], AF.Square))
            on("scalar", c_ev)

            def c_st(t, tc2=tc2, dd=dd):
                t.wait_ge(sS.h, sAOe[(tc2, dd)])
                if dd == 0 and tc2 > 1:
                    t.wait_ge(sV.h, vSt2[tc2 - 2, "read"])
                t.matmul(pSTa[0:1, 0:TC], onesc_f[:], ao[tc2 % 2][:, dd * TC:(dd + 1) * TC],
                         start=(dd == 0), stop=(dd == DD - 1))
                ins = t.matmul(pSTa[0:1, 512:512 + TC], onesc_b[:], aosq[dd % 4][:],
                               start=(dd == 0), stop=(dd == DD - 1))
                tSts[(tc2, dd)] = sT.inc(ins)
            on("tensor", c_st)

        def c_stats_v(v, tc2=tc2):
            v.wait_ge(sT.h, tSts[(tc2, DD - 1)])
            v.tensor_scalar(m_row[0:1, 0:TC], pSTa[0:1, 0:TC], INV_D, None, ALU.mult)
            v.tensor_scalar(var_row[0:1, 0:TC], pSTa[0:1, 512:512 + TC], INV_D,
                            EPS, ALU.mult, ALU.add)
            v.tensor_mul(ln_row[0:1, 0:TC], m_row[0:1, 0:TC], m_row[0:1, 0:TC])
            vSt2[tc2] = sV.inc(v.tensor_sub(var_row[0:1, 0:TC], var_row[0:1, 0:TC],
                                            ln_row[0:1, 0:TC]))
            vSt2[tc2, "read"] = vSt2[tc2]
        on("vector", c_stats_v)

        def c_rstd(s, tc2=tc2):
            s.wait_ge(sV.h, vSt2[tc2])
            s.activation(ln_row[0:1, 0:TC], var_row[0:1, 0:TC], AF.Ln)
            sS.inc(s.activation(r_row[0:1, 0:TC], ln_row[0:1, 0:TC], AF.Exp, scale=-0.5))
        on("scalar", c_rstd)
        sR2 = sS.n

        def c_c2(v, sR2=sR2):
            v.wait_ge(sS.h, sR2)
            sV.inc(v.scalar_tensor_tensor(c2_row[0:1, 0:TC], m_row[0:1, 0:TC], -1.0,
                                          r_row[0:1, 0:TC], ALU.mult, ALU.mult))
        on("vector", c_c2)
        vC2c = sV.n

        def c_bc(t, tc2=tc2, vC2c=vC2c):
            t.wait_ge(sV.h, vC2c)
            if tc2 > 0:
                t.wait_ge(sV.h, vYn[(tc2 - 1, DD - 1)])
            t.matmul(pBCr[:], onesr_f[0:1, 0:128], r_row[0:1, 0:TC],
                     start=True, stop=True)
            sT.inc(t.matmul(pBCc[:], onesr_f[0:1, 0:128], c2_row[0:1, 0:TC],
                            start=True, stop=True))
        on("tensor", c_bc)
        tBC2 = sT.n

        for dd in range(DD):
            def c_res(v, tc2=tc2, dd=dd, tBC2=tBC2):
                if dd == 0:
                    v.wait_ge(sT.h, tBC2)
                W(v, cD_xo[(tc2, dd)])
                sl = ao[tc2 % 2][:, dd * TC:(dd + 1) * TC]
                v.tensor_mul(sl, sl, pBCr[:])
                v.tensor_add(sl, sl, pBCc[:])
                vRes[(tc2, dd)] = sV.inc(v.tensor_add(sl, sl, xo[dd % 4][:]))
            on("vector", c_res)

            def c_yw(g, tc2=tc2, dd=dd):
                g.wait_ge(sV.h, vRes[(tc2, dd)])
                cD_y[(tc2, dd)] = (sY, sY.dma(g.dma_start(
                    y_s[dd * 128:(dd + 1) * 128, tc2 * TC:(tc2 + 1) * TC],
                    ao[tc2 % 2][:, dd * TC:(dd + 1) * TC])))
            on("gpsimd", c_yw)

            def c_xo2(s, tc2=tc2, dd=dd):
                if dd + 4 < DD:
                    s.wait_ge(sV.h, vRes[(tc2, dd)])
                    sem = sXO[(dd + 4) % 4]
                    cD_xo[(tc2, dd + 4)] = (sem, sem.dma(s.dma_start(
                        xo[(dd + 4) % 4][:],
                        xT[(dd + 4) * 128:(dd + 5) * 128, tc2 * TC:(tc2 + 1) * TC])))
            on("scalar", c_xo2)

        for dd in range(DD):
            def c_ysq(s, tc2=tc2, dd=dd):
                s.wait_ge(sV.h, vRes[(tc2, dd)])
                if (tc2, dd - 3, "y") in tSts:
                    s.wait_ge(sT.h, tSts[(tc2, dd - 3, "y")])
                sl = ao[tc2 % 2][:, dd * TC:(dd + 1) * TC]
                sAOe[(tc2, dd, "y")] = sS.inc(s.activation(ysq[dd % 4][:], sl,
                                                           AF.Square))
            on("scalar", c_ysq)

            def c_yst(t, tc2=tc2, dd=dd):
                t.wait_ge(sS.h, sAOe[(tc2, dd, "y")])
                if dd == 0 and tc2 > 1:
                    t.wait_ge(sV.h, vSt3[tc2 - 2])
                t.matmul(pSTb[0:1, 0:TC], onesc_f[:], ao[tc2 % 2][:, dd * TC:(dd + 1) * TC],
                         start=(dd == 0), stop=(dd == DD - 1))
                ins = t.matmul(pSTb[0:1, 512:512 + TC], onesc_b[:], ysq[dd % 4][:],
                               start=(dd == 0), stop=(dd == DD - 1))
                tSts[(tc2, dd, "y")] = sT.inc(ins)
            on("tensor", c_yst)

        def c_stats3(v, tc2=tc2):
            v.wait_ge(sT.h, tSts[(tc2, DD - 1, "y")])
            v.tensor_scalar(m_row[0:1, 0:TC], pSTb[0:1, 0:TC], INV_D, None, ALU.mult)
            v.tensor_scalar(var_row[0:1, 0:TC], pSTb[0:1, 512:512 + TC], INV_D,
                            EPS, ALU.mult, ALU.add)
            v.tensor_mul(ln_row[0:1, 0:TC], m_row[0:1, 0:TC], m_row[0:1, 0:TC])
            vSt3[tc2] = sV.inc(v.tensor_sub(var_row[0:1, 0:TC], var_row[0:1, 0:TC],
                                            ln_row[0:1, 0:TC]))
        on("vector", c_stats3)

        def c_rstd3(s, tc2=tc2):
            s.wait_ge(sV.h, vSt3[tc2])
            s.activation(ln_row[0:1, 0:TC], var_row[0:1, 0:TC], AF.Ln)
            sS.inc(s.activation(r_row[0:1, 0:TC], ln_row[0:1, 0:TC], AF.Exp, scale=-0.5))
        on("scalar", c_rstd3)
        sR3 = sS.n

        def c_c23(v, sR3=sR3):
            v.wait_ge(sS.h, sR3)
            sV.inc(v.scalar_tensor_tensor(c2_row[0:1, 0:TC], m_row[0:1, 0:TC], -1.0,
                                          r_row[0:1, 0:TC], ALU.mult, ALU.mult))
        on("vector", c_c23)
        vC23 = sV.n

        def c_bc3(t, tc2=tc2, vC23=vC23):
            t.wait_ge(sV.h, vC23)
            t.wait_ge(sV.h, vRes[(tc2, DD - 1)])
            t.matmul(pBCr[:], onesr_f[0:1, 0:128], r_row[0:1, 0:TC],
                     start=True, stop=True)
            sT.inc(t.matmul(pBCc[:], onesr_f[0:1, 0:128], c2_row[0:1, 0:TC],
                            start=True, stop=True))
        on("tensor", c_bc3)
        tBC3 = sT.n

        for dd in range(DD):
            def c_yn(v, tc2=tc2, dd=dd, tBC3=tBC3):
                if dd == 0:
                    v.wait_ge(sT.h, tBC3)
                if (tc2, dd - 1) in cD_yn:
                    W(v, cD_yn[(tc2, dd - 1)])
                elif tc2 > 0 and (tc2 - 1, DD - 1) in cD_yn:
                    W(v, cD_yn[(tc2 - 1, DD - 1)])
                v.tensor_mul(ynt[:], ao[tc2 % 2][:, dd * TC:(dd + 1) * TC], pBCr[:])
                vYn[(tc2, dd)] = sV.inc(v.tensor_add(ynt[:], ynt[:], pBCc[:]))
            on("vector", c_yn)

            def c_ynw(g, tc2=tc2, dd=dd):
                g.wait_ge(sV.h, vYn[(tc2, dd)])
                sem = sYNW[dd % 2]
                cD_yn[(tc2, dd)] = (sem, sem.dma(g.dma_start(
                    yn_s[dd * 128:(dd + 1) * 128, tc2 * TC:(tc2 + 1) * TC],
                    ynt[:])))
            on("gpsimd", c_ynw)

    c_endT, c_endS, c_endV = sT.n, sS.n, sV.n
    y_fin = (sY, sY.n)
    yn_fin = [(s_, s_.n) for s_ in sYNW]
    p_ctx.close()
    c_ctx.close()
    oall_ctx.close()

    d_ctx = ExitStack()
    yn_in = [SB(d_ctx, f"yn_in{i}", [128, DD * TC], BF16) for i in range(2)]
    gbuf = SB(d_ctx, "gbuf", [128, NHT * TC], BF16)
    w1b = [SB(d_ctx, f"w1b{i}", [128, D], BF16) for i in range(2)]
    w3b = [SB(d_ctx, f"w3b{i}", [128, D], BF16) for i in range(2)]
    w2b = [SB(d_ctx, f"w2b{i}", [128, HIDP], BF16) for i in range(2)]
    fbuf = SB(d_ctx, "fbuf", [128, DD * TC], F32)
    y16 = SB(d_ctx, "y16", [128, DD * TC], F32)
    sil = [SB(d_ctx, f"sil{i}", [128, TC], BF16) for i in range(2)]
    fsq = [SB(d_ctx, f"fsq{i}", [128, TC], BF16) for i in range(2)]

    p_ctx = ExitStack()
    pH1 = [PS(p_ctx, "pH1a", [128, TC]), PS(p_ctx, "pH1b", [128, TC])]
    pH3 = [PS(p_ctx, "pH3a", [128, TC]), PS(p_ctx, "pH3b", [128, TC])]
    pDO = [PS(p_ctx, "pDO0", [128, TC]), PS(p_ctx, "pDO1", [128, TC])]
    pSTd = PS(p_ctx, "pSTd", [1, 1024])

    cD_ynr = {}
    cD_w1 = {}
    cD_w2 = {}
    cD_y16 = {}
    tUP = {}
    sSil = {}
    vG = {}
    tDO = {}
    sFe = {}
    tStd = {}
    vOut = {}
    vSt4 = {}
    cOUT = {}

    for tc in range(NTC):
        if tc == 0:
            def d_sync0(sy, tc=tc):
                sy.wait_ge(sT.h, c_endT)
                sy.wait_ge(sS.h, c_endS)
                sy.wait_ge(sV.h, c_endV)
                W(sy, y_fin)
                for sv in yn_fin:
                    W(sy, sv)
                for sv in wb_fin:
                    W(sy, sv)
                cD_ynr[tc] = (sYNR, sYNR.dma(sy.dma_start(
                    yn_in[tc % 2][:].rearrange("p (a s) -> p a s", s=TC),
                    dv(yn_s, tc * TC, TC))))
                cD_y16[tc] = (sY16, sY16.dma(sy.dma_start(
                    y16[:].rearrange("p (a s) -> p a s", s=TC),
                    dv(y_s, tc * TC, TC))))
            on("sync", d_sync0)

        sSilBase = sS.n

        def psil(hc, sSilBase=sSilBase, tc=tc):
            sSil[(tc, hc)] = sSilBase + hc + 1
            return sSil[(tc, hc)]

        for hc in range(NHT):
            jprev = NHT - 1 - ((NHT - 1 - hc) % 2)

            def d_wup(sy, tc=tc, hc=hc, jprev=jprev):
                if (tc, hc - 2) in tUP:
                    sy.wait_ge(sT.h, tUP[(tc, hc - 2)])
                elif tc > 0 and (tc - 1, jprev) in tUP:
                    sy.wait_ge(sT.h, tUP[(tc - 1, jprev)])
                sem = sW1[hc % 2]
                sem.dma(sy.dma_start(w1b[hc % 2][:],
                                     w1q_s[hc * 128:(hc + 1) * 128, :]))
                cD_w1[(tc, hc)] = (sem, sem.dma(sy.dma_start(
                    w3b[hc % 2][:], w3q_s[hc * 128:(hc + 1) * 128, :])))
            on("sync", d_wup)

            def d_up(t, tc=tc, hc=hc, jprev=jprev):
                W(t, cD_w1[(tc, hc)])
                if hc == 0:
                    W(t, cD_ynr[tc])
                    if tc == 0:
                        t.wait_ge(sV.h, c_endV)
                        t.wait_ge(sS.h, c_endS)
                if hc - 2 >= 0:
                    t.wait_ge(sS.h, psil(hc - 2))
                    t.wait_ge(sV.h, vG[(tc, hc - 2)])
                elif tc > 0 and (tc - 1, jprev) in sSil:
                    t.wait_ge(sS.h, sSil[(tc - 1, jprev)])
                    t.wait_ge(sV.h, vG[(tc - 1, jprev)])
                for dd in range(DD):
                    t.matmul(pH1[hc % 2][:], w1b[hc % 2][:, dd * 128:(dd + 1) * 128],
                             yn_in[tc % 2][:, dd * TC:(dd + 1) * TC],
                             start=(dd == 0), stop=(dd == DD - 1))
                for dd in range(DD):
                    ins = t.matmul(pH3[hc % 2][:], w3b[hc % 2][:, dd * 128:(dd + 1) * 128],
                                   yn_in[tc % 2][:, dd * TC:(dd + 1) * TC],
                                   start=(dd == 0), stop=(dd == DD - 1))
                tUP[(tc, hc)] = sT.inc(ins)
            on("tensor", d_up)

            def d_g(v, tc=tc, hc=hc):
                v.wait_ge(sS.h, psil(hc))
                if tc > 0 and hc == 0:
                    v.wait_ge(sT.h, tDO[(tc - 1, DD - 1)])
                vG[(tc, hc)] = sV.inc(v.scalar_tensor_tensor(
                    gbuf[:, hc * TC:(hc + 1) * TC], pH3[hc % 2][:], sF[:, 1:2],
                    sil[hc % 2][:], ALU.mult, ALU.mult))
            on("vector", d_g)

        def d_sil_all(s, tc=tc, sSilBase=sSilBase):
            for hc in range(NHT):
                s.wait_ge(sT.h, tUP[(tc, hc)])
                if hc - 2 >= 0:
                    s.wait_ge(sV.h, vG[(tc, hc - 2)])
                c = sS.inc(s.activation(sil[hc % 2][:], pH1[hc % 2][:],
                                        AF.Silu, scale=sF[:, 0:1]))
                assert c == sSilBase + hc + 1
        on("scalar", d_sil_all)

        if tc + 1 < NTC:
            def d_ynl(sy, tc=tc):
                if (tc - 1, NHT - 1) in tUP:
                    sy.wait_ge(sT.h, tUP[(tc - 1, NHT - 1)])
                cD_ynr[tc + 1] = (sYNR, sYNR.dma(sy.dma_start(
                    yn_in[(tc + 1) % 2][:].rearrange("p (a s) -> p a s", s=TC),
                    dv(yn_s, (tc + 1) * TC, TC))))
            on("sync", d_ynl)

        if tc > 0:
            def d_y16l(sy, tc=tc):
                sy.wait_ge(sV.h, vOut[(tc - 1, DD - 1)])
                cD_y16[tc] = (sY16, sY16.dma(sy.dma_start(
                    y16[:].rearrange("p (a s) -> p a s", s=TC),
                    dv(y_s, tc * TC, TC))))
            on("sync", d_y16l)

        for dd in range(DD):
            def d_w2l(sy, tc=tc, dd=dd):
                if (tc, dd - 2) in tDO:
                    sy.wait_ge(sT.h, tDO[(tc, dd - 2)])
                elif tc > 0 and (tc - 1, DD - 2 + dd) in tDO:
                    sy.wait_ge(sT.h, tDO[(tc - 1, DD - 2 + dd)])
                sem = sW2[dd % 2]
                cD_w2[(tc, dd)] = (sem, sem.dma(sy.dma_start(
                    w2b[dd % 2][:], w2q_s[dd * 128:(dd + 1) * 128, :])))
            on("sync", d_w2l)

            def d_do(t, tc=tc, dd=dd):
                W(t, cD_w2[(tc, dd)])
                if dd == 0:
                    t.wait_ge(sV.h, vG[(tc, NHT - 1)])
                    if tc > 0:
                        t.wait_ge(sV.h, vOut[(tc - 1, DD - 1)])
                if (tc, dd - 2) in sFe:
                    t.wait_ge(sV.h, sFe[(tc, dd - 2)])
                elif tc > 0 and (tc - 1, DD - 2 + dd) in sFe:
                    t.wait_ge(sV.h, sFe[(tc - 1, DD - 2 + dd)])
                for hc in range(NHT):
                    ins = t.matmul(pDO[dd % 2][:],
                                   w2b[dd % 2][:, hc * 128:(hc + 1) * 128],
                                   gbuf[:, hc * TC:(hc + 1) * TC],
                                   start=(hc == 0), stop=(hc == NHT - 1))
                tDO[(tc, dd)] = sT.inc(ins)
            on("tensor", d_do)

            def d_fe(v, tc=tc, dd=dd):
                v.wait_ge(sT.h, tDO[(tc, dd)])
                if tc > 0 and dd == 0:
                    v.wait_ge(sOUT.h, cOUT[tc - 1])
                sl = fbuf[:, dd * TC:(dd + 1) * TC]
                v.tensor_scalar(sl, pDO[dd % 2][:], sF[:, 2:3], None, ALU.mult)
                sFe[(tc, dd)] = sV.inc(v.tensor_mul(fsq[dd % 2][:], sl, sl))
            on("vector", d_fe)

            def d_st(t, tc=tc, dd=dd):
                t.wait_ge(sV.h, sFe[(tc, dd)])
                if dd == 0 and tc > 0:
                    t.wait_ge(sV.h, vSt4[tc - 1])
                t.matmul(pSTd[0:1, 0:TC], onesc_f[:], fbuf[:, dd * TC:(dd + 1) * TC],
                         start=(dd == 0), stop=(dd == DD - 1))
                ins = t.matmul(pSTd[0:1, 512:512 + TC], onesc_b[:], fsq[dd % 2][:],
                               start=(dd == 0), stop=(dd == DD - 1))
                tStd[(tc, dd)] = sT.inc(ins)
            on("tensor", d_st)

        def d_stats(v, tc=tc):
            v.wait_ge(sT.h, tStd[(tc, DD - 1)])
            v.tensor_scalar(m_row[0:1, 0:TC], pSTd[0:1, 0:TC], INV_D, None, ALU.mult)
            v.tensor_scalar(var_row[0:1, 0:TC], pSTd[0:1, 512:512 + TC], INV_D,
                            EPS, ALU.mult, ALU.add)
            v.tensor_mul(ln_row[0:1, 0:TC], m_row[0:1, 0:TC], m_row[0:1, 0:TC])
            vSt4[tc] = sV.inc(v.tensor_sub(var_row[0:1, 0:TC], var_row[0:1, 0:TC],
                                           ln_row[0:1, 0:TC]))
        on("vector", d_stats)

        def d_rstd(s, tc=tc):
            s.wait_ge(sV.h, vSt4[tc])
            s.activation(ln_row[0:1, 0:TC], var_row[0:1, 0:TC], AF.Ln)
            sS.inc(s.activation(r_row[0:1, 0:TC], ln_row[0:1, 0:TC], AF.Exp, scale=-0.5))
        on("scalar", d_rstd)
        sR4 = sS.n

        def d_c2(v, sR4=sR4):
            v.wait_ge(sS.h, sR4)
            sV.inc(v.scalar_tensor_tensor(c2_row[0:1, 0:TC], m_row[0:1, 0:TC], -1.0,
                                          r_row[0:1, 0:TC], ALU.mult, ALU.mult))
        on("vector", d_c2)
        vC24 = sV.n

        def d_bc(t, tc=tc, vC24=vC24):
            t.wait_ge(sV.h, vC24)
            t.wait_ge(sV.h, sFe[(tc, DD - 1)])
            t.matmul(pDO[0][:], onesr_f[0:1, 0:128], r_row[0:1, 0:TC],
                     start=True, stop=True)
            sT.inc(t.matmul(pDO[1][:], onesr_f[0:1, 0:128], c2_row[0:1, 0:TC],
                            start=True, stop=True))
        on("tensor", d_bc)
        tBC4 = sT.n

        for dd in range(DD):
            def d_out(v, tc=tc, dd=dd, tBC4=tBC4):
                if dd == 0:
                    v.wait_ge(sT.h, tBC4)
                    W(v, cD_y16[tc])
                sl = fbuf[:, dd * TC:(dd + 1) * TC]
                v.tensor_mul(sl, sl, pDO[0][:])
                v.tensor_add(sl, sl, pDO[1][:])
                vOut[(tc, dd)] = sV.inc(v.tensor_add(sl, sl,
                                                     y16[:, dd * TC:(dd + 1) * TC]))
            on("vector", d_out)

        def d_ow(g, tc=tc):
            g.wait_ge(sV.h, vOut[(tc, DD - 1)])
            cOUT[tc] = sOUT.dma(g.dma_start(
                dv(outT, tc * TC, TC), fbuf[:].rearrange("p (a s) -> p a s", s=TC)))
        on("gpsimd", d_ow)

    if dump:
        dthr = nc.dram_tensor("dump_thr", [128, 3], F32, kind="ExternalOutput").ap()
        dsf = nc.dram_tensor("dump_sf", [128, 3], F32, kind="ExternalOutput").ap()
        dsacc = nc.dram_tensor("dump_sacc", [128, 3], F32, kind="ExternalOutput").ap()
        dmp = {}
        for nm, t_, rows, cols in (
                ("xn_s", xn_s, DD * 128, TOWN), ("qt_s", qt_s, H * 128, TOWN),
                ("y_s", y_s, DD * 128, TOWN), ("yn_s", yn_s, DD * 128, TOWN),
                ("w1q_s", w1q_s, NHT * 128, D), ("w3q_s", w3q_s, NHT * 128, D),
                ("w2q_s", w2q_s, DD * 128, HIDP)):
            dt_ = F32 if nm == "y_s" else BF16
            dmp[nm] = nc.dram_tensor("dump_" + nm, [rows, cols], dt_,
                                     kind="ExternalOutput").ap()

        def d_dump(sy):
            sy.wait_ge(sOUT.h, cOUT[NTC - 1])
            n = 0
            for nm, t_, rows, cols in (
                    ("xn_s", xn_s, DD * 128, TOWN), ("qt_s", qt_s, H * 128, TOWN),
                    ("y_s", y_s, DD * 128, TOWN), ("yn_s", yn_s, DD * 128, TOWN),
                    ("w1q_s", w1q_s, NHT * 128, D), ("w3q_s", w3q_s, NHT * 128, D),
                    ("w2q_s", w2q_s, DD * 128, HIDP)):
                sOUT.dma(sy.dma_start(dmp[nm], t_))
                n += 16
            sOUT.dma(sy.dma_start(dthr, thrP[:]))
            sOUT.dma(sy.dma_start(dsf, sF[:]))
            sOUT.dma(sy.dma_start(dsacc, sacc_dbg[:]))
        on("sync", d_dump)

    def d_final(sy):
        sy.wait_ge(sOUT.h, sOUT.n)
    on("sync", d_final)

    p_ctx.close()
    d_ctx.close()
    ctx.close()
    return nc



def _swizzle_w13(w, cfg):
    D, HID, HIDP, DD, NHT = cfg["D"], cfg["HID"], cfg["HIDP"], cfg["DD"], cfg["NHT"]
    wt = np.zeros((D, HIDP), np.float32)
    wt[:, :HID] = np.asarray(w, np.float32).T
    return np.ascontiguousarray(
        wt.reshape(DD, 128, NHT, 128).transpose(2, 1, 0, 3).reshape(NHT * 128, D))


def _swizzle_wo(w):
    w = np.asarray(w, np.float32)
    hd, d = w.shape
    H_, DD_ = hd // 128, d // 128
    return np.ascontiguousarray(
        w.reshape(H_, 128, DD_, 128).transpose(2, 1, 0, 3).reshape(d, hd))


def _swizzle_w2(w, cfg):
    D, HID, HIDP, DD, NHT = cfg["D"], cfg["HID"], cfg["HIDP"], cfg["DD"], cfg["NHT"]
    wt = np.zeros((HIDP, D), np.float32)
    wt[:HID, :] = np.asarray(w, np.float32).T
    return np.ascontiguousarray(
        wt.reshape(NHT, 128, DD, 128).transpose(2, 1, 0, 3).reshape(DD * 128, HIDP))


def make_in_maps(cfg, x, Wq, Bq, Wk, Bk, Wv, Bv, Wo, Bo, W1, W3, W2, n_cores):
    TOWN = cfg["TOWN"]
    shared = {
        "wq": np.ascontiguousarray(np.asarray(Wq, np.float32)),
        "wk": np.ascontiguousarray(np.asarray(Wk, np.float32)),
        "wv": np.ascontiguousarray(np.asarray(Wv, np.float32)),
        "wo": _swizzle_wo(Wo),
        "bq": np.ascontiguousarray(np.asarray(Bq, np.float32).reshape(1, -1)),
        "bk": np.ascontiguousarray(np.asarray(Bk, np.float32).reshape(1, -1)),
        "bv": np.ascontiguousarray(np.asarray(Bv, np.float32).reshape(1, -1)),
        "bo": np.ascontiguousarray(np.asarray(Bo, np.float32).reshape(1, -1)),
        "w1s": _swizzle_w13(W1, cfg),
        "w3s": _swizzle_w13(W3, cfg),
        "w2s": _swizzle_w2(W2, cfg),
    }
    in_maps = []
    nseq = x.shape[1] // TOWN
    for c in range(n_cores):
        b, half = divmod(c, nseq)
        xt = x[b].T
        own = xt[:, half * TOWN:(half + 1) * TOWN]
        parts = [own] + [xt[:, i * TOWN:(i + 1) * TOWN]
                         for i in range(nseq) if i != half]
        m = dict(shared)
        m["xT"] = np.ascontiguousarray(np.concatenate(parts, axis=1))
        in_maps.append(m)
    return in_maps


def kernel(L0_attn_inputs_q, L0_attn_wq, L0_attn_bq, L0_attn_wk, L0_attn_bk,
           L0_attn_wv, L0_attn_bv, L0_attn_wo, L0_attn_bo,
           L0_ffn_w1, L0_ffn_w3, L0_ffn_w2):
    cfg = REAL
    TOWN = cfg["TOWN"]
    x = np.asarray(L0_attn_inputs_q, np.float32)
    B, S, D = x.shape
    in_maps = make_in_maps(cfg, x, L0_attn_wq, L0_attn_bq, L0_attn_wk, L0_attn_bk,
                           L0_attn_wv, L0_attn_bv, L0_attn_wo, L0_attn_bo,
                           L0_ffn_w1, L0_ffn_w3, L0_ffn_w2, 8)
    nc = build_nc(cfg)
    res = run_bass_kernel_spmd(nc, in_maps, core_ids=list(range(8)))
    out = np.empty((B, S, D), np.float32)
    for c in range(8):
        b, half = divmod(c, 2)
        out[b, half * TOWN:(half + 1) * TOWN, :] = res.results[c]["outT"].T
    return out


if __name__ == "__main__":
    import time, sys
    cfgname = sys.argv[1] if len(sys.argv) > 1 else "SMALL"
    t0 = time.time()
    nc = build_nc(REAL if cfgname == "REAL" else SMALL)
    ni = sum(1 for _ in nc.all_instructions())
    print(f"built {cfgname} ok in {time.time()-t0:.1f}s, {ni} instructions")


# version 13 (final)
# speedup vs baseline: 1.0871x; 1.0256x over previous
import math
from contextlib import ExitStack

import numpy as np
import ml_dtypes

import concourse.bass as bass
import concourse.mybir as mybir
from concourse.bass_utils import run_bass_kernel_spmd

F32 = mybir.dt.float32
BF16 = mybir.dt.bfloat16
AF = mybir.ActivationFunctionType
ALU = mybir.AluOpType
AX = mybir.AxisListType

EPS = 1e-5


def make_cfg(D, S, TOWN, H, HK, HID, TCA, TC, QC):
    DH = 128
    cfg = dict(D=D, S=S, TOWN=TOWN, H=H, HK=HK, HID=HID, DH=DH)
    cfg["G"] = H // HK
    cfg["KVD"] = HK * DH
    cfg["DD"] = D // 128
    cfg["NKT"] = S // 128
    cfg["NHT"] = (HID + 127) // 128
    cfg["HIDP"] = cfg["NHT"] * 128
    cfg["TCA"] = TCA
    cfg["TC"] = TC
    cfg["QC"] = QC
    cfg["NCHA"] = S // TCA
    cfg["OWNA"] = TOWN // TCA
    cfg["NTC"] = TOWN // TC
    cfg["NQC"] = TOWN // QC
    cfg["NID"] = cfg["NQC"] * H
    cfg["NW"] = HID * D
    cfg["SCL"] = 1.0 / math.sqrt(DH)
    assert TCA <= 256 and TC <= 512 and QC <= 1024
    tiles = []
    for m in (0, 1):
        for r in range(cfg["NHT"]):
            c0 = 0
            while c0 < D:
                w = min(1024, D - c0)
                tiles.append((m, r * 128, c0, w))
                c0 += w
    for r in range(cfg["DD"]):
        c0 = 0
        while c0 < cfg["HIDP"]:
            w = min(1024, cfg["HIDP"] - c0)
            tiles.append((2, r * 128, c0, w))
            c0 += w
    cfg["TILES"] = tiles
    cfg["FIN_IDX"] = min(4, cfg["NID"] - 2)
    cfg["W_START"] = cfg["FIN_IDX"] + 1
    cfg["WQPOS"] = min(len(tiles), max(1, (2 * len(tiles)) // 3))
    return cfg


REAL = make_cfg(D=2048, S=4096, TOWN=2048, H=16, HK=4, HID=5461,
                TCA=256, TC=512, QC=1024)
SMALL = make_cfg(D=512, S=512, TOWN=256, H=4, HK=2, HID=300,
                 TCA=128, TC=128, QC=128)
MED = make_cfg(D=1024, S=2048, TOWN=2048, H=8, HK=2, HID=700,
               TCA=256, TC=512, QC=1024)


class Sem:
    def __init__(self, ctx, nc, name):
        self.h = ctx.enter_context(nc.semaphore(name))
        self.n = 0

    def inc(self, ins, k=1):
        ins.then_inc(self.h, k)
        self.n += k
        return self.n

    def dma(self, ins):
        return self.inc(ins, 16)


def build_nc(cfg, dump=False):
    D, S, TOWN = cfg["D"], cfg["S"], cfg["TOWN"]
    H, HK, G, DH, KVD = cfg["H"], cfg["HK"], cfg["G"], cfg["DH"], cfg["KVD"]
    DD, NKT, NHT, HIDP = cfg["DD"], cfg["NKT"], cfg["NHT"], cfg["HIDP"]
    TCA, TC, QC = cfg["TCA"], cfg["TC"], cfg["QC"]
    NCHA, OWNA, NTC, NID = cfg["NCHA"], cfg["OWNA"], cfg["NTC"], cfg["NID"]
    TILES, FIN_IDX, W_START, WQPOS = (cfg["TILES"], cfg["FIN_IDX"],
                                      cfg["W_START"], cfg["WQPOS"])
    SCL = cfg["SCL"]
    NT = len(TILES)
    INV_D = 1.0 / D
    NM = KVD // 128
    NTT = TCA // 128
    NSC = (QC + 511) // 512
    ESR = 4

    nc = bass.Bass("TRN2", target_bir_lowering=False, debug=False,
                   detect_race_conditions=False)

    xT = nc.dram_tensor("xT", [D, S], F32, kind="ExternalInput").ap()
    wq = nc.dram_tensor("wq", [D, D], F32, kind="ExternalInput").ap()
    wk = nc.dram_tensor("wk", [D, KVD], F32, kind="ExternalInput").ap()
    wv = nc.dram_tensor("wv", [D, KVD], F32, kind="ExternalInput").ap()
    wo = nc.dram_tensor("wo", [D, D], F32, kind="ExternalInput").ap()
    bqd = nc.dram_tensor("bq", [1, D], F32, kind="ExternalInput").ap()
    bkd = nc.dram_tensor("bk", [1, KVD], F32, kind="ExternalInput").ap()
    bvd = nc.dram_tensor("bv", [1, KVD], F32, kind="ExternalInput").ap()
    bod = nc.dram_tensor("bo", [1, D], F32, kind="ExternalInput").ap()
    w1s = nc.dram_tensor("w1s", [NHT * 128, D], F32, kind="ExternalInput").ap()
    w3s = nc.dram_tensor("w3s", [NHT * 128, D], F32, kind="ExternalInput").ap()
    w2s = nc.dram_tensor("w2s", [DD * 128, HIDP], F32, kind="ExternalInput").ap()
    outT = nc.dram_tensor("outT", [D, TOWN], F32, kind="ExternalOutput").ap()

    xn_s = nc.dram_tensor("xn_s", [DD * 128, TOWN], BF16, kind="Internal").ap()
    qt_s = nc.dram_tensor("qt_s", [H * 128, TOWN], BF16, kind="Internal").ap()
    y_s = nc.dram_tensor("y_s", [DD * 128, TOWN], F32, kind="Internal").ap()
    yn_s = nc.dram_tensor("yn_s", [DD * 128, TOWN], BF16, kind="Internal").ap()
    w1q_s = nc.dram_tensor("w1q_s", [NHT * 128, D], BF16, kind="Internal").ap()
    w3q_s = nc.dram_tensor("w3q_s", [NHT * 128, D], BF16, kind="Internal").ap()
    w2q_s = nc.dram_tensor("w2q_s", [DD * 128, HIDP], BF16, kind="Internal").ap()
    wo_s = nc.dram_tensor("wo_s", [DD * 128, D], BF16, kind="Internal").ap()
    WSRC = [w1s, w3s, w2s]
    WDST = [w1q_s, w3q_s, w2q_s]

    onesr_f_t = nc.inline_tensor(np.ones((1, 128), np.float32), "onesr_f")
    onesc_f_t = nc.inline_tensor(np.ones((128, 1), np.float32), "onesc_f")
    onesc_b_t = nc.inline_tensor(
        np.ones((128, 1), np.float32).astype(ml_dtypes.bfloat16), "onesc_b")
    onesr_b_t = nc.inline_tensor(
        np.ones((1, 512), np.float32).astype(ml_dtypes.bfloat16), "onesr_b")

    ctx = ExitStack()

    def SB(stack, name, shape, dt, side="left"):
        return stack.enter_context(nc.sbuf_tensor(name, shape, dt, side=side))

    onesr_f = SB(ctx, "c_onesr_f", [1, 128], F32)
    onesc_f = SB(ctx, "c_onesc_f", [128, 1], F32)
    onesc_b = SB(ctx, "c_onesc_b", [128, 1], BF16)
    onesr_b = SB(ctx, "c_onesr_b", [1, 512], BF16)
    negb = SB(ctx, "c_negb", [128, 1], F32)
    thrP = SB(ctx, "c_thrP", [128, 3], F32)
    thrN = SB(ctx, "c_thrN", [128, 3], F32)
    sF = SB(ctx, "c_sF", [128, 3], F32)
    sacc_dbg = SB(ctx, "c_sacc_dbg", [128, 3], F32)
    qn_all = SB(ctx, "c_qn_all", [1, 256], F32)
    kn_all = SB(ctx, "c_kn_all", [1, 256], F32)
    sthr_row = SB(ctx, "c_sthr", [1, 8], F32)
    qmax = SB(ctx, "c_qmax", [1, 1], F32)
    kmax = SB(ctx, "c_kmax", [1, 1], F32)
    r11a = SB(ctx, "c_r11a", [1, 1], F32)
    r11b = SB(ctx, "c_r11b", [1, 1], F32)
    m_row = SB(ctx, "c_m_row", [1, 512], F32)
    var_row = SB(ctx, "c_var_row", [1, 512], F32)
    ln_row = SB(ctx, "c_ln_row", [1, 512], F32)
    r_row = SB(ctx, "c_r_row", [1, 512], F32)
    c2_row = SB(ctx, "c_c2_row", [1, 512], F32)
    thr_row = SB(ctx, "c_thr_row", [1, 3], F32)

    sT = Sem(ctx, nc, "sT")
    sS = Sem(ctx, nc, "sS")
    sV = Sem(ctx, nc, "sV")
    sG = Sem(ctx, nc, "sG")
    sC = Sem(ctx, nc, "sC")
    sX = [Sem(ctx, nc, f"sX{i}") for i in range(2)]
    sXNW = [Sem(ctx, nc, f"sXNW{i}") for i in range(2)]
    sXN2 = Sem(ctx, nc, "sXN2")
    sQW = [Sem(ctx, nc, f"sQW{i}") for i in range(2)]
    sQR = [Sem(ctx, nc, f"sQR{i}") for i in range(2)]
    sWF = [Sem(ctx, nc, f"sWF{i}") for i in range(2)]
    sWB = [Sem(ctx, nc, f"sWB{i}") for i in range(2)]
    sXO = [Sem(ctx, nc, f"sXO{i}") for i in range(4)]
    sY = Sem(ctx, nc, "sY")
    sYNW = [Sem(ctx, nc, f"sYNW{i}") for i in range(2)]
    sYNR = Sem(ctx, nc, "sYNR")
    sY16 = Sem(ctx, nc, "sY16")
    sW1 = [Sem(ctx, nc, f"sW1{i}") for i in range(2)]
    sW2 = [Sem(ctx, nc, f"sW2{i}") for i in range(2)]
    sWO = [Sem(ctx, nc, f"sWO{i}") for i in range(2)]
    sGC = Sem(ctx, nc, "sGC")
    sGW = [Sem(ctx, nc, f"sGW{i}") for i in range(2)]
    sWBG = [Sem(ctx, nc, f"sWBG{i}") for i in range(2)]
    sOUT = Sem(ctx, nc, "sOUT")

    def W(eng, sv):
        eng.wait_ge(sv[0].h, sv[1])

    block = ctx.enter_context(nc.Block())

    def on(engname, fn):
        getattr(block, engname)(fn)

    def dv(ap, c0, n):
        return ap.rearrange("(a p) s -> p a s", p=128)[:, :, c0:c0 + n]

    def _pro_sync(sy):
        sC.dma(sy.dma_start(onesr_f[:], onesr_f_t.ap()))
        sC.dma(sy.dma_start(onesc_f[:], onesc_f_t.ap()))
        sC.dma(sy.dma_start(onesc_b[:], onesc_b_t.ap()))
        sC.dma(sy.dma_start(onesr_b[:], onesr_b_t.ap()))
    on("sync", _pro_sync)
    cD_const = (sC, sC.n)

    def _pro_vec(v):
        v.memset(qmax[:], 0.0)
        sV.inc(v.memset(kmax[:], 0.0))
    on("vector", _pro_vec)
    vInit = sV.n

    ktv_ctx = ExitStack()
    KTs = SB(ktv_ctx, "KTs", [128, HK * S], BF16)
    Vs = SB(ktv_ctx, "Vs", [128, NKT * KVD], BF16)
    wfp = [SB(ktv_ctx, f"wfp{i}", [128, 1024], F32) for i in range(2)]
    absd = SB(ktv_ctx, "absd", [128, 1024], BF16)
    rsall = SB(ktv_ctx, "rsall", [128, NT], F32)
    sacc = SB(ktv_ctx, "sacc", [128, 3], F32)

    a1_ctx = ExitStack()
    bk_b = SB(a1_ctx, "bk_b", [1, KVD], BF16)
    bv_b = SB(a1_ctx, "bv_b", [1, KVD], BF16)
    wkb = SB(a1_ctx, "wkb", [128, DD * KVD], BF16)
    wvb = SB(a1_ctx, "wvb", [128, DD * KVD], BF16)
    xbuf = [SB(a1_ctx, f"xbuf{i}", [128, DD * TCA], F32) for i in range(2)]
    xnbuf = [SB(a1_ctx, f"xnbuf{i}", [128, DD * TCA], BF16) for i in range(2)]
    xsq = [SB(a1_ctx, f"xsq{i}", [128, TCA], BF16) for i in range(2)]
    tmpx = SB(a1_ctx, "tmpx", [128, TCA], F32)

    def _pro_gp(g):
        sGC.dma(g.dma_start(bk_b[:], bkd))
        sGC.dma(g.dma_start(bv_b[:], bvd))
        sGC.dma(g.dma_start(wkb[:].rearrange("p (a s) -> p a s", s=KVD),
                            wk.rearrange("(a p) s -> p a s", p=128)))
        sGC.dma(g.dma_start(wvb[:].rearrange("p (a s) -> p a s", s=KVD),
                            wv.rearrange("(a p) s -> p a s", p=128)))
    on("gpsimd", _pro_gp)
    cGD_wkv = (sGC, sGC.n)

    cGW = {}
    sAbs = {}
    MATR = {}
    _c = 0
    for _t, (_m, _r, _cc, _n) in enumerate(TILES):
        MATR.setdefault(_m, [_t, _t])[1] = _t

    def spass_emit(t0, t1):
        if t0 >= t1:
            return
        sBase = sS.n

        def sp_g(g, t0=t0, t1=t1, sBase=sBase):
            for t in range(t0, t1):
                mat, r0, c0, ncols = TILES[t]
                if t - 2 >= t0:
                    g.wait_ge(sS.h, sBase + (t - 2 - t0) + 1)
                elif (t - 2) in sAbs:
                    g.wait_ge(sS.h, sAbs[t - 2])
                cGW[t] = (sGW[t % 2], sGW[t % 2].dma(g.dma_start(
                    wfp[t % 2][:, 0:ncols], WSRC[mat][r0:r0 + 128, c0:c0 + ncols])))
        on("gpsimd", sp_g)

        def sp_s(s, t0=t0, t1=t1, sBase=sBase):
            for t in range(t0, t1):
                mat, r0, c0, ncols = TILES[t]
                W(s, cGW[t])
                sAbs[t] = sS.inc(s.activation(absd[:, 0:ncols], wfp[t % 2][:, 0:ncols],
                                              AF.Abs, accum_out=rsall[:, t:t + 1]))
                assert sAbs[t] == sBase + (t - t0) + 1
        on("scalar", sp_s)

    NT_A1 = (3 * NT) // 4
    spA1 = [(NT_A1 * j // NCHA, NT_A1 * (j + 1) // NCHA) for j in range(NCHA)]
    spA2 = [(NT_A1 + (NT - NT_A1) * j // NTC, NT_A1 + (NT - NT_A1) * (j + 1) // NTC)
            for j in range(NTC)]

    p_ctx = ExitStack()

    def PS(stack, name, shape):
        return stack.enter_context(nc.psum_tensor(name, shape, F32))

    pKT = [PS(p_ctx, "pKT0", [128, TCA]), PS(p_ctx, "pKT1", [128, TCA])]
    pV = [PS(p_ctx, "pV0", [128, KVD]), PS(p_ctx, "pV1", [128, KVD])]
    pST = PS(p_ctx, "pST", [1, 1024])
    pBC = [PS(p_ctx, "pBC0", [128, 2 * TCA]), PS(p_ctx, "pBC1", [128, 2 * TCA])]

    chunk_order = list(range(OWNA, NCHA)) + list(range(OWNA))
    cD_x = {}
    cD_xnw = {}
    tSQ = {}
    tST = {-1: 0}
    vST = {-1: vInit}
    vXN = {-1: vInit, -2: vInit}
    tKV = {-1: 0, -2: 0}
    sKTe = {}
    sVe = {}

    for jj, j in enumerate(chunk_order):
        c0 = j * TCA
        own = j < OWNA
        p = jj % 2

        def a1_sync(sy, jj=jj, c0=c0, p=p):
            if jj == 0:
                W(sy, cD_const)
            sy.wait_ge(sV.h, vXN[jj - 2])
            cD_x[jj] = (sX[p], sX[p].dma(sy.dma_start(
                xbuf[p][:].rearrange("p (a s) -> p a s", s=TCA), dv(xT, c0, TCA))))
        on("sync", a1_sync)

        tBase = sT.n
        sSQ = {}

        def a1_sq(v, jj=jj, p=p, tBase=tBase, sSQ=sSQ):
            for dd in range(DD):
                if dd < 2:
                    if (jj - 1, DD - 2 + dd) in tSQ:
                        v.wait_ge(sT.h, tSQ[(jj - 1, DD - 2 + dd)])
                else:
                    v.wait_ge(sT.h, tBase + (dd - 2) + 1)
                if dd == 0:
                    W(v, cD_x[jj])
                sl = xbuf[p][:, dd * TCA:(dd + 1) * TCA]
                sSQ[dd] = sV.inc(v.tensor_mul(xsq[dd % 2][:], sl, sl))
        on("vector", a1_sq)

        def a1_stats_mm(t, jj=jj, p=p, tBase=tBase, sSQ=sSQ):
            for dd in range(DD):
                if dd == 0:
                    t.wait_ge(sV.h, vST[jj - 1])
                    W(t, cD_x[jj])
                t.matmul(pST[0:1, 0:TCA], onesc_f[:],
                         xbuf[p][:, dd * TCA:(dd + 1) * TCA],
                         start=(dd == 0), stop=(dd == DD - 1))
                t.wait_ge(sV.h, sSQ[dd])
                ins = t.matmul(pST[0:1, 512:512 + TCA], onesc_b[:], xsq[dd % 2][:],
                               start=(dd == 0), stop=(dd == DD - 1))
                tSQ[(jj, dd)] = sT.inc(ins)
                assert tSQ[(jj, dd)] == tBase + dd + 1
            tST[jj] = sT.n
        on("tensor", a1_stats_mm)

        def a1_stats_v(v, jj=jj):
            v.wait_ge(sT.h, tST[jj])
            v.tensor_scalar(m_row[0:1, 0:TCA], pST[0:1, 0:TCA], INV_D, None, ALU.mult)
            v.tensor_scalar(var_row[0:1, 0:TCA], pST[0:1, 512:512 + TCA], INV_D,
                            EPS, ALU.mult, ALU.add)
            v.tensor_mul(ln_row[0:1, 0:TCA], m_row[0:1, 0:TCA], m_row[0:1, 0:TCA])
            vST[jj] = sV.inc(v.tensor_sub(var_row[0:1, 0:TCA], var_row[0:1, 0:TCA],
                                          ln_row[0:1, 0:TCA]))
        on("vector", a1_stats_v)

        def a1_rstd(s, jj=jj):
            s.wait_ge(sV.h, vST[jj])
            s.activation(ln_row[0:1, 0:TCA], var_row[0:1, 0:TCA], AF.Ln)
            sS.inc(s.activation(r_row[0:1, 0:TCA], ln_row[0:1, 0:TCA], AF.Exp,
                                scale=-0.5))
        on("scalar", a1_rstd)
        sR1 = sS.n

        def a1_c2(v, sR1=sR1):
            v.wait_ge(sS.h, sR1)
            sV.inc(v.scalar_tensor_tensor(c2_row[0:1, 0:TCA], m_row[0:1, 0:TCA],
                                          -1.0, r_row[0:1, 0:TCA], ALU.mult, ALU.mult))
        on("vector", a1_c2)
        vC2 = sV.n

        def a1_bc(t, jj=jj, p=p, vC2=vC2):
            t.wait_ge(sV.h, vC2)
            t.wait_ge(sV.h, vXN[jj - 2])
            t.matmul(pBC[p][:, 0:TCA], onesr_f[0:1, 0:128], r_row[0:1, 0:TCA],
                     start=True, stop=True)
            sT.inc(t.matmul(pBC[p][:, TCA:2 * TCA], onesr_f[0:1, 0:128],
                            c2_row[0:1, 0:TCA], start=True, stop=True))
        on("tensor", a1_bc)
        tBC1 = sT.n

        def a1_xn(v, jj=jj, p=p, tBC1=tBC1):
            v.wait_ge(sT.h, tBC1)
            v.wait_ge(sT.h, tKV[jj - 2])
            if (jj - 2) in cD_xnw:
                W(v, cD_xnw[jj - 2])
            for dd in range(DD):
                v.tensor_mul(tmpx[:], xbuf[p][:, dd * TCA:(dd + 1) * TCA],
                             pBC[p][:, 0:TCA])
                ins = v.tensor_add(xnbuf[p][:, dd * TCA:(dd + 1) * TCA], tmpx[:],
                                   pBC[p][:, TCA:2 * TCA])
            vXN[jj] = sV.inc(ins)
        on("vector", a1_xn)

        if own:
            def a1_xnw(sy, jj=jj, c0=c0, p=p):
                sy.wait_ge(sV.h, vXN[jj])
                sem = sXNW[jj % 2]
                cD_xnw[jj] = (sem, sem.dma(sy.dma_start(
                    dv(xn_s, c0, TCA),
                    xnbuf[p][:].rearrange("p (a s) -> p a s", s=TCA))))
            on("sync", a1_xnw)

        for m in range(NM):
            def a1_k(t, jj=jj, c0=c0, p=p, m=m):
                if m == 0:
                    t.wait_ge(sV.h, vXN[jj])
                    if jj == 0:
                        W(t, cGD_wkv)
                if (jj, m - 2) in sKTe:
                    t.wait_ge(sV.h, sKTe[(jj, m - 2)])
                elif (jj - 1, NM - 2 + m) in sKTe:
                    t.wait_ge(sV.h, sKTe[(jj - 1, NM - 2 + m)])
                for dd in range(DD):
                    t.matmul(pKT[m % 2][:],
                             wkb[:, dd * KVD + m * 128:dd * KVD + (m + 1) * 128],
                             xnbuf[p][:, dd * TCA:(dd + 1) * TCA],
                             start=(dd == 0), stop=False)
                ins = t.matmul(pKT[m % 2][:], bk_b[0:1, m * 128:(m + 1) * 128],
                               onesr_b[0:1, 0:TCA], start=False, stop=True)
                sKTe[(jj, m, "mm")] = sT.inc(ins)
            on("tensor", a1_k)

            def a1_ke(v, jj=jj, c0=c0, m=m):
                v.wait_ge(sT.h, sKTe[(jj, m, "mm")])
                sKTe[(jj, m)] = sV.inc(v.tensor_copy(
                    KTs[:, m * S + c0:m * S + c0 + TCA], pKT[m % 2][:]))
            on("vector", a1_ke)

        for tt in range(NTT):
            def a1_v(t, jj=jj, c0=c0, p=p, tt=tt):
                if (jj, tt - 2) in sVe:
                    t.wait_ge(sV.h, sVe[(jj, tt - 2)])
                elif (jj - 1, NTT - 2 + tt) in sVe:
                    t.wait_ge(sV.h, sVe[(jj - 1, NTT - 2 + tt)])
                for dd in range(DD):
                    t.matmul(pV[tt % 2][:],
                             xnbuf[p][:, dd * TCA + tt * 128:dd * TCA + (tt + 1) * 128],
                             wvb[:, dd * KVD:(dd + 1) * KVD],
                             start=(dd == 0), stop=False)
                ins = t.matmul(pV[tt % 2][:], onesr_b[0:1, 0:128], bv_b[0:1, 0:KVD],
                               start=False, stop=True)
                sVe[(jj, tt, "mm")] = sT.inc(ins)
            on("tensor", a1_v)

            def a1_ve(v, jj=jj, c0=c0, tt=tt):
                v.wait_ge(sT.h, sVe[(jj, tt, "mm")])
                ktg = c0 // 128 + tt
                sVe[(jj, tt)] = sV.inc(v.tensor_copy(
                    Vs[:, ktg * KVD:(ktg + 1) * KVD], pV[tt % 2][:]))
            on("vector", a1_ve)
        tKV[jj] = sT.n
        spass_emit(*spA1[jj])

    a1_endT, a1_endS, a1_endV = sT.n, sS.n, sV.n
    a1_xnw_fin = [(s_, s_.n) for s_ in sXNW]
    p_ctx.close()
    a1_ctx.close()

    a2_ctx = ExitStack()
    bq_b = SB(a2_ctx, "bq_b", [1, D], BF16)
    wqb = SB(a2_ctx, "wqb", [128, DD * D], BF16)
    xn_in = SB(a2_ctx, "xn_in", [128, DD * TC], BF16)
    qtev = SB(a2_ctx, "qtev", [128, TC], BF16)
    qsq = SB(a2_ctx, "qsq", [128, TC], BF16)

    p_ctx = ExitStack()
    pQT = [PS(p_ctx, "pQT0", [128, TC]), PS(p_ctx, "pQT1", [128, TC])]
    pQN = PS(p_ctx, "pQN", [1, 512])

    def _gp_wq(g):
        g.wait_ge(sT.h, a1_endT)
        g.wait_ge(sS.h, a1_endS)
        g.wait_ge(sV.h, a1_endV)
        sGC.dma(g.dma_start(bq_b[:], bqd))
        sGC.dma(g.dma_start(wqb[:].rearrange("p (a s) -> p a s", s=D),
                            wq.rearrange("(a p) s -> p a s", p=128)))
        sGC.dma(g.dma_start(wo_s, wo))
    on("gpsimd", _gp_wq)
    cGD_wq = (sGC, sGC.n)

    sKq = {}
    tKn = {}
    vKn = {-1: a1_endV}
    KCW = min(512, S, TC)
    nkc = S // KCW
    for kvh in range(HK):
        for ch in range(nkc):
            i = kvh * nkc + ch

            def kn_s(v, kvh=kvh, ch=ch, i=i):
                if (i - 1) in tKn:
                    v.wait_ge(sT.h, tKn[i - 1])
                if i == 0:
                    v.wait_ge(sV.h, a1_endV)
                sl = KTs[:, kvh * S + ch * KCW:kvh * S + (ch + 1) * KCW]
                sKq[i] = sV.inc(v.tensor_mul(qsq[:, 0:KCW], sl, sl))
            on("vector", kn_s)

            def kn_t(t, i=i):
                t.wait_ge(sV.h, sKq[i])
                t.wait_ge(sV.h, vKn[i - 1])
                if i == 0:
                    t.wait_ge(sV.h, a1_endV)
                tKn[i] = sT.inc(t.matmul(pQN[0:1, 0:KCW], onesc_b[:],
                                         qsq[:, 0:KCW], start=True, stop=True))
            on("tensor", kn_t)

            def kn_v(v, i=i):
                v.wait_ge(sT.h, tKn[i])
                vKn[i] = sV.inc(v.reduce_max(kn_all[0:1, i:i + 1],
                                             pQN[0:1, 0:KCW], AX.X))
            on("vector", kn_v)
    vKn_end = sV.n

    cD_xn2 = {}
    cD_qtw = {}
    tQ = {}
    sQe = {}
    tQn = {}
    vQn = {-1: vKn_end}

    for tc in range(NTC):
        def a2_sync(sy, tc=tc):
            if tc == 0:
                sy.wait_ge(sT.h, a1_endT)
                sy.wait_ge(sS.h, a1_endS)
                sy.wait_ge(sV.h, a1_endV)
                for sv in a1_xnw_fin:
                    W(sy, sv)
            else:
                sy.wait_ge(sT.h, tQ[(tc - 1, H - 1)])
            cD_xn2[tc] = (sXN2, sXN2.dma(sy.dma_start(
                xn_in[:].rearrange("p (a s) -> p a s", s=TC), dv(xn_s, tc * TC, TC))))
        on("sync", a2_sync)

        for m in range(H):
            i = tc * H + m

            def a2_q(t, tc=tc, m=m, i=i):
                if m == 0:
                    W(t, cD_xn2[tc])
                if i == 0:
                    W(t, cGD_wq)
                    t.wait_ge(sS.h, a1_endS)
                    t.wait_ge(sV.h, a1_endV)
                if (i - 2) in sQe:
                    t.wait_ge(sV.h, sQe[i - 2])
                for dd in range(DD):
                    t.matmul(pQT[i % 2][:],
                             wqb[:, dd * D + m * 128:dd * D + (m + 1) * 128],
                             xn_in[:, dd * TC:(dd + 1) * TC],
                             start=(dd == 0), stop=False)
                ins = t.matmul(pQT[i % 2][:], bq_b[0:1, m * 128:(m + 1) * 128],
                               onesr_b[0:1, 0:TC], start=False, stop=True)
                tQ[(tc, m)] = sT.inc(ins)
            on("tensor", a2_q)

            def a2_ev(v, tc=tc, m=m, i=i):
                v.wait_ge(sT.h, tQ[(tc, m)])
                if (i - 1) in cD_qtw:
                    W(v, cD_qtw[i - 1])
                if (i - 1) in tQn:
                    v.wait_ge(sT.h, tQn[i - 1])
                v.tensor_copy(qtev[:], pQT[i % 2][:])
                sQe[i] = sV.inc(v.tensor_mul(qsq[:], qtev[:], qtev[:]))
            on("vector", a2_ev)

            def a2_qn(t, i=i):
                t.wait_ge(sV.h, sQe[i])
                t.wait_ge(sV.h, vQn[i - 1])
                tQn[i] = sT.inc(t.matmul(pQN[0:1, 0:TC], onesc_b[:], qsq[:],
                                         start=True, stop=True))
            on("tensor", a2_qn)

            def a2_qnv(v, i=i):
                v.wait_ge(sT.h, tQn[i])
                vQn[i] = sV.inc(v.reduce_max(qn_all[0:1, i:i + 1],
                                             pQN[0:1, 0:TC], AX.X))
            on("vector", a2_qnv)

            def a2_qw(sy, tc=tc, m=m, i=i):
                sy.wait_ge(sV.h, sQe[i])
                sem = sQW[i % 2]
                cD_qtw[i] = (sem, sem.dma(sy.dma_start(
                    qt_s[m * 128:(m + 1) * 128, tc * TC:(tc + 1) * TC], qtev[:])))
            on("sync", a2_qw)
        if m == H - 1:
            spass_emit(*spA2[tc])

    qtw_fin = [(s_, s_.n) for s_ in sQW]
    vQn_end = sV.n

    NQN = NTC * H
    NKN = HK * nkc

    def a2_nb_v(v):
        v.reduce_max(r11a[:], qn_all[0:1, 0:NQN], AX.X)
        v.reduce_max(r11b[:], kn_all[0:1, 0:NKN], AX.X)
        v.tensor_copy(ln_row[0:1, 0:TCA], var_row[0:1, 0:TCA])
        sV.inc(v.tensor_mul(r11a[:], r11a[:], r11b[:]))
    on("vector", a2_nb_v)
    vQK = sV.n

    def a2_nb_s(s):
        s.wait_ge(sV.h, vQK)
        s.activation(r11b[:], r11a[:], AF.Ln)
        s.activation(ln_row[0:1, 0:TCA], var_row[0:1, 0:TCA], AF.Copy)
        sS.inc(s.activation(r11a[:], r11b[:], AF.Exp, scale=0.5))
    on("scalar", a2_nb_s)
    sQK = sS.n

    def a2_nb_v2(v):
        v.wait_ge(sS.h, sQK)
        sV.inc(v.tensor_scalar(r11b[:], r11a[:], -SCL, -2.0, ALU.mult, ALU.add))
    on("vector", a2_nb_v2)
    vNB = sV.n

    def a2_nb_t(t):
        t.wait_ge(sV.h, vNB)
        t.wait_ge(sV.h, sQe[NTC * H - 1])
        sT.inc(t.matmul(pQT[1][:, 0:1], onesr_f[0:1, 0:128], r11b[0:1, 0:1],
                        start=True, stop=True))
    on("tensor", a2_nb_t)
    tNB = sT.n

    def a2_nb_c(s):
        s.wait_ge(sT.h, tNB)
        sS.inc(s.copy(negb[:], pQT[1][:, 0:1]))
    on("scalar", a2_nb_c)

    a2_endT, a2_endS, a2_endV = sT.n, sS.n, sV.n
    p_ctx.close()
    a2_ctx.close()

    oall_ctx = ExitStack()
    Oall = SB(oall_ctx, "Oall", [128, H * TOWN], BF16, side="right")

    b_ctx = ExitStack()
    qtb = [SB(b_ctx, f"qtb{i}", [128, QC], BF16) for i in range(2)]
    es = [SB(b_ctx, f"es{i}", [128, QC], BF16) for i in range(ESR)]
    zbs = SB(b_ctx, "zbs", [128, QC], F32)
    zrow = SB(b_ctx, "zrow", [1, QC], F32)
    zlog = SB(b_ctx, "zlog", [1, QC], F32)
    gneg = [SB(b_ctx, f"gneg{i}", [128, 1024], BF16) for i in range(2)]
    epair = [SB(b_ctx, f"epair{i}", [128, QC], BF16) for i in range(2)]
    equad = [SB(b_ctx, f"equad{i}", [128, QC], BF16) for i in range(2)]
    wqo = [SB(b_ctx, f"wqo{i}", [128, 1024], BF16) for i in range(2)]

    p_ctx = ExitStack()
    SP = [PS(p_ctx, "SP0", [128, QC]), PS(p_ctx, "SP1", [128, QC])]
    OAC = PS(p_ctx, "OAC", [128, QC])
    pZB = PS(p_ctx, "pZB", [128, QC])

    wt = {i: [] for i in range(NID)}
    niters = NID - W_START
    for k in range(NT):
        wt[W_START + (k * niters) // NT].append(k)

    tS_r = {}
    tAVZ = {}
    sEXP = {}
    for kt in range(NKT):
        tAVZ[(-1, kt)] = a2_endT
        sEXP[(-1, kt)] = a2_endS
    tZB = {}
    pred_tZB = {}
    tZp = {}
    for pk in range(max(1, NKT // 4)):
        tZp[(-1, pk)] = a2_endT
    sZBS = {-1: a2_endS}
    sREC = {}
    vMul = {-1: a2_endV}
    cQT = {}
    cWF = {}
    vQt = {}
    vThr = a2_endV

    def qt_src(i):
        qc, h = divmod(i, H)
        return qt_s[h * 128:(h + 1) * 128, qc * QC:(qc + 1) * QC]

    def _emit_zb(t, pidx, rec_val):
        t.wait_ge(sS.h, rec_val)
        for ns in range(NSC):
            n0, n1 = ns * 512, min(QC, ns * 512 + 512)
            ins = t.matmul(pZB[:, n0:n1], onesr_f[0:1, 0:128],
                           zrow[0:1, n0:n1], start=True, stop=True)
        tZB[pidx] = sT.inc(ins)
        if pidx in pred_tZB:
            assert tZB[pidx] == pred_tZB[pidx], (pidx, tZB[pidx], pred_tZB[pidx])

    def b_sync0(sy):
        for sv in qtw_fin:
            W(sy, sv)
        cQT[0] = (sQR[0], sQR[0].dma(sy.dma_start(qtb[0][:], qt_src(0))))
    on("sync", b_sync0)

    for idx in range(NID):
        qc, h = divmod(idx, H)
        kvh = h // G

        if idx == FIN_IDX:
            def fin_zb(t, idx=idx):
                _emit_zb(t, idx - 1, sREC[idx - 1])
            on("tensor", fin_zb)

            def fin_v0(v):
                v.wait_ge(sS.h, sAbs[NT - 1])
                for mat in range(3):
                    lo, hi = MATR[mat]
                    ins = v.tensor_reduce(sacc[:, mat:mat + 1],
                                          rsall[:, lo:hi + 1], AX.X, ALU.add)
                sV.inc(ins)
            on("vector", fin_v0)
            vRed3 = sV.n

            def fin_t1(t, idx=idx, vRed3=vRed3):
                t.wait_ge(sV.h, vRed3)
                t.wait_ge(sS.h, sREC[idx - 1])
                t.wait_ge(sS.h, sZBS[idx - 1])
                for mat in range(3):
                    ins = t.matmul(pZB[0:1, mat:mat + 1], onesc_f[:],
                                   sacc[:, mat:mat + 1], start=True, stop=True)
                sT.inc(ins)
            on("tensor", fin_t1)
            tFin1 = sT.n

            def fin_v1(v, tFin1=tFin1):
                v.wait_ge(sT.h, tFin1)
                v.tensor_copy(sacc_dbg[:], sacc[:])
                v.tensor_scalar(sthr_row[0:1, 0:3], pZB[0:1, 0:3],
                                0.5 / cfg["NW"], None, ALU.mult)
                sV.inc(v.tensor_scalar(sthr_row[0:1, 3:6], pZB[0:1, 0:3],
                                       1.0 / cfg["NW"], None, ALU.mult))
            on("vector", fin_v1)
            vFin1 = sV.n

            def fin_t2(t, idx=idx, vFin1=vFin1):
                t.wait_ge(sV.h, vFin1)
                t.wait_ge(sS.h, sZBS[idx - 1])
                sT.inc(t.matmul(SP[0][:, 0:6], onesr_f[0:1, 0:128],
                                sthr_row[0:1, 0:6], start=True, stop=True))
            on("tensor", fin_t2)
            tFin2 = sT.n

            def fin_v2(v, tFin2=tFin2):
                v.wait_ge(sT.h, tFin2)
                v.tensor_copy(thrP[:], SP[0][:, 0:3])
                v.tensor_copy(sF[:], SP[0][:, 3:6])
                sV.inc(v.tensor_scalar(thrN[:], SP[0][:, 0:3], -1.0, None, ALU.mult))
            on("vector", fin_v2)
            vThr = sV.n

        vBase = sV.n
        wlist = wt[idx]

        NPR_ = NKT // 2
        NQD_ = NKT // 4
        qq = []
        for t in wlist:
            qq.append((t, "lt"))
            qq.append((t, "stt"))
        sched = []
        qi = 0
        for pk in range(NPR_):
            sched.append(("pair", pk))
            if pk % 2 == 1:
                sched.append(("quad", pk // 2))
            want = ((pk + 1) * len(qq)) // NPR_
            while qi < want:
                sched.append(("q",) + qq[qi])
                qi += 1
        sched.append(("mult",))
        while qi < len(qq):
            sched.append(("q",) + qq[qi])
            qi += 1
        pos = {}
        _n = vBase
        for op in sched:
            if op[0] in ("pair", "quad"):
                _n += 1
                pos[(op[0], op[1])] = _n
            elif op[0] == "mult":
                _n += 1
                pos[("mult",)] = _n
            elif op[2] == "stt":
                _n += 1
                pos[("stt", op[1])] = _n

        def vq_pred(t, pos=pos):
            return pos[("stt", t)]

        def b_sync(sy, idx=idx, vBase=vBase, wlist=wlist):
            if idx + 1 < NID:
                if idx >= 1:
                    sy.wait_ge(sT.h, tS_r[(idx - 1, NKT - 1)])
                sem = sQR[(idx + 1) % 2]
                cQT[idx + 1] = (sem, sem.dma(sy.dma_start(qtb[(idx + 1) % 2][:],
                                                          qt_src(idx + 1))))
            for t in wlist:
                if t < 2:
                    sy.wait_ge(sS.h, sAbs[NT - 1])
                elif (t - 2) in vQt:
                    sy.wait_ge(sV.h, vQt[t - 2])
                else:
                    sy.wait_ge(sV.h, vq_pred(t - 2))
                mat, r0, c0, ncols = TILES[t]
                sem = sWF[t % 2]
                cWF[t] = (sem, sem.dma(sy.dma_start(
                    wfp[t % 2][:, 0:ncols], WSRC[mat][r0:r0 + 128, c0:c0 + ncols])))
        on("sync", b_sync)

        sBase = sS.n

        def b_tensor(t, idx=idx, qc=qc, h=h, kvh=kvh, sBase=sBase, vBase=vBase):
            NPR = NKT // 2

            def pexp(kt):
                return sBase + kt + 1

            def vpair(pk):
                return pos[("pair", pk)]

            def s_mm(kt, first=False):
                if first:
                    W(t, cQT[idx])
                    t.wait_ge(sS.h, sEXP[(idx - 1, NKT - 2)])
                    if idx == FIN_IDX:
                        t.wait_ge(sV.h, vThr)
                elif kt == 1:
                    t.wait_ge(sS.h, sEXP[(idx - 1, NKT - 1)])
                else:
                    t.wait_ge(sS.h, pexp(kt - 2))
                for ns in range(NSC):
                    n0, n1 = ns * 512, min(QC, ns * 512 + 512)
                    ins = t.matmul(SP[kt % 2][:, n0:n1],
                                   KTs[:, kvh * S + kt * 128:kvh * S + (kt + 1) * 128],
                                   qtb[idx % 2][:, n0:n1], start=True, stop=True)
                tS_r[(idx, kt)] = sT.inc(ins)

            def av_mm(kt):
                t.wait_ge(sS.h, pexp(kt))
                if kt == 0:
                    t.wait_ge(sV.h, vMul[idx - 1])
                    if idx == FIN_IDX:
                        t.wait_ge(sV.h, vThr)
                for ns in range(NSC):
                    n0, n1 = ns * 512, min(QC, ns * 512 + 512)
                    ins = t.matmul(OAC[:, n0:n1],
                                   Vs[:, kt * KVD + kvh * DH:kt * KVD + (kvh + 1) * DH],
                                   es[kt % ESR][:, n0:n1],
                                   start=(kt == 0), stop=(kt == NKT - 1))
                tAVZ[(idx, kt)] = sT.inc(ins)

            def z_mm(qd):
                t.wait_ge(sV.h, pos[("quad", qd)])
                if qd == 0:
                    t.wait_ge(sS.h, sZBS[idx - 1])
                    if idx == FIN_IDX:
                        t.wait_ge(sV.h, vThr)
                for ns in range(NSC):
                    n0, n1 = ns * 512, min(QC, ns * 512 + 512)
                    ins = t.matmul(pZB[0:1, n0:n1], onesc_b[:], equad[qd % 2][:, n0:n1],
                                   start=(qd == 0), stop=(qd == NKT // 4 - 1))
                tZp[(idx, qd)] = sT.inc(ins)

            s_mm(0, first=True)
            if NKT > 1:
                s_mm(1)
            if idx > 0 and idx != FIN_IDX:
                _emit_zb(t, idx - 1, sREC[idx - 1])
            for kt in range(1, NKT):
                if kt + 1 < NKT:
                    s_mm(kt + 1)
                av_mm(kt - 1)
                if kt % 4 == 0:
                    z_mm(kt // 4 - 1)
            av_mm(NKT - 1)
            z_mm(NKT // 4 - 1)
            if idx == NID - 1:
                _emit_zb(t, idx, sBase + NKT + 1)
        on("tensor", b_tensor)
        if idx + 1 < NID:
            pred_tZB[idx] = sT.n + (1 if idx + 1 == FIN_IDX else 3)

        def b_scalar(s, idx=idx, sBase=sBase, vBase=vBase):
            NPR = NKT // 2
            for kt in range(NKT):
                s.wait_ge(sT.h, tS_r[(idx, kt)])
                if kt >= ESR:
                    s.wait_ge(sT.h, tAVZ[(idx, kt - ESR)])
                    s.wait_ge(sV.h, vBase + (kt - ESR) // 2 + 1)
                else:
                    s.wait_ge(sT.h, tAVZ[(idx - 1, NKT - ESR + kt)])
                    s.wait_ge(sT.h, tZp[(idx - 1, (NKT - ESR + kt) // 4)])
                c = sS.inc(s.activation(es[kt % ESR][:], SP[kt % 2][:], AF.Exp,
                                        bias=negb[:], scale=SCL))
                sEXP[(idx, kt)] = c
                assert c == sBase + kt + 1
            s.wait_ge(sT.h, tZp[(idx, NKT // 4 - 1)])
            s.activation(zlog[:], pZB[0:1, :], AF.Ln)
            sREC[idx] = sS.inc(s.activation(zrow[:], zlog[:], AF.Exp, scale=-1.0))
            s.wait_ge(sT.h, tZB[idx] if idx in tZB else pred_tZB[idx])
            sZBS[idx] = sS.inc(s.copy(zbs[:], pZB[:]))
            assert sZBS[idx] == sBase + NKT + 2
        on("scalar", b_scalar)

        def b_vector(v, idx=idx, qc=qc, h=h, vBase=vBase, sBase=sBase,
                     sched=sched, pos=pos):
            NPR = NKT // 2
            for op in sched:
                if op[0] == "pair":
                    pk = op[1]
                    v.wait_ge(sS.h, sBase + 2 * pk + 2)
                    c = sV.inc(v.tensor_add(epair[pk % 2][:], es[(2 * pk) % ESR][:],
                                            es[(2 * pk + 1) % ESR][:]))
                    assert c == pos[("pair", pk)]
                elif op[0] == "quad":
                    qd = op[1]
                    if (idx, qd - 2) in tZp:
                        v.wait_ge(sT.h, tZp[(idx, qd - 2)])
                    elif (idx - 1, NKT // 4 - 2 + qd) in tZp:
                        v.wait_ge(sT.h, tZp[(idx - 1, NKT // 4 - 2 + qd)])
                    c = sV.inc(v.tensor_add(equad[qd % 2][:], epair[0][:],
                                            epair[1][:]))
                    assert c == pos[("quad", qd)]
                elif op[0] == "mult":
                    v.wait_ge(sS.h, sZBS[idx])
                    v.wait_ge(sT.h, tAVZ[(idx, NKT - 1)])
                    vMul[idx] = sV.inc(v.tensor_mul(
                        Oall[:, h * TOWN + qc * QC:h * TOWN + (qc + 1) * QC],
                        OAC[:], zbs[:]))
                    assert vMul[idx] == pos[("mult",)]
                else:
                    t, kind = op[1], op[2]
                    mat, r0, c0, ncols = TILES[t]
                    if kind == "lt":
                        W(v, cWF[t])
                        v.tensor_scalar(gneg[t % 2][:, 0:ncols],
                                        wfp[t % 2][:, 0:ncols],
                                        thrN[:, mat:mat + 1], None, ALU.is_lt)
                    else:
                        if t >= 2:
                            v.wait_ge(sWBG[t % 2].h, 16 * ((t - 2) // 2 + 1))
                        c = sV.inc(v.scalar_tensor_tensor(
                            wqo[t % 2][:, 0:ncols], wfp[t % 2][:, 0:ncols],
                            thrP[:, mat:mat + 1], gneg[t % 2][:, 0:ncols],
                            ALU.is_gt, ALU.subtract))
                        vQt[t] = c
                        assert c == pos[("stt", t)]
        on("vector", b_vector)

        def b_wbg(g, wlist=wlist):
            for t in wlist:
                g.wait_ge(sV.h, vQt[t])
                mat, r0, c0, ncols = TILES[t]
                c = sWBG[t % 2].dma(g.dma_start(
                    WDST[mat][r0:r0 + 128, c0:c0 + ncols], wqo[t % 2][:, 0:ncols]))
                assert c == 16 * (t // 2 + 1)
        if wlist:
            on("gpsimd", b_wbg)

    wb_fin = [(s_, s_.n) for s_ in sWBG]
    assert sWBG[0].n + sWBG[1].n == 16 * NT
    b_endT, b_endS, b_endV = sT.n, sS.n, sV.n
    p_ctx.close()
    b_ctx.close()
    ktv_ctx.close()

    c_ctx = ExitStack()
    bo_b = SB(c_ctx, "bo_b", [1, D], BF16)
    wob = [SB(c_ctx, f"wob{i}", [128, D], BF16) for i in range(2)]
    ao = [SB(c_ctx, f"ao{i}", [128, DD * TC], F32) for i in range(2)]
    xo = [SB(c_ctx, f"xo{i}", [128, TC], F32) for i in range(4)]
    aosq = [SB(c_ctx, f"aosq{i}", [128, TC], BF16) for i in range(4)]
    ysq = [SB(c_ctx, f"ysq{i}", [128, TC], BF16) for i in range(4)]
    ynt = SB(c_ctx, "ynt", [128, TC], BF16)

    def _gp_bo(g):
        g.wait_ge(sT.h, b_endT)
        g.wait_ge(sV.h, b_endV)
        g.wait_ge(sS.h, b_endS)
        sGC.dma(g.dma_start(bo_b[:], bod))
    on("gpsimd", _gp_bo)
    cGD_wo = (sGC, sGC.n)

    p_ctx = ExitStack()
    pAO = [PS(p_ctx, "pAO0", [128, TC]), PS(p_ctx, "pAO1", [128, TC])]
    pSTa = PS(p_ctx, "pSTa", [1, 1024])
    pSTb = PS(p_ctx, "pSTb", [1, 1024])
    pBCr = PS(p_ctx, "pBCr", [128, TC])
    pBCc = PS(p_ctx, "pBCc", [128, TC])

    cD_xo = {}
    cD_wo = {}
    cD_y = {}
    cD_yn = {}
    tAO = {}
    sAOe = {}
    tSts = {}
    vRes = {}
    vYn = {}
    vSt2 = {}
    vSt3 = {-1: b_endV}

    for tc2 in range(NTC):
        def c_x0(s, tc2=tc2):
            if tc2 == 0:
                s.wait_ge(sT.h, b_endT)
                s.wait_ge(sV.h, b_endV)
            else:
                s.wait_ge(sV.h, vRes[(tc2 - 1, DD - 1)])
            for dd in range(min(4, DD)):
                sem = sXO[dd % 4]
                cD_xo[(tc2, dd)] = (sem, sem.dma(s.dma_start(
                    xo[dd % 4][:],
                    xT[dd * 128:(dd + 1) * 128, tc2 * TC:(tc2 + 1) * TC])))
        on("scalar", c_x0)

        for dd in range(DD):
            def c_wol(sy, tc2=tc2, dd=dd):
                if tc2 == 0 and dd < 2:
                    sy.wait_ge(sT.h, b_endT)
                    sy.wait_ge(sS.h, b_endS)
                    sy.wait_ge(sV.h, b_endV)
                if (tc2, dd - 2) in tAO:
                    sy.wait_ge(sT.h, tAO[(tc2, dd - 2)])
                elif tc2 > 0 and (tc2 - 1, DD - 2 + dd) in tAO:
                    sy.wait_ge(sT.h, tAO[(tc2 - 1, DD - 2 + dd)])
                sem = sWO[dd % 2]
                cD_wo[(tc2, dd)] = (sem, sem.dma(sy.dma_start(
                    wob[dd % 2][:], wo_s[dd * 128:(dd + 1) * 128, :])))
            on("sync", c_wol)

            def c_ao(t, tc2=tc2, dd=dd):
                if tc2 == 0 and dd == 0:
                    W(t, cGD_wo)
                    t.wait_ge(sV.h, b_endV)
                    t.wait_ge(sS.h, b_endS)
                W(t, cD_wo[(tc2, dd)])
                if (tc2, dd - 2) in sAOe:
                    t.wait_ge(sS.h, sAOe[(tc2, dd - 2)])
                elif tc2 > 0 and (tc2 - 1, DD - 2 + dd) in sAOe:
                    t.wait_ge(sS.h, sAOe[(tc2 - 1, DD - 2 + dd)])
                for hh in range(H):
                    t.matmul(pAO[dd % 2][:],
                             wob[dd % 2][:, hh * 128:(hh + 1) * 128],
                             Oall[:, hh * TOWN + tc2 * TC:hh * TOWN + (tc2 + 1) * TC],
                             start=(hh == 0), stop=False)
                ins = t.matmul(pAO[dd % 2][:], bo_b[0:1, dd * 128:(dd + 1) * 128],
                               onesr_b[0:1, 0:TC], start=False, stop=True)
                tAO[(tc2, dd)] = sT.inc(ins)
            on("tensor", c_ao)

            def c_ev(s, tc2=tc2, dd=dd):
                s.wait_ge(sT.h, tAO[(tc2, dd)])
                if tc2 > 1:
                    s.wait_ge(sV.h, vYn[(tc2 - 2, dd)])
                    W(s, cD_y[(tc2 - 2, dd)])
                if (tc2, dd - 3) in tSts:
                    s.wait_ge(sT.h, tSts[(tc2, dd - 3)])
                sl = ao[tc2 % 2][:, dd * TC:(dd + 1) * TC]
                s.activation(sl, pAO[dd % 2][:], AF.Copy)
                sAOe[(tc2, dd)] = sS.inc(
                    s.activation(aosq[dd % 4][:], pAO[dd % 2][:], AF.Square))
            on("scalar", c_ev)

            def c_st(t, tc2=tc2, dd=dd):
                t.wait_ge(sS.h, sAOe[(tc2, dd)])
                if dd == 0 and tc2 > 1:
                    t.wait_ge(sV.h, vSt2[tc2 - 2, "read"])
                t.matmul(pSTa[0:1, 0:TC], onesc_f[:], ao[tc2 % 2][:, dd * TC:(dd + 1) * TC],
                         start=(dd == 0), stop=(dd == DD - 1))
                ins = t.matmul(pSTa[0:1, 512:512 + TC], onesc_b[:], aosq[dd % 4][:],
                               start=(dd == 0), stop=(dd == DD - 1))
                tSts[(tc2, dd)] = sT.inc(ins)
            on("tensor", c_st)

        def c_stats_v(v, tc2=tc2):
            v.wait_ge(sT.h, tSts[(tc2, DD - 1)])
            v.tensor_scalar(m_row[0:1, 0:TC], pSTa[0:1, 0:TC], INV_D, None, ALU.mult)
            v.tensor_scalar(var_row[0:1, 0:TC], pSTa[0:1, 512:512 + TC], INV_D,
                            EPS, ALU.mult, ALU.add)
            v.tensor_mul(ln_row[0:1, 0:TC], m_row[0:1, 0:TC], m_row[0:1, 0:TC])
            vSt2[tc2] = sV.inc(v.tensor_sub(var_row[0:1, 0:TC], var_row[0:1, 0:TC],
                                            ln_row[0:1, 0:TC]))
            vSt2[tc2, "read"] = vSt2[tc2]
        on("vector", c_stats_v)

        def c_rstd(s, tc2=tc2):
            s.wait_ge(sV.h, vSt2[tc2])
            s.activation(ln_row[0:1, 0:TC], var_row[0:1, 0:TC], AF.Ln)
            sS.inc(s.activation(r_row[0:1, 0:TC], ln_row[0:1, 0:TC], AF.Exp, scale=-0.5))
        on("scalar", c_rstd)
        sR2 = sS.n

        def c_c2(v, sR2=sR2):
            v.wait_ge(sS.h, sR2)
            sV.inc(v.scalar_tensor_tensor(c2_row[0:1, 0:TC], m_row[0:1, 0:TC], -1.0,
                                          r_row[0:1, 0:TC], ALU.mult, ALU.mult))
        on("vector", c_c2)
        vC2c = sV.n

        def c_bc(t, tc2=tc2, vC2c=vC2c):
            t.wait_ge(sV.h, vC2c)
            if tc2 > 0:
                t.wait_ge(sV.h, vYn[(tc2 - 1, DD - 1)])
            t.matmul(pBCr[:], onesr_f[0:1, 0:128], r_row[0:1, 0:TC],
                     start=True, stop=True)
            sT.inc(t.matmul(pBCc[:], onesr_f[0:1, 0:128], c2_row[0:1, 0:TC],
                            start=True, stop=True))
        on("tensor", c_bc)
        tBC2 = sT.n

        for dd in range(DD):
            def c_res(v, tc2=tc2, dd=dd, tBC2=tBC2):
                if dd == 0:
                    v.wait_ge(sT.h, tBC2)
                W(v, cD_xo[(tc2, dd)])
                sl = ao[tc2 % 2][:, dd * TC:(dd + 1) * TC]
                v.tensor_mul(sl, sl, pBCr[:])
                v.tensor_add(sl, sl, pBCc[:])
                vRes[(tc2, dd)] = sV.inc(v.tensor_add(sl, sl, xo[dd % 4][:]))
            on("vector", c_res)

            def c_yw(g, tc2=tc2, dd=dd):
                g.wait_ge(sV.h, vRes[(tc2, dd)])
                cD_y[(tc2, dd)] = (sY, sY.dma(g.dma_start(
                    y_s[dd * 128:(dd + 1) * 128, tc2 * TC:(tc2 + 1) * TC],
                    ao[tc2 % 2][:, dd * TC:(dd + 1) * TC])))
            on("gpsimd", c_yw)

            def c_xo2(s, tc2=tc2, dd=dd):
                if dd + 4 < DD:
                    s.wait_ge(sV.h, vRes[(tc2, dd)])
                    sem = sXO[(dd + 4) % 4]
                    cD_xo[(tc2, dd + 4)] = (sem, sem.dma(s.dma_start(
                        xo[(dd + 4) % 4][:],
                        xT[(dd + 4) * 128:(dd + 5) * 128, tc2 * TC:(tc2 + 1) * TC])))
            on("scalar", c_xo2)

        for dd in range(DD):
            def c_ysq(s, tc2=tc2, dd=dd):
                s.wait_ge(sV.h, vRes[(tc2, dd)])
                if (tc2, dd - 3, "y") in tSts:
                    s.wait_ge(sT.h, tSts[(tc2, dd - 3, "y")])
                sl = ao[tc2 % 2][:, dd * TC:(dd + 1) * TC]
                sAOe[(tc2, dd, "y")] = sS.inc(s.activation(ysq[dd % 4][:], sl,
                                                           AF.Square))
            on("scalar", c_ysq)

            def c_yst(t, tc2=tc2, dd=dd):
                t.wait_ge(sS.h, sAOe[(tc2, dd, "y")])
                if dd == 0 and tc2 > 1:
                    t.wait_ge(sV.h, vSt3[tc2 - 2])
                t.matmul(pSTb[0:1, 0:TC], onesc_f[:], ao[tc2 % 2][:, dd * TC:(dd + 1) * TC],
                         start=(dd == 0), stop=(dd == DD - 1))
                ins = t.matmul(pSTb[0:1, 512:512 + TC], onesc_b[:], ysq[dd % 4][:],
                               start=(dd == 0), stop=(dd == DD - 1))
                tSts[(tc2, dd, "y")] = sT.inc(ins)
            on("tensor", c_yst)

        def c_stats3(v, tc2=tc2):
            v.wait_ge(sT.h, tSts[(tc2, DD - 1, "y")])
            v.tensor_scalar(m_row[0:1, 0:TC], pSTb[0:1, 0:TC], INV_D, None, ALU.mult)
            v.tensor_scalar(var_row[0:1, 0:TC], pSTb[0:1, 512:512 + TC], INV_D,
                            EPS, ALU.mult, ALU.add)
            v.tensor_mul(ln_row[0:1, 0:TC], m_row[0:1, 0:TC], m_row[0:1, 0:TC])
            vSt3[tc2] = sV.inc(v.tensor_sub(var_row[0:1, 0:TC], var_row[0:1, 0:TC],
                                            ln_row[0:1, 0:TC]))
        on("vector", c_stats3)

        def c_rstd3(s, tc2=tc2):
            s.wait_ge(sV.h, vSt3[tc2])
            s.activation(ln_row[0:1, 0:TC], var_row[0:1, 0:TC], AF.Ln)
            sS.inc(s.activation(r_row[0:1, 0:TC], ln_row[0:1, 0:TC], AF.Exp, scale=-0.5))
        on("scalar", c_rstd3)
        sR3 = sS.n

        def c_c23(v, sR3=sR3):
            v.wait_ge(sS.h, sR3)
            sV.inc(v.scalar_tensor_tensor(c2_row[0:1, 0:TC], m_row[0:1, 0:TC], -1.0,
                                          r_row[0:1, 0:TC], ALU.mult, ALU.mult))
        on("vector", c_c23)
        vC23 = sV.n

        def c_bc3(t, tc2=tc2, vC23=vC23):
            t.wait_ge(sV.h, vC23)
            t.wait_ge(sV.h, vRes[(tc2, DD - 1)])
            t.matmul(pBCr[:], onesr_f[0:1, 0:128], r_row[0:1, 0:TC],
                     start=True, stop=True)
            sT.inc(t.matmul(pBCc[:], onesr_f[0:1, 0:128], c2_row[0:1, 0:TC],
                            start=True, stop=True))
        on("tensor", c_bc3)
        tBC3 = sT.n

        for dd in range(DD):
            def c_yn(v, tc2=tc2, dd=dd, tBC3=tBC3):
                if dd == 0:
                    v.wait_ge(sT.h, tBC3)
                if (tc2, dd - 1) in cD_yn:
                    W(v, cD_yn[(tc2, dd - 1)])
                elif tc2 > 0 and (tc2 - 1, DD - 1) in cD_yn:
                    W(v, cD_yn[(tc2 - 1, DD - 1)])
                v.tensor_mul(ynt[:], ao[tc2 % 2][:, dd * TC:(dd + 1) * TC], pBCr[:])
                vYn[(tc2, dd)] = sV.inc(v.tensor_add(ynt[:], ynt[:], pBCc[:]))
            on("vector", c_yn)

            def c_ynw(g, tc2=tc2, dd=dd):
                g.wait_ge(sV.h, vYn[(tc2, dd)])
                sem = sYNW[dd % 2]
                cD_yn[(tc2, dd)] = (sem, sem.dma(g.dma_start(
                    yn_s[dd * 128:(dd + 1) * 128, tc2 * TC:(tc2 + 1) * TC],
                    ynt[:])))
            on("gpsimd", c_ynw)

    c_endT, c_endS, c_endV = sT.n, sS.n, sV.n
    y_fin = (sY, sY.n)
    yn_fin = [(s_, s_.n) for s_ in sYNW]
    p_ctx.close()
    c_ctx.close()
    oall_ctx.close()

    d_ctx = ExitStack()
    yn_in = [SB(d_ctx, f"yn_in{i}", [128, DD * TC], BF16) for i in range(2)]
    gbuf = SB(d_ctx, "gbuf", [128, NHT * TC], BF16)
    w1b = [SB(d_ctx, f"w1b{i}", [128, D], BF16) for i in range(2)]
    w3b = [SB(d_ctx, f"w3b{i}", [128, D], BF16) for i in range(2)]
    w2b = [SB(d_ctx, f"w2b{i}", [128, HIDP], BF16) for i in range(2)]
    fbuf = SB(d_ctx, "fbuf", [128, DD * TC], F32)
    y16 = SB(d_ctx, "y16", [128, DD * TC], F32)
    sil = [SB(d_ctx, f"sil{i}", [128, TC], BF16) for i in range(2)]
    fsq = [SB(d_ctx, f"fsq{i}", [128, TC], BF16) for i in range(2)]

    p_ctx = ExitStack()
    pH1 = [PS(p_ctx, "pH1a", [128, TC]), PS(p_ctx, "pH1b", [128, TC])]
    pH3 = [PS(p_ctx, "pH3a", [128, TC]), PS(p_ctx, "pH3b", [128, TC])]
    pDO = [PS(p_ctx, "pDO0", [128, TC]), PS(p_ctx, "pDO1", [128, TC])]
    pSTd = PS(p_ctx, "pSTd", [1, 1024])

    cD_ynr = {}
    cD_w1 = {}
    cD_w2 = {}
    cD_y16 = {}
    tUP = {}
    sSil = {}
    vG = {}
    tDO = {}
    sFe = {}
    tStd = {}
    vOut = {}
    vSt4 = {}
    cOUT = {}

    for tc in range(NTC):
        if tc == 0:
            def d_sync0(sy, tc=tc):
                sy.wait_ge(sT.h, c_endT)
                sy.wait_ge(sS.h, c_endS)
                sy.wait_ge(sV.h, c_endV)
                W(sy, y_fin)
                for sv in yn_fin:
                    W(sy, sv)
                for sv in wb_fin:
                    W(sy, sv)
                cD_ynr[tc] = (sYNR, sYNR.dma(sy.dma_start(
                    yn_in[tc % 2][:].rearrange("p (a s) -> p a s", s=TC),
                    dv(yn_s, tc * TC, TC))))
                cD_y16[tc] = (sY16, sY16.dma(sy.dma_start(
                    y16[:].rearrange("p (a s) -> p a s", s=TC),
                    dv(y_s, tc * TC, TC))))
            on("sync", d_sync0)

        sSilBase = sS.n

        def psil(hc, sSilBase=sSilBase, tc=tc):
            sSil[(tc, hc)] = sSilBase + hc + 1
            return sSil[(tc, hc)]

        for hc in range(NHT):
            jprev = NHT - 1 - ((NHT - 1 - hc) % 2)

            def d_wup(sy, tc=tc, hc=hc, jprev=jprev):
                if (tc, hc - 2) in tUP:
                    sy.wait_ge(sT.h, tUP[(tc, hc - 2)])
                elif tc > 0 and (tc - 1, jprev) in tUP:
                    sy.wait_ge(sT.h, tUP[(tc - 1, jprev)])
                sem = sW1[hc % 2]
                sem.dma(sy.dma_start(w1b[hc % 2][:],
                                     w1q_s[hc * 128:(hc + 1) * 128, :]))
                cD_w1[(tc, hc)] = (sem, sem.dma(sy.dma_start(
                    w3b[hc % 2][:], w3q_s[hc * 128:(hc + 1) * 128, :])))
            on("sync", d_wup)

            def d_up(t, tc=tc, hc=hc, jprev=jprev):
                W(t, cD_w1[(tc, hc)])
                if hc == 0:
                    W(t, cD_ynr[tc])
                    if tc == 0:
                        t.wait_ge(sV.h, c_endV)
                        t.wait_ge(sS.h, c_endS)
                if hc - 2 >= 0:
                    t.wait_ge(sS.h, psil(hc - 2))
                    t.wait_ge(sV.h, vG[(tc, hc - 2)])
                elif tc > 0 and (tc - 1, jprev) in sSil:
                    t.wait_ge(sS.h, sSil[(tc - 1, jprev)])
                    t.wait_ge(sV.h, vG[(tc - 1, jprev)])
                for dd in range(DD):
                    t.matmul(pH1[hc % 2][:], w1b[hc % 2][:, dd * 128:(dd + 1) * 128],
                             yn_in[tc % 2][:, dd * TC:(dd + 1) * TC],
                             start=(dd == 0), stop=(dd == DD - 1))
                for dd in range(DD):
                    ins = t.matmul(pH3[hc % 2][:], w3b[hc % 2][:, dd * 128:(dd + 1) * 128],
                                   yn_in[tc % 2][:, dd * TC:(dd + 1) * TC],
                                   start=(dd == 0), stop=(dd == DD - 1))
                tUP[(tc, hc)] = sT.inc(ins)
            on("tensor", d_up)

            def d_g(v, tc=tc, hc=hc):
                v.wait_ge(sS.h, psil(hc))
                if tc > 0 and hc == 0:
                    v.wait_ge(sT.h, tDO[(tc - 1, DD - 1)])
                vG[(tc, hc)] = sV.inc(v.scalar_tensor_tensor(
                    gbuf[:, hc * TC:(hc + 1) * TC], pH3[hc % 2][:], sF[:, 1:2],
                    sil[hc % 2][:], ALU.mult, ALU.mult))
            on("vector", d_g)

        def d_sil_all(s, tc=tc, sSilBase=sSilBase):
            for hc in range(NHT):
                s.wait_ge(sT.h, tUP[(tc, hc)])
                if hc - 2 >= 0:
                    s.wait_ge(sV.h, vG[(tc, hc - 2)])
                c = sS.inc(s.activation(sil[hc % 2][:], pH1[hc % 2][:],
                                        AF.Silu, scale=sF[:, 0:1]))
                assert c == sSilBase + hc + 1
        on("scalar", d_sil_all)

        if tc + 1 < NTC:
            def d_ynl(sy, tc=tc):
                if (tc - 1, NHT - 1) in tUP:
                    sy.wait_ge(sT.h, tUP[(tc - 1, NHT - 1)])
                cD_ynr[tc + 1] = (sYNR, sYNR.dma(sy.dma_start(
                    yn_in[(tc + 1) % 2][:].rearrange("p (a s) -> p a s", s=TC),
                    dv(yn_s, (tc + 1) * TC, TC))))
            on("sync", d_ynl)

        if tc > 0:
            def d_y16l(sy, tc=tc):
                sy.wait_ge(sV.h, vOut[(tc - 1, DD - 1)])
                cD_y16[tc] = (sY16, sY16.dma(sy.dma_start(
                    y16[:].rearrange("p (a s) -> p a s", s=TC),
                    dv(y_s, tc * TC, TC))))
            on("sync", d_y16l)

        for dd in range(DD):
            def d_w2l(sy, tc=tc, dd=dd):
                if (tc, dd - 2) in tDO:
                    sy.wait_ge(sT.h, tDO[(tc, dd - 2)])
                elif tc > 0 and (tc - 1, DD - 2 + dd) in tDO:
                    sy.wait_ge(sT.h, tDO[(tc - 1, DD - 2 + dd)])
                sem = sW2[dd % 2]
                cD_w2[(tc, dd)] = (sem, sem.dma(sy.dma_start(
                    w2b[dd % 2][:], w2q_s[dd * 128:(dd + 1) * 128, :])))
            on("sync", d_w2l)

            def d_do(t, tc=tc, dd=dd):
                W(t, cD_w2[(tc, dd)])
                if dd == 0:
                    t.wait_ge(sV.h, vG[(tc, NHT - 1)])
                    if tc > 0:
                        t.wait_ge(sV.h, vOut[(tc - 1, DD - 1)])
                if (tc, dd - 2) in sFe:
                    t.wait_ge(sV.h, sFe[(tc, dd - 2)])
                elif tc > 0 and (tc - 1, DD - 2 + dd) in sFe:
                    t.wait_ge(sV.h, sFe[(tc - 1, DD - 2 + dd)])
                for hc in range(NHT):
                    ins = t.matmul(pDO[dd % 2][:],
                                   w2b[dd % 2][:, hc * 128:(hc + 1) * 128],
                                   gbuf[:, hc * TC:(hc + 1) * TC],
                                   start=(hc == 0), stop=(hc == NHT - 1))
                tDO[(tc, dd)] = sT.inc(ins)
            on("tensor", d_do)

            def d_fe(v, tc=tc, dd=dd):
                v.wait_ge(sT.h, tDO[(tc, dd)])
                if tc > 0 and dd == 0:
                    v.wait_ge(sOUT.h, cOUT[tc - 1])
                sl = fbuf[:, dd * TC:(dd + 1) * TC]
                v.tensor_scalar(sl, pDO[dd % 2][:], sF[:, 2:3], None, ALU.mult)
                sFe[(tc, dd)] = sV.inc(v.tensor_mul(fsq[dd % 2][:], sl, sl))
            on("vector", d_fe)

            def d_st(t, tc=tc, dd=dd):
                t.wait_ge(sV.h, sFe[(tc, dd)])
                if dd == 0 and tc > 0:
                    t.wait_ge(sV.h, vSt4[tc - 1])
                t.matmul(pSTd[0:1, 0:TC], onesc_f[:], fbuf[:, dd * TC:(dd + 1) * TC],
                         start=(dd == 0), stop=(dd == DD - 1))
                ins = t.matmul(pSTd[0:1, 512:512 + TC], onesc_b[:], fsq[dd % 2][:],
                               start=(dd == 0), stop=(dd == DD - 1))
                tStd[(tc, dd)] = sT.inc(ins)
            on("tensor", d_st)

        def d_stats(v, tc=tc):
            v.wait_ge(sT.h, tStd[(tc, DD - 1)])
            v.tensor_scalar(m_row[0:1, 0:TC], pSTd[0:1, 0:TC], INV_D, None, ALU.mult)
            v.tensor_scalar(var_row[0:1, 0:TC], pSTd[0:1, 512:512 + TC], INV_D,
                            EPS, ALU.mult, ALU.add)
            v.tensor_mul(ln_row[0:1, 0:TC], m_row[0:1, 0:TC], m_row[0:1, 0:TC])
            vSt4[tc] = sV.inc(v.tensor_sub(var_row[0:1, 0:TC], var_row[0:1, 0:TC],
                                           ln_row[0:1, 0:TC]))
        on("vector", d_stats)

        def d_rstd(s, tc=tc):
            s.wait_ge(sV.h, vSt4[tc])
            s.activation(ln_row[0:1, 0:TC], var_row[0:1, 0:TC], AF.Ln)
            sS.inc(s.activation(r_row[0:1, 0:TC], ln_row[0:1, 0:TC], AF.Exp, scale=-0.5))
        on("scalar", d_rstd)
        sR4 = sS.n

        def d_c2(v, sR4=sR4):
            v.wait_ge(sS.h, sR4)
            sV.inc(v.scalar_tensor_tensor(c2_row[0:1, 0:TC], m_row[0:1, 0:TC], -1.0,
                                          r_row[0:1, 0:TC], ALU.mult, ALU.mult))
        on("vector", d_c2)
        vC24 = sV.n

        def d_bc(t, tc=tc, vC24=vC24):
            t.wait_ge(sV.h, vC24)
            t.wait_ge(sV.h, sFe[(tc, DD - 1)])
            t.matmul(pDO[0][:], onesr_f[0:1, 0:128], r_row[0:1, 0:TC],
                     start=True, stop=True)
            sT.inc(t.matmul(pDO[1][:], onesr_f[0:1, 0:128], c2_row[0:1, 0:TC],
                            start=True, stop=True))
        on("tensor", d_bc)
        tBC4 = sT.n

        for dd in range(DD):
            def d_out(v, tc=tc, dd=dd, tBC4=tBC4):
                if dd == 0:
                    v.wait_ge(sT.h, tBC4)
                    W(v, cD_y16[tc])
                sl = fbuf[:, dd * TC:(dd + 1) * TC]
                v.tensor_mul(sl, sl, pDO[0][:])
                v.tensor_add(sl, sl, pDO[1][:])
                vOut[(tc, dd)] = sV.inc(v.tensor_add(sl, sl,
                                                     y16[:, dd * TC:(dd + 1) * TC]))
            on("vector", d_out)

        def d_ow(g, tc=tc):
            g.wait_ge(sV.h, vOut[(tc, DD - 1)])
            cOUT[tc] = sOUT.dma(g.dma_start(
                dv(outT, tc * TC, TC), fbuf[:].rearrange("p (a s) -> p a s", s=TC)))
        on("gpsimd", d_ow)

    if dump:
        dthr = nc.dram_tensor("dump_thr", [128, 3], F32, kind="ExternalOutput").ap()
        dsf = nc.dram_tensor("dump_sf", [128, 3], F32, kind="ExternalOutput").ap()
        dsacc = nc.dram_tensor("dump_sacc", [128, 3], F32, kind="ExternalOutput").ap()
        dmp = {}
        for nm, t_, rows, cols in (
                ("xn_s", xn_s, DD * 128, TOWN), ("qt_s", qt_s, H * 128, TOWN),
                ("y_s", y_s, DD * 128, TOWN), ("yn_s", yn_s, DD * 128, TOWN),
                ("w1q_s", w1q_s, NHT * 128, D), ("w3q_s", w3q_s, NHT * 128, D),
                ("w2q_s", w2q_s, DD * 128, HIDP)):
            dt_ = F32 if nm == "y_s" else BF16
            dmp[nm] = nc.dram_tensor("dump_" + nm, [rows, cols], dt_,
                                     kind="ExternalOutput").ap()

        def d_dump(sy):
            sy.wait_ge(sOUT.h, cOUT[NTC - 1])
            n = 0
            for nm, t_, rows, cols in (
                    ("xn_s", xn_s, DD * 128, TOWN), ("qt_s", qt_s, H * 128, TOWN),
                    ("y_s", y_s, DD * 128, TOWN), ("yn_s", yn_s, DD * 128, TOWN),
                    ("w1q_s", w1q_s, NHT * 128, D), ("w3q_s", w3q_s, NHT * 128, D),
                    ("w2q_s", w2q_s, DD * 128, HIDP)):
                sOUT.dma(sy.dma_start(dmp[nm], t_))
                n += 16
            sOUT.dma(sy.dma_start(dthr, thrP[:]))
            sOUT.dma(sy.dma_start(dsf, sF[:]))
            sOUT.dma(sy.dma_start(dsacc, sacc_dbg[:]))
        on("sync", d_dump)

    def d_final(sy):
        sy.wait_ge(sOUT.h, sOUT.n)
    on("sync", d_final)

    p_ctx.close()
    d_ctx.close()
    ctx.close()
    return nc



def _swizzle_w13(w, cfg):
    D, HID, HIDP, DD, NHT = cfg["D"], cfg["HID"], cfg["HIDP"], cfg["DD"], cfg["NHT"]
    wt = np.zeros((D, HIDP), np.float32)
    wt[:, :HID] = np.asarray(w, np.float32).T
    return np.ascontiguousarray(
        wt.reshape(DD, 128, NHT, 128).transpose(2, 1, 0, 3).reshape(NHT * 128, D))


def _swizzle_wo(w):
    w = np.asarray(w, np.float32)
    hd, d = w.shape
    H_, DD_ = hd // 128, d // 128
    return np.ascontiguousarray(
        w.reshape(H_, 128, DD_, 128).transpose(2, 1, 0, 3).reshape(d, hd))


def _swizzle_w2(w, cfg):
    D, HID, HIDP, DD, NHT = cfg["D"], cfg["HID"], cfg["HIDP"], cfg["DD"], cfg["NHT"]
    wt = np.zeros((HIDP, D), np.float32)
    wt[:HID, :] = np.asarray(w, np.float32).T
    return np.ascontiguousarray(
        wt.reshape(NHT, 128, DD, 128).transpose(2, 1, 0, 3).reshape(DD * 128, HIDP))


def make_in_maps(cfg, x, Wq, Bq, Wk, Bk, Wv, Bv, Wo, Bo, W1, W3, W2, n_cores):
    TOWN = cfg["TOWN"]
    shared = {
        "wq": np.ascontiguousarray(np.asarray(Wq, np.float32)),
        "wk": np.ascontiguousarray(np.asarray(Wk, np.float32)),
        "wv": np.ascontiguousarray(np.asarray(Wv, np.float32)),
        "wo": _swizzle_wo(Wo),
        "bq": np.ascontiguousarray(np.asarray(Bq, np.float32).reshape(1, -1)),
        "bk": np.ascontiguousarray(np.asarray(Bk, np.float32).reshape(1, -1)),
        "bv": np.ascontiguousarray(np.asarray(Bv, np.float32).reshape(1, -1)),
        "bo": np.ascontiguousarray(np.asarray(Bo, np.float32).reshape(1, -1)),
        "w1s": _swizzle_w13(W1, cfg),
        "w3s": _swizzle_w13(W3, cfg),
        "w2s": _swizzle_w2(W2, cfg),
    }
    in_maps = []
    nseq = x.shape[1] // TOWN
    for c in range(n_cores):
        b, half = divmod(c, nseq)
        xt = x[b].T
        own = xt[:, half * TOWN:(half + 1) * TOWN]
        parts = [own] + [xt[:, i * TOWN:(i + 1) * TOWN]
                         for i in range(nseq) if i != half]
        m = dict(shared)
        m["xT"] = np.ascontiguousarray(np.concatenate(parts, axis=1))
        in_maps.append(m)
    return in_maps


def kernel(L0_attn_inputs_q, L0_attn_wq, L0_attn_bq, L0_attn_wk, L0_attn_bk,
           L0_attn_wv, L0_attn_bv, L0_attn_wo, L0_attn_bo,
           L0_ffn_w1, L0_ffn_w3, L0_ffn_w2):
    cfg = REAL
    TOWN = cfg["TOWN"]
    x = np.asarray(L0_attn_inputs_q, np.float32)
    B, S, D = x.shape
    in_maps = make_in_maps(cfg, x, L0_attn_wq, L0_attn_bq, L0_attn_wk, L0_attn_bk,
                           L0_attn_wv, L0_attn_bv, L0_attn_wo, L0_attn_bo,
                           L0_ffn_w1, L0_ffn_w3, L0_ffn_w2, 8)
    nc = build_nc(cfg)
    res = run_bass_kernel_spmd(nc, in_maps, core_ids=list(range(8)))
    out = np.empty((B, S, D), np.float32)
    for c in range(8):
        b, half = divmod(c, 2)
        out[b, half * TOWN:(half + 1) * TOWN, :] = res.results[c]["outT"].T
    return out


if __name__ == "__main__":
    import time, sys
    cfgname = sys.argv[1] if len(sys.argv) > 1 else "SMALL"
    t0 = time.time()
    nc = build_nc(REAL if cfgname == "REAL" else SMALL)
    ni = sum(1 for _ in nc.all_instructions())
    print(f"built {cfgname} ok in {time.time()-t0:.1f}s, {ni} instructions")
